# version 11
# baseline (speedup 1.0000x reference)
import math
import numpy as np
from contextlib import ExitStack

import concourse.bass as bass
import concourse.mybir as mybir
from concourse.bass_utils import run_bass_kernel_spmd

F32 = mybir.dt.float32
F32R = mybir.dt.float32r
AF = mybir.ActivationFunctionType
ALU = mybir.AluOpType
AX = mybir.AxisListType

B, S, D = 2, 4096, 4096
H, KV, HD = 32, 8, 128
G = H // KV
I = 11008
IC = I // 128
NKT = D // 128
CH = 512
NSLOT = 8
NSL = (4, 9)
SCALE = 1.0 / math.sqrt(HD)
EPS = 1e-5
NEG = -1.0e30
FFN_GROUPS = [8] * 10 + [6]
ENGS = ("sync", "gpsimd", "vector", "scalar", "tensor")


class Sem:
    def __init__(self, h):
        self.h = h
        self.n = 0

    def bump(self, d=1):
        self.n += d
        return self.n


class Plan:
    def __init__(self):
        self.q = {e: [] for e in ENGS}
        self.autosem = {}

    def emit(self, eng, fn):
        self.q[eng].append(fn)

    def dma(self, eng, sem, out_ap, in_ap):
        t = sem.bump(16)
        self.emit(eng, lambda e, o=out_ap, i=in_ap, s=sem.h:
                  e.dma_start(out=o, in_=i).then_inc(s, 16))
        return t

    def wait(self, eng, sem, val):
        if val > 0:
            self.emit(eng, lambda e, s=sem.h, v=val: e.wait_ge(s, v))

    def op(self, eng, name, *a, inc=None, **kw):
        auto = self.autosem.get(eng)
        if auto is not None:
            if auto.n > 0:
                self.wait(eng, auto, auto.n)
            if inc is None:
                inc = (auto, 1)
            else:
                assert inc[0] is auto, "non-auto inc on %s" % eng
        t = None
        if inc is not None:
            t = inc[0].bump(inc[1])
            ih = inc[0].h
        else:
            ih = None

        def f(e, nm=name, aa=a, kk=kw, hh=ih, vv=(inc[1] if inc else 0)):
            ins = getattr(e, nm)(*aa, **kk)
            if hh is not None:
                ins.then_inc(hh, vv)
        self.emit(eng, f)
        return t

    def barrier(self, dsems):
        for e in ENGS:
            s = dsems[e]
            s.bump(1)
            self.emit(e, lambda en, h=s.h: en.sem_inc(h, 1))
        for e in ENGS:
            for e2 in ENGS:
                if e2 != e:
                    self.wait(e, dsems[e2], dsems[e2].n)


def build(debug=False):
    nc = bass.Bass("TRN2", target_bir_lowering=False)
    dr = lambda n, sh, dt_=F32, k="ExternalInput": nc.dram_tensor(n, sh, dt_, kind=k).ap()

    xTp = dr("xTp", [2, NKT, 128, CH])
    wq_t = dr("wq_t", [KV, NKT, 128, 512])
    wk_t = dr("wk_t", [KV, NKT, 128, 128])
    wv_t = dr("wv_t", [2, NKT, 128, 512])
    wo_t = dr("wo_t", [4, 32, 8, 128, 128])
    w13_t = dr("w13_t", [IC, 4, 2, 8, 128, 128])
    w2_t = dr("w2_t", [2752, 128, 128])
    scl_h = dr("scl", [128, 2 * IC + 32])
    cst_h = dr("csts", [128, 24])
    idn_h = dr("idn", [128, 128])
    prot_h = dr("prot", [128, 128])
    tri_h = dr("tri", [4, 128, CH])
    base_h = dr("base", [128, 13])
    cosk_h = dr("cosk", [2, 128, CH])
    sink_h = dr("sink", [2, 128, CH])
    onec_h = dr("onec", [128, 1])
    oner_h = dr("oner", [1, 128])
    out_h = dr("out", [2, CH, D], F32, "ExternalOutput")

    dbg = {}
    if debug:
        for nm, sh, dt_ in [("d_xn3", [NKT, 128, CH], F32R),
                            ("d_k00", [128, CH], F32R),
                            ("d_v00", [128, 512], F32R),
                            ("d_q0", [128, CH], F32R),
                            ("d_attn0", [128, CH], F32R),
                            ("d_acc0", [128, CH], F32)]:
            dbg[nm] = dr(nm, sh, dt_, "ExternalOutput")

    k_loc = dr("k_loc", [2, KV, 128, CH], F32R, "Internal")
    v_loc = dr("v_loc", [2, 4, 128, KV * 128], F32R, "Internal")
    k_gat = dr("k_gat", [8, KV, 128, CH], F32R, "Internal")
    v_gat = dr("v_gat", [8, 4, 128, KV * 128], F32R, "Internal")
    xn_sc = dr("xn_sc", [2, NKT, 128, CH], F32R, "Internal")
    at_sc = dr("at_sc", [H, 128, CH], F32R, "Internal")

    pl = Plan()

    with ExitStack() as st:
        def mksem(name):
            return Sem(st.enter_context(nc.semaphore(name)))

        sLD = mksem("sLD")
        sLW = mksem("sLW")
        sSP = mksem("sSP")
        sMM = mksem("sMM")
        sEV = mksem("sEV")
        sAC = mksem("sAC")
        sST = mksem("sST")
        dsems = {e: mksem("sdn_" + e) for e in ENGS}
        pl.autosem = {"vector": sEV, "scalar": sAC}
        s_cs = mksem("s_cs")
        sSPx = mksem("sSPx"); sSPk = mksem("sSPk"); sSPv = mksem("sSPv")
        sSPa = mksem("sSPa")
        sSTo = [mksem("sSTo0"), mksem("sSTo1")]
        sCC = mksem("sCC")

        c_idn = st.enter_context(nc.sbuf_tensor("c_idn", [128, 128], F32))
        c_idnr = st.enter_context(nc.sbuf_tensor("c_idnr", [128, 128], F32R))
        c_prot = st.enter_context(nc.sbuf_tensor("c_prot", [128, 128], F32R))
        c_tri = st.enter_context(nc.sbuf_tensor("c_tri", [128, 4 * CH], F32R))
        c_scl = st.enter_context(nc.sbuf_tensor("c_scl", [128, 2 * IC + 32], F32))
        c_cst = st.enter_context(nc.sbuf_tensor("c_cst", [128, 24], F32))
        c_onec = st.enter_context(nc.sbuf_tensor("c_onec", [128, 1], F32R))
        c_oner = st.enter_context(nc.sbuf_tensor("c_oner", [1, 128], F32R))
        c_cos = st.enter_context(nc.sbuf_tensor("c_cos", [128, CH], F32))
        c_sin = st.enter_context(nc.sbuf_tensor("c_sin", [128, CH], F32))

        pl.dma("sync", sLD, c_idn[:], idn_h)
        pl.dma("gpsimd", sLW, c_tri[:].rearrange("p (q f) -> p q f", q=4),
               tri_h.rearrange("q p f -> p q f"))
        pl.dma("gpsimd", sLW, c_idnr[:], idn_h)
        pl.dma("sync", sLD, c_scl[:], scl_h)
        pl.dma("sync", sLD, c_cst[:], cst_h)
        pl.dma("gpsimd", sLW, c_prot[:], prot_h)
        pl.dma("gpsimd", sLW, c_onec[:], onec_h)
        pl.dma("gpsimd", sLW, c_oner[:], oner_h)
        for e in ("vector", "scalar", "tensor"):
            pl.wait(e, sLD, sLD.n)
            pl.wait(e, sLW, sLW.n)

        pb = [st.enter_context(nc.psum_tensor("pb%d" % i, [128, 512], F32))
              for i in range(8)]
        bfree = [None] * 8

        def pe_take(bi):
            if bfree[bi] is not None:
                pl.wait("tensor", bfree[bi][0], bfree[bi][1])

        def mm(bi, out_ap, lhsT, rhs, start, stop, inc=None):
            def f(e, o=out_ap, l=lhsT, r=rhs, s=start, p=stop,
                  hh=(inc[0].h if inc else None), vv=(inc[1] if inc else 0)):
                ins = e.matmul(o, l, r, start=s, stop=p)
                if hh is not None:
                    ins.then_inc(hh, vv)
            if inc is not None:
                t = inc[0].bump(inc[1])
            else:
                t = None
            pl.emit("tensor", f)
            return t

        def transp(out_ap, in_ap, inc=None):
            def f(e, o=out_ap, i=in_ap,
                  hh=(inc[0].h if inc else None), vv=(inc[1] if inc else 0)):
                ins = e.transpose(o, i, c_idn[:])
                if hh is not None:
                    ins.then_inc(hh, vv)
            if inc is not None:
                t = inc[0].bump(inc[1])
            else:
                t = None
            pl.emit("tensor", f)
            return t

        def load_cos_sin(slot):
            pl.dma("sync", s_cs, c_cos[:], cosk_h[slot])
            t2 = pl.dma("sync", s_cs, c_sin[:], sink_h[slot])
            return t2

        with ExitStack() as p1:
            sb1 = lambda n, sh, dt_: p1.enter_context(nc.sbuf_tensor(n, sh, dt_))
            A_buf = sb1("A_buf", [128, NKT * CH], F32R)
            sq_b = sb1("sq_b", [128, CH], F32R)
            wkb = sb1("wkb", [128, 2 * NKT * 128], F32R)
            wvb = sb1("wvb", [128, 2 * 4 * 512], F32R)
            krawb = sb1("krawb", [128, CH], F32R)
            ktmp = sb1("ktmp", [128, 2 * CH], F32)
            K_out = sb1("K_out", [128, CH], F32R)
            V_out = sb1("V_out", [128, 512], F32R)
            mrow = sb1("mrow", [1, CH], F32R)
            rrow = sb1("rrow", [1, CH], F32R)
            srow = sb1("srow", [1, CH], F32)
            sp1 = sb1("sp1", [1, 8], F32)

            s_x = mksem("s_x")
            s_wk = [mksem("s_wk0"), mksem("s_wk1")]
            s_wv = [mksem("s_wv0"), mksem("s_wv1")]
            wk_free = [0, 0]
            wv_free = [0, 0]
            abuf_free_mm = 0
            abuf_spill = 0
            kout_spill = 0
            vout_spill = 0

            for s in range(2):
                pl.wait("sync", sEV, sEV.n)
                tcs = load_cos_sin(s)
                pl.wait("gpsimd", sMM, abuf_free_mm)
                pl.wait("gpsimd", sSPx, abuf_spill)
                for blk in range(4):
                    kt0 = blk * 8
                    tx_all = pl.dma("gpsimd", s_x,
                                    A_buf[:, kt0 * CH:(kt0 + 8) * CH]
                                    .rearrange("p (k f) -> p k f", k=8),
                                    xTp[s, kt0:kt0 + 8].rearrange("k p f -> p k f"))

                pe_take(0)
                pe_take(1)
                last_sq = 0
                pl.wait("tensor", s_x, tx_all)
                pl.wait("vector", s_x, tx_all)
                for kt in range(NKT):
                    xt = A_buf[:, kt * CH:(kt + 1) * CH]
                    mm(0, pb[0][0:1, :], c_onec[:], xt, kt == 0, kt == NKT - 1)
                    pl.wait("vector", sMM, last_sq)
                    tev = pl.op("vector", "tensor_mul", sq_b[:],
                                xt.bitcast(F32), xt.bitcast(F32), inc=(sEV, 1))
                    pl.wait("tensor", sEV, tev)
                    last_sq = mm(1, pb[1][0:1, :], c_onec[:], sq_b[:],
                                 kt == 0, kt == NKT - 1, inc=(sMM, 1))

                pl.wait("vector", sMM, last_sq)
                t0 = pl.op("vector", "tensor_scalar", mrow[:],
                           pb[0][0:1, :], 1.0 / D, None, ALU.mult,
                           inc=(sEV, 1))
                bfree[0] = (sEV, t0)
                t1 = pl.op("vector", "tensor_scalar", srow[:], pb[1][0:1, :],
                           1.0 / D, None, ALU.mult, inc=(sEV, 1))
                bfree[1] = (sEV, t1)
                pl.wait("vector", sEV, t1)
                pl.op("vector", "tensor_mul", ktmp[0:1, 0:CH],
                      mrow[:].bitcast(F32), mrow[:].bitcast(F32))
                tev = pl.op("vector", "tensor_tensor", srow[:], srow[:],
                            ktmp[0:1, 0:CH], ALU.subtract, inc=(sEV, 1))
                pl.wait("scalar", sEV, tev)
                tac = pl.op("scalar", "activation", srow[:], srow[:], AF.Sqrt,
                            bias=c_cst[0:1, 2:3], scale=1.0, inc=(sAC, 1))
                pl.wait("vector", sAC, tac)
                pl.op("vector", "tensor_copy", sp1[:, 0:1], c_cst[0:1, 0:1])
                tev = pl.op("vector", "reciprocal", rrow[:],
                            srow[:], inc=(sEV, 1))

                pe_take(2)
                pe_take(3)
                pl.wait("tensor", sEV, tev)
                mm(2, pb[2][:, 0:CH], c_oner[:], mrow[:], True, True)
                tmm = mm(3, pb[3][:, 0:CH], c_oner[:], rrow[:], True, True,
                         inc=(sMM, 1))

                pl.wait("vector", sMM, tmm)
                tev = 0
                for kt in range(NKT):
                    xt = A_buf[:, kt * CH:(kt + 1) * CH]
                    pl.op("vector", "tensor_tensor", ktmp[:, 0:CH],
                          xt.bitcast(F32), pb[2][:, 0:CH], ALU.subtract)
                    inc = (sEV, 1) if kt == NKT - 1 else None
                    tev2 = pl.op("vector", "tensor_tensor", xt, ktmp[:, 0:CH],
                                 pb[3][:, 0:CH], ALU.mult, inc=inc)
                    if tev2 is not None:
                        tev = tev2
                bfree[2] = (sEV, tev)
                bfree[3] = (sEV, tev)

                if True:
                    own = s
                    pl.wait("sync", sEV, tev)
                    abuf_spill = pl.dma("sync", sSPx,
                                        xn_sc[own].rearrange("k p f -> p k f"),
                                        A_buf[:].rearrange("p (k f) -> p k f",
                                                           k=NKT))
                    if debug and s == 0:
                        pl.dma("sync", sST,
                               dbg["d_xn3"].rearrange("k p f -> p k f"),
                               A_buf[:]
                               .rearrange("p (k f) -> p k f", k=NKT))
                        pl.wait("gpsimd", sST, sST.n)
                        pl.wait("tensor", sST, sST.n)
                        pl.wait("vector", sST, sST.n)

                pl.wait("tensor", sEV, tev)
                pl.wait("vector", s_cs, tcs)

                for kvh in range(KV):
                    buf = kvh % 2
                    pl.wait("gpsimd", sMM, wk_free[buf])
                    twk = pl.dma("gpsimd", s_wk[buf],
                                 wkb[:, buf * NKT * 128:(buf + 1) * NKT * 128]
                                 .rearrange("p (k f) -> p k f", k=NKT),
                                 wk_t[kvh].rearrange("k p f -> p k f"))
                    pl.wait("tensor", s_wk[buf], twk)
                    bk = 4 + (kvh % 2)
                    pe_take(bk)
                    for kt in range(NKT):
                        tmm = mm(bk, pb[bk][:, 0:CH],
                                 wkb[:, (buf * NKT + kt) * 128:
                                     (buf * NKT + kt + 1) * 128],
                                 A_buf[:, kt * CH:(kt + 1) * CH],
                                 kt == 0, kt == NKT - 1,
                                 inc=(sMM, 1) if kt == NKT - 1 else None)
                    wk_free[buf] = tmm
                    pl.wait("vector", sMM, tmm)
                    sEVt = pl.op("vector", "tensor_copy", krawb[:],
                                 pb[bk][:, 0:CH], inc=(sEV, 1))
                    bfree[bk] = (sEV, sEVt)
                    pe_take(6)
                    pl.wait("tensor", sEV, sEVt)
                    trot = mm(6, pb[6][:, 0:CH], c_prot[:], krawb[:], True, True,
                              inc=(sMM, 1))
                    pl.wait("vector", sSPk, kout_spill)
                    pl.wait("vector", sEV, sEVt)
                    pl.op("vector", "tensor_tensor", ktmp[:, 0:CH],
                          krawb[:].bitcast(F32), c_cos[:], ALU.mult)
                    pl.wait("vector", sMM, trot)
                    t6 = pl.op("vector", "tensor_tensor", ktmp[:, CH:2 * CH],
                               pb[6][:, 0:CH], c_sin[:], ALU.mult, inc=(sEV, 1))
                    bfree[6] = (sEV, t6)
                    tev = pl.op("vector", "tensor_add", K_out[:],
                                ktmp[:, 0:CH], ktmp[:, CH:2 * CH], inc=(sEV, 1))
                    pl.wait("sync", sEV, tev)
                    kout_spill = pl.dma("sync", sSPk, k_loc[s, kvh], K_out[:])
                    if debug and s == 0 and kvh == 0:
                        pl.dma("sync", sST, dbg["d_k00"], K_out[:])
                        pl.wait("vector", sST, sST.n)

                last_v = 0
                for vch in range(2):
                    for tt in range(4):
                        pe_take(tt)
                    for blk in range(8):
                        buf = blk % 2
                        pl.wait("gpsimd", sMM, wv_free[buf])
                        twv = pl.dma("gpsimd", s_wv[buf],
                                     wvb[:, buf * 4 * 512:(buf + 1) * 4 * 512]
                                     .rearrange("p (k f) -> p k f", k=4),
                                     wv_t[vch, blk * 4:(blk + 1) * 4].rearrange("k p f -> p k f"))
                        pl.wait("tensor", s_wv[buf], twv)
                        for k4 in range(4):
                            kt = blk * 4 + k4
                            for tt in range(4):
                                inc = (sMM, 1) if (k4 == 3 and tt == 3) else None
                                tmm = mm(tt, pb[tt][:, 0:512],
                                         A_buf[:, kt * CH + tt * 128:
                                               kt * CH + (tt + 1) * 128],
                                         wvb[:, (buf * 4 + k4) * 512:
                                             (buf * 4 + k4 + 1) * 512],
                                         kt == 0, kt == NKT - 1, inc=inc)
                                if tmm is not None:
                                    wv_free[buf] = tmm
                                    last_v = tmm
                    pl.wait("vector", sMM, last_v)
                    for tt in range(4):
                        pl.wait("vector", sSPv, vout_spill)
                        sEVt = pl.op("vector", "tensor_copy", V_out[:],
                                     pb[tt][:, 0:512], inc=(sEV, 1))
                        bfree[tt] = (sEV, sEVt)
                        pl.wait("sync", sEV, sEVt)
                        vout_spill = pl.dma("sync", sSPv,
                                            v_loc[s, tt, :, vch * 512:
                                                 (vch + 1) * 512], V_out[:])
                        if debug and s == 0 and tt == 0 and vch == 0:
                            pl.dma("sync", sST, dbg["d_v00"], V_out[:])
                            pl.wait("vector", sST, sST.n)
                abuf_free_mm = last_v

            t_spills_p1 = (sSPx.n, sSPk.n, sSPv.n)
            pl.wait("sync", sSPx, sSPx.n)
            pl.wait("sync", sSPk, sSPk.n)
            pl.wait("sync", sSPv, sSPv.n)
            pl.wait("gpsimd", sSPk, sSPk.n)
            pl.wait("gpsimd", sSPv, sSPv.n)
            pl.emit("gpsimd", lambda g: g.collective_compute(
                "AllGather", ALU.bypass,
                replica_groups=[[0, 1, 2, 3], [4, 5, 6, 7]],
                ins=[k_loc[:]], outs=[k_gat[:]]).then_inc(sCC.h, 1))
            sCC.bump(1)
            pl.emit("gpsimd", lambda g: g.collective_compute(
                "AllGather", ALU.bypass,
                replica_groups=[[0, 1, 2, 3], [4, 5, 6, 7]],
                ins=[v_loc[:]], outs=[v_gat[:]]).then_inc(sCC.h, 1))
            sCC.bump(1)
            t_cc = sCC.n
        pl.barrier(dsems)

        for own in range(2):
            nsl = NSL[own]
            ncan = nsl - 1
            base_off12 = 0 if own == 0 else 4
            tcs = load_cos_sin(own)

            with ExitStack() as p2:
                sb2 = lambda n, sh, dt_: p2.enter_context(
                    nc.sbuf_tensor(n + "_%d" % own, sh, dt_))
                xnb = sb2("xnb", [128, 2 * 2 * CH], F32R)
                wqb = sb2("wqb", [128, 2 * 2 * 512], F32R)
                q_all = sb2("q_all", [128, H * CH], F32R)
                qraw = sb2("qraw", [128, CH], F32R)
                K_sb = sb2("K_sb", [128, 2 * nsl * CH], F32R)
                V_sb = sb2("V_sb", [128, 2 * nsl * 4 * 128], F32R)
                baseb = sb2("baseb", [128, 13], F32)
                kk_r = sb2("kk_r", [128, CH], F32R)
                qq_r = sb2("qq_r", [128, CH], F32R)
                kcmax = sb2("kcmax", [1, 16], F32)
                km2 = sb2("km2", [1, 16], F32)
                qn2_row = sb2("qn2_row", [1, 4 * CH], F32)
                nm_row = sb2("nm_row", [1, 2 * 4 * CH], F32R)
                rc_row = sb2("rc_row", [1, CH], F32R)
                rcrep = sb2("rcrep", [128, CH], F32)
                expT = sb2("expT", [128, 2 * CH], F32R)
                pvout = sb2("pvout", [128, CH], F32R)

                s_base = mksem("s_base%d" % own)
                s_kv = [mksem("s_kv%d_0" % own), mksem("s_kv%d_1" % own)]
                s_wq = [mksem("s_wq%d_0" % own), mksem("s_wq%d_1" % own)]
                s_xn2 = [mksem("s_xn%d_0" % own), mksem("s_xn%d_1" % own)]
                tb = pl.dma("sync", s_base, baseb[:], base_h)
                pl.wait("scalar", s_base, tb)
                pl.wait("sync", sSPx, t_spills_p1[0])
                pl.wait("sync", sSPk, t_spills_p1[1])
                pl.wait("sync", sSPv, t_spills_p1[2])

                wq_free = [0, 0]
                xn_free = [0, 0]
                kv_free = [0, 0]
                kv_tick = [0, 0]
                expT_free = [0, 0]
                pvout_sp = 0

                for kvh in range(KV):
                    for ht in range(4):
                        pe_take(ht)
                    for blk in range(16):
                        buf = blk % 2
                        pl.wait("gpsimd", sMM, wq_free[buf])
                        twq = pl.dma("gpsimd", s_wq[buf],
                                     wqb[:, buf * 2 * 512:(buf + 1) * 2 * 512]
                                     .rearrange("p (k f) -> p k f", k=2),
                                     wq_t[kvh, blk * 2:(blk + 1) * 2]
                                     .rearrange("k p f -> p k f"))
                        pl.wait("sync", sMM, xn_free[buf])
                        txn = pl.dma("sync", s_xn2[buf],
                                     xnb[:, buf * 2 * CH:(buf + 1) * 2 * CH]
                                     .rearrange("p (k f) -> p k f", k=2),
                                     xn_sc[own, blk * 2:(blk + 1) * 2]
                                     .rearrange("k p f -> p k f"))
                        pl.wait("tensor", s_wq[buf], twq)
                        pl.wait("tensor", s_xn2[buf], txn)
                        for k4 in range(2):
                            kt = blk * 2 + k4
                            for ht in range(4):
                                inc = (sMM, 1) if (k4 == 1 and ht == 3) else None
                                tmm = mm(ht, pb[ht][:, 0:CH],
                                         wqb[:, (buf * 2 + k4) * 512 + ht * 128:
                                             (buf * 2 + k4) * 512 + (ht + 1) * 128],
                                         xnb[:, (buf * 2 + k4) * CH:
                                             (buf * 2 + k4 + 1) * CH],
                                         kt == 0, kt == NKT - 1, inc=inc)
                        wq_free[buf] = tmm
                        xn_free[buf] = tmm
                    tqp = sMM.n
                    pl.wait("vector", sMM, tqp)
                    if kvh == 0:
                        pl.wait("vector", s_cs, tcs)
                    for ht in range(4):
                        head = kvh * 4 + ht
                        sEVt = pl.op("vector", "tensor_copy", qraw[:],
                                     pb[ht][:, 0:CH], inc=(sEV, 1))
                        bfree[ht] = (sEV, sEVt)
                        pe_take(4)
                        pl.wait("tensor", sEV, sEVt)
                        trot = mm(4, pb[4][:, 0:CH], c_prot[:], qraw[:],
                                  True, True, inc=(sMM, 1))
                        pl.wait("vector", sEV, sEVt)
                        pl.op("vector", "tensor_tensor",
                              kk_r[:],
                              qraw[:].bitcast(F32), c_cos[:], ALU.mult)
                        pl.wait("vector", sMM, trot)
                        t4 = pl.op("vector", "tensor_tensor",
                                   qq_r[:],
                                   pb[4][:, 0:CH], c_sin[:], ALU.mult,
                                   inc=(sEV, 1))
                        bfree[4] = (sEV, t4)
                        pl.op("vector", "tensor_add",
                              q_all[:, head * CH:(head + 1) * CH],
                              kk_r[:].bitcast(F32), qq_r[:].bitcast(F32),
                              inc=(sEV, 1))
                tq_ready = sEV.n
                pl.wait("tensor", sEV, tq_ready)
                if debug and own == 0:
                    pl.wait("sync", sEV, tq_ready)
                    pl.dma("sync", sST, dbg["d_q0"], q_all[:, 0:CH])
                    pl.wait("vector", sST, sST.n)

                csl = slice(0, 6, 2) if own == 0 else slice(0, 8)
                canon = (range(0, 6, 2) if own == 0 else range(8))

                def kv_load(kvh):
                    sl = kvh % 2
                    ko = sl * nsl * CH
                    vo = sl * nsl * 4 * 128
                    pl.wait("scalar", sMM, kv_free[sl])
                    if kvh == 0:
                        pl.wait("scalar", sCC, t_cc)
                    pl.dma("scalar", s_kv[sl],
                           K_sb[:, ko:ko + ncan * CH]
                           .rearrange("p (s f) -> p s f", s=ncan),
                           k_gat[csl, kvh].rearrange("s p f -> p s f"))
                    pl.dma("scalar", s_kv[sl],
                           K_sb[:, ko + ncan * CH:ko + nsl * CH],
                           k_loc[own, kvh])
                    for si, cp in enumerate(canon):
                        pl.dma("scalar", s_kv[sl],
                               V_sb[:, vo + si * 4 * 128:vo + (si + 1) * 4 * 128]
                               .rearrange("p (t f) -> p t f", t=4),
                               v_gat[cp, :, :, kvh * 128:(kvh + 1) * 128]
                               .rearrange("t p f -> p t f"))
                    kv_tick[sl] = pl.dma(
                        "scalar", s_kv[sl],
                        V_sb[:, vo + ncan * 4 * 128:vo + nsl * 4 * 128]
                        .rearrange("p (t f) -> p t f", t=4),
                        v_loc[own, :, :, kvh * 128:(kvh + 1) * 128]
                        .rearrange("t p f -> p t f"))

                def prep(kvh):
                    sl = kvh % 2
                    ko = sl * nsl * CH
                    pl.wait("vector", s_kv[sl], kv_tick[sl])
                    for kc in range(nsl):
                        tkk = pl.op("vector", "tensor_mul", kk_r[:],
                                    K_sb[:, ko + kc * CH:ko + (kc + 1) * CH]
                                    .bitcast(F32),
                                    K_sb[:, ko + kc * CH:ko + (kc + 1) * CH]
                                    .bitcast(F32), inc=(sEV, 1))
                        pe_take(7)
                        pl.wait("tensor", sEV, tkk)
                        tmm = mm(7, pb[7][0:1, 0:CH], c_onec[:], kk_r[:],
                                 True, True, inc=(sMM, 1))
                        pl.wait("vector", sMM, tmm)
                        tkn = pl.op("vector", "reduce_max",
                                    kcmax[0:1, kc:kc + 1],
                                    pb[7][0:1, 0:CH], axis=AX.X,
                                    inc=(sEV, 1))
                        bfree[7] = (sEV, tkn)
                    pl.op("vector", "tensor_copy", km2[0:1, sl * 8 + 6:sl * 8 + 7],
                          c_cst[0:1, 0:1])
                    pl.op("vector", "reduce_max", km2[0:1, sl * 8:sl * 8 + 1],
                          kcmax[0:1, 0:nsl], axis=AX.X)
                    pl.op("vector", "tensor_copy", km2[0:1, sl * 8 + 1:sl * 8 + 2],
                          c_cst[0:1, 0:1])
                    pl.op("vector", "tensor_copy", km2[0:1, sl * 8 + 3:sl * 8 + 4],
                          km2[0:1, sl * 8:sl * 8 + 1])
                    for ht in range(4):
                        head = kvh * 4 + ht
                        qv = q_all[:, head * CH:(head + 1) * CH]
                        tqq = pl.op("vector", "tensor_mul", qq_r[:],
                                    qv.bitcast(F32), qv.bitcast(F32),
                                    inc=(sEV, 1))
                        pe_take(7)
                        pl.wait("tensor", sEV, tqq)
                        tmm = mm(7, pb[7][0:1, 0:CH], c_onec[:], qq_r[:],
                                 True, True, inc=(sMM, 1))
                        pl.wait("vector", sMM, tmm)
                        tqn = pl.op("vector", "tensor_scalar",
                                    qn2_row[0:1, ht * CH:(ht + 1) * CH],
                                    pb[7][0:1, 0:CH],
                                    km2[0:1, sl * 8 + 3:sl * 8 + 4], None, ALU.mult,
                                    inc=(sEV, 1))
                        bfree[7] = (sEV, tqn)
                    pl.wait("scalar", sEV, sEV.n)
                    pl.op("scalar", "activation",
                          qn2_row[0:1, :], qn2_row[0:1, :], AF.Ln,
                          bias=0.0, scale=1.0, inc=(sAC, 1))
                    tsq = pl.op("scalar", "activation",
                                qn2_row[0:1, :], qn2_row[0:1, :], AF.Exp,
                                bias=0.0, scale=0.5, inc=(sAC, 1))
                    pl.wait("vector", sAC, tsq)
                    tnm = pl.op("vector", "tensor_scalar",
                                nm_row[0:1, sl * 4 * CH:(sl + 1) * 4 * CH],
                                qn2_row[0:1, :], -1.001, None,
                                ALU.mult, inc=(sEV, 1))
                    return tnm

                kv_load(0)
                kv_load(1)
                nm_ticks = [None] * KV
                nm_ticks[0] = prep(0)
                nm_ticks[1] = prep(1)
                pend_norm = [None]

                def emit_norm(head, bpv, bsm, t_pv):
                    nonlocal pvout_sp
                    pl.wait("vector", sMM, t_pv)
                    trc = pl.op("vector", "reciprocal", rc_row[0:1, :],
                                pb[bsm][0:1, 0:CH], inc=(sEV, 1))
                    bfree[bsm] = (sEV, trc)
                    pe_take(6)
                    pl.wait("tensor", sEV, trc)
                    tmm = mm(6, pb[6][:, 0:CH], c_oner[:], rc_row[0:1, :],
                             True, True, inc=(sMM, 1))
                    pl.wait("vector", sMM, tmm)
                    trcp = pl.op("vector", "tensor_copy", rcrep[:],
                                 pb[6][:, 0:CH], inc=(sEV, 1))
                    bfree[6] = (sEV, trcp)
                    pl.wait("vector", sEV, trcp)
                    pl.wait("vector", sSPa, pvout_sp)
                    tev = pl.op("vector", "tensor_mul", pvout[:],
                                pb[bpv][:, 0:CH], rcrep[:], inc=(sEV, 1))
                    bfree[bpv] = (sEV, tev)
                    pl.wait("sync", sEV, tev)
                    pvout_sp = pl.dma("sync", sSPa, at_sc[head], pvout[:])
                    if debug and own == 0 and head == 0:
                        pl.dma("sync", sST, dbg["d_attn0"], pvout[:])
                        pl.wait("vector", sST, sST.n)

                for kvh in range(KV):
                    sl = kvh % 2
                    ko = sl * nsl * CH
                    vo = sl * nsl * 4 * 128
                    pl.wait("tensor", sEV, nm_ticks[kvh])
                    pl.wait("tensor", s_kv[sl], kv_tick[sl])
                    for ht in range(4):
                        head = kvh * 4 + ht
                        qv = q_all[:, head * CH:(head + 1) * CH]
                        nmv = nm_row[0:1, (sl * 4 + ht) * CH:
                                     (sl * 4 + ht + 1) * CH]
                        hp = ht % 2
                        bpv, bsm = 4 + hp, 0 + hp
                        nktile = nsl * 4
                        pe_take(bpv)
                        pe_take(bsm)
                        exp_tick = [0] * nktile

                        def emit_score(ki):
                            bs = 2 + (ki % 2)
                            pe_take(bs)
                            mm(bs, pb[bs][:, 0:CH],
                               K_sb[:, ko + ki * 128:ko + (ki + 1) * 128],
                               qv, True, False)
                            if ki // 4 == nsl - 1:
                                mm(bs, pb[bs][:, 0:CH], c_idnr[:],
                                   c_tri[:, (ki % 4) * CH:(ki % 4 + 1) * CH],
                                   False, False)
                            tmm = mm(bs, pb[bs][:, 0:CH], c_oner[:],
                                     nmv, False, True, inc=(sMM, 1))
                            eb = ki % 2
                            pl.wait("scalar", sMM, tmm)
                            pl.wait("scalar", sMM, expT_free[eb])
                            bcol = base_off12 + (ki // 4)
                            tex = pl.op("scalar", "activation",
                                        expT[:, eb * CH:(eb + 1) * CH],
                                        pb[bs][:, 0:CH], AF.Exp,
                                        bias=baseb[:, bcol:bcol + 1],
                                        scale=SCALE, inc=(sAC, 1))
                            bfree[bs] = (sAC, tex)
                            exp_tick[ki] = tex

                        def emit_pv(ki):
                            eb = ki % 2
                            pl.wait("tensor", sAC, exp_tick[ki])
                            mm(bpv, pb[bpv][:, 0:CH],
                               V_sb[:, vo + ki * 128:vo + (ki + 1) * 128],
                               expT[:, eb * CH:(eb + 1) * CH],
                               ki == 0, ki == nktile - 1)
                            tsm = mm(bsm, pb[bsm][0:1, 0:CH], c_onec[:],
                                     expT[:, eb * CH:(eb + 1) * CH],
                                     ki == 0, ki == nktile - 1,
                                     inc=(sMM, 1))
                            expT_free[eb] = tsm

                        emit_score(0)
                        for ki in range(nktile):
                            if ki + 1 < nktile:
                                emit_score(ki + 1)
                            emit_pv(ki)
                        t_pv_done = sMM.n
                        if ht == 0 and kvh + 1 < KV:
                            nm_ticks[kvh + 1] = prep(kvh + 1)
                        if pend_norm[0] is not None:
                            emit_norm(*pend_norm[0])
                        pend_norm[0] = (head, bpv, bsm, t_pv_done)
                    kv_free[sl] = sMM.n
                    if kvh + 2 < KV:
                        kv_load(kvh + 2)
                emit_norm(*pend_norm[0])
                pend_norm[0] = None
                t_at_spills = sSPa.n
            pl.barrier(dsems)

            accs = ExitStack()
            acc = accs.enter_context(
                nc.sbuf_tensor("acc_%d" % own, [128, NKT * CH], F32))
            with ExitStack() as p2c:
                sb2c = lambda n, sh, dt_: p2c.enter_context(
                    nc.sbuf_tensor(n + "_%d" % own, sh, dt_))
                atg = sb2c("atg", [128, 2 * 8 * CH], F32R)
                wob = sb2c("wob", [128, 2 * 32 * 128], F32R)
                s_atg = [mksem("s_atg%d_0" % own), mksem("s_atg%d_1" % own)]
                s_wo = [mksem("s_wo%d_%d" % (own, i)) for i in range(2)]
                atg_free = [0, 0]
                wob_free = [0, 0]
                wo_tick = [0, 0]
                acc_tick = [0] * NKT
                pl.wait("sync", sSPa, t_at_spills)
                for qg in range(4):
                    gbuf = qg % 2
                    pl.wait("sync", sMM, atg_free[gbuf])
                    tat = pl.dma("sync", s_atg[gbuf],
                                 atg[:, gbuf * 8 * CH:(gbuf + 1) * 8 * CH]
                                 .rearrange("p (k f) -> p k f", k=8),
                                 at_sc[qg * 8:(qg + 1) * 8].rearrange("k p f -> p k f"))
                    pl.wait("tensor", s_atg[gbuf], tat)
                    for dmch in range(NKT):
                        wbuf = (dmch // 4) % 2
                        if dmch % 4 == 0:
                            pl.wait("gpsimd", sMM, wob_free[wbuf])
                            two = pl.dma(
                                "gpsimd", s_wo[wbuf],
                                wob[:, wbuf * 32 * 128:(wbuf + 1) * 32 * 128]
                                .rearrange("p (d k f) -> p d k f", d=4, k=8),
                                wo_t[qg, dmch:dmch + 4]
                                .rearrange("d k p f -> p d k f"))
                            wo_tick[wbuf] = two
                        pl.wait("tensor", s_wo[wbuf], wo_tick[wbuf])
                        bo = dmch % 2
                        pe_take(bo)
                        dsub = dmch % 4
                        for ik in range(8):
                            tmm = mm(bo, pb[bo][:, 0:CH],
                                     wob[:, (wbuf * 32 + dsub * 8 + ik) * 128:
                                         (wbuf * 32 + dsub * 8 + ik + 1) * 128],
                                     atg[:, (gbuf * 8 + ik) * CH:
                                         (gbuf * 8 + ik + 1) * CH],
                                     ik == 0, ik == 7,
                                     inc=(sMM, 1) if ik == 7 else None)
                        if dmch % 4 == 3:
                            wob_free[wbuf] = tmm
                        atg_free[gbuf] = tmm
                        pl.wait("vector", sMM, tmm)
                        if qg > 0:
                            pl.wait("vector", sEV, acc_tick[dmch])
                        if qg == 0:
                            ta = pl.op("vector", "tensor_copy",
                                       acc[:, dmch * CH:(dmch + 1) * CH],
                                       pb[bo][:, 0:CH], inc=(sEV, 1))
                        else:
                            ta = pl.op("vector", "tensor_add",
                                       acc[:, dmch * CH:(dmch + 1) * CH],
                                       acc[:, dmch * CH:(dmch + 1) * CH],
                                       pb[bo][:, 0:CH], inc=(sEV, 1))
                        bfree[bo] = (sEV, ta)
                        acc_tick[dmch] = ta
            pl.barrier(dsems)

            with ExitStack() as p3:
                sb3 = lambda n, sh, dt_: p3.enter_context(
                    nc.sbuf_tensor(n + "_%d" % own, sh, dt_))
                xnf = sb3("xnf", [128, NKT * CH], F32R)
                w13b = sb3("w13b", [128, 4 * 2 * 8 * 128], F32R)
                w2b = sb3("w2b", [128, 3 * 8 * 128], F32R)
                hgb = sb3("hgb", [128, 8 * CH], F32R)
                silub = sb3("silub", [128, CH], F32)

                s_xnf = mksem("s_xnf%d" % own)
                s_w13 = [mksem("s_w13%d_%d" % (own, i)) for i in range(4)]
                s_w2 = [mksem("s_w2%d_%d" % (own, i)) for i in range(3)]
                txn = pl.dma("sync", s_xnf,
                             xnf[:].rearrange("p (k f) -> p k f", k=NKT),
                             xn_sc[own].rearrange("k p f -> p k f"))
                pl.wait("tensor", s_xnf, txn)
                w13_free = [0, 0, 0, 0]
                w13_tick = {}
                w2_free = [0, 0, 0]

                def emit_w13_load(Q):
                    if Q >= IC * 4:
                        return
                    slot = Q % 4
                    pl.wait("gpsimd", sMM, w13_free[slot])
                    w13_tick[Q] = pl.dma(
                        "gpsimd", s_w13[slot],
                        w13b[:, slot * 2 * 8 * 128:(slot + 1) * 2 * 8 * 128]
                        .rearrange("p (w k f) -> p w k f", w=2, k=8),
                        w13_t[Q // 4, Q % 4].rearrange("w k p f -> p w k f"))

                emit_w13_load(0)
                emit_w13_load(1)
                emit_w13_load(2)
                hg_free = 0
                silu_free = [0]
                ic0 = 0
                for g, gsz in enumerate(FFN_GROUPS):
                    pl.wait("scalar", sMM, hg_free)
                    pl.wait("vector", sMM, hg_free)
                    for ii in range(gsz):
                        ich = ic0 + ii
                        bu, bg = 0 + (ii % 2), 2 + (ii % 2)
                        for q in range(4):
                            Q = ich * 4 + q
                            slot = Q % 4
                            emit_w13_load(Q + 3)
                            pl.wait("tensor", s_w13[slot], w13_tick[Q])
                            if q == 0:
                                pe_take(bu)
                                pe_take(bg)
                            for k8 in range(8):
                                kt = q * 8 + k8
                                base = slot * 2 * 8 * 128
                                mm(bu, pb[bu][:, 0:CH],
                                   w13b[:, base + k8 * 128:
                                        base + (k8 + 1) * 128],
                                   xnf[:, kt * CH:(kt + 1) * CH],
                                   kt == 0, kt == NKT - 1)
                                tmm = mm(bg, pb[bg][:, 0:CH],
                                         w13b[:, base + (8 + k8) * 128:
                                              base + (8 + k8 + 1) * 128],
                                         xnf[:, kt * CH:(kt + 1) * CH],
                                         kt == 0, kt == NKT - 1,
                                         inc=(sMM, 1) if k8 == 7 else None)
                            w13_free[slot] = tmm
                        pl.wait("scalar", sMM, tmm)
                        pl.wait("scalar", sEV, silu_free[0])
                        sACt = pl.op("scalar", "activation", silub[:],
                                     pb[bu][:, 0:CH], AF.Silu,
                                     bias=0.0, scale=c_scl[:, ich:ich + 1],
                                     inc=(sAC, 1))
                        bfree[bu] = (sAC, sACt)
                        pl.wait("vector", sAC, sACt)
                        sEVt = pl.op("vector", "scalar_tensor_tensor",
                                     hgb[:, ii * CH:(ii + 1) * CH],
                                     pb[bg][:, 0:CH],
                                     c_scl[:, IC + ich:IC + ich + 1],
                                     silub[:], ALU.mult, ALU.mult,
                                     inc=(sEV, 1))
                        bfree[bg] = (sEV, sEVt)
                        silu_free[0] = sEVt
                    t_h_ready = sEV.n
                    pl.wait("tensor", sEV, t_h_ready)
                    for dmch in range(NKT):
                        wbuf = dmch % 3
                        pl.wait("gpsimd", sMM, w2_free[wbuf])
                        goff = (g * 32 + dmch) * 8 if g < 10 else (2560 + dmch * 6)
                        t2 = pl.dma("gpsimd", s_w2[wbuf],
                                    w2b[:, wbuf * 8 * 128:
                                        wbuf * 8 * 128 + gsz * 128]
                                    .rearrange("p (k f) -> p k f", k=gsz),
                                    w2_t[goff:goff + gsz].rearrange("k p f -> p k f"))
                        pl.wait("tensor", s_w2[wbuf], t2)
                        bd = 4 + (dmch % 2)
                        pe_take(bd)
                        for ik in range(gsz):
                            tmm = mm(bd, pb[bd][:, 0:CH],
                                     w2b[:, (wbuf * 8 + ik) * 128:
                                         (wbuf * 8 + ik + 1) * 128],
                                     hgb[:, ik * CH:(ik + 1) * CH],
                                     ik == 0, ik == gsz - 1,
                                     inc=(sMM, 1) if ik == gsz - 1 else None)
                        w2_free[wbuf] = tmm
                        hg_free = tmm
                        pl.wait("vector", sMM, tmm)
                        pl.wait("vector", sEV, acc_tick[dmch])
                        ta = pl.op("vector", "scalar_tensor_tensor",
                                   acc[:, dmch * CH:(dmch + 1) * CH],
                                   pb[bd][:, 0:CH],
                                   c_scl[:, 2 * IC + dmch:2 * IC + dmch + 1],
                                   acc[:, dmch * CH:(dmch + 1) * CH],
                                   ALU.mult, ALU.add, inc=(sEV, 1))
                        bfree[bd] = (sEV, ta)
                        acc_tick[dmch] = ta
                    ic0 += gsz
                if debug and own == 0:
                    pl.wait("vector", sEV, acc_tick[0])
                    tev = pl.op("vector", "tensor_copy", silub[:],
                                acc[:, 0:CH], inc=(sEV, 1))
                    pl.wait("sync", sEV, tev)
                    pl.dma("sync", sST, dbg["d_acc0"], silub[:])
                    pl.wait("scalar", sST, sST.n)
                    pl.wait("vector", sST, sST.n)
            pl.barrier(dsems)

            with ExitStack() as p4:
                sb4 = lambda n, sh, dt_: p4.enter_context(
                    nc.sbuf_tensor(n + "_%d" % own, sh, dt_))
                xres = sb4("xres", [128, 2 * CH], F32)
                stf = sb4("stf", [128, 2 * CH], F32)
                sto = sb4("sto", [128, 2 * CH], F32)
                s_xres = [mksem("s_xr%d_0" % own), mksem("s_xr%d_1" % own)]
                xres_free = [0, 0]
                stf_free = [0, 0]
                sto_free = [0, 0]
                for dmch in range(NKT):
                    dbuf = dmch % 2
                    pl.wait("sync", sEV, xres_free[dbuf])
                    tx = pl.dma("sync", s_xres[dbuf],
                                xres[:, dbuf * CH:(dbuf + 1) * CH],
                                xTp[own, dmch])
                    pl.wait("vector", s_xres[dbuf], tx)
                    pl.wait("vector", sMM, stf_free[dbuf])
                    pl.wait("vector", sEV, acc_tick[dmch])
                    tev = pl.op("vector", "tensor_add",
                                stf[:, dbuf * CH:(dbuf + 1) * CH],
                                acc[:, dmch * CH:(dmch + 1) * CH],
                                xres[:, dbuf * CH:(dbuf + 1) * CH],
                                inc=(sEV, 1))
                    xres_free[dbuf] = tev
                    bt = 6 + dbuf
                    pe_take(bt)
                    pl.wait("tensor", sEV, tev)
                    for tt in range(4):
                        ttr = transp(pb[bt][:, tt * 128:(tt + 1) * 128],
                                     stf[:, dbuf * CH + tt * 128:
                                         dbuf * CH + (tt + 1) * 128],
                                     inc=(sMM, 1) if tt == 3 else None)
                    stf_free[dbuf] = ttr
                    pl.wait("scalar", sMM, ttr)
                    pl.wait("scalar", sSTo[dbuf], sto_free[dbuf])
                    sACt = pl.op("scalar", "activation",
                                 sto[:, dbuf * CH:(dbuf + 1) * CH],
                                 pb[bt][:, 0:CH], AF.Identity,
                                 bias=0.0, scale=1.0, inc=(sAC, 1))
                    bfree[bt] = (sAC, sACt)
                    pl.wait("sync", sAC, sACt)
                    tst = pl.dma("sync", sSTo[dbuf],
                                 out_h[own, :, dmch * 128:(dmch + 1) * 128]
                                 .rearrange("(t p) f -> p t f", p=128),
                                 sto[:, dbuf * CH:(dbuf + 1) * CH]
                                 .rearrange("p (t f) -> p t f", t=4))
                    sto_free[dbuf] = tst
                pl.wait("sync", sSTo[0], sSTo[0].n)
                pl.wait("sync", sSTo[1], sSTo[1].n)
                pl.wait("sync", sST, sST.n)
            accs.close()
            pl.barrier(dsems)

        pl.wait("sync", sSTo[0], sSTo[0].n)
        pl.wait("sync", sSTo[1], sSTo[1].n)
        pl.wait("sync", sST, sST.n)

        with nc.allow_low_precision(
                "fp32r outputs are the fp32-emulating hi/lo bf16 rounding; "
                "all matmul accumulation stays fp32 in PSUM"), \
             nc.Block() as block:
            @block.sync
            def _(sync):
                for fn in pl.q["sync"]:
                    fn(sync)

            @block.gpsimd
            def _(gpsimd):
                for fn in pl.q["gpsimd"]:
                    fn(gpsimd)

            @block.vector
            def _(vector):
                for fn in pl.q["vector"]:
                    fn(vector)

            @block.scalar
            def _(scalar):
                for fn in pl.q["scalar"]:
                    fn(scalar)

            @block.tensor
            def _(tensor):
                for fn in pl.q["tensor"]:
                    fn(tensor)

    return nc


def make_perm(j):
    A, Bc = j, 7 - j
    slots = [None] * 8
    slots[3], slots[7] = A, Bc
    for c in range(A):
        slots[c] = c
    rest = [c for c in range(8) if c not in (A, Bc) and c >= A]
    free = [s for s in (0, 1, 2, 4, 5, 6) if slots[s] is None]
    for s, c in zip(free, rest):
        slots[s] = c
    return slots, A, Bc


def rope_tables(positions):
    d2 = np.arange(0, HD, 2, dtype=np.float64) / HD
    inv = 1.0 / (10000.0 ** d2)
    fr = np.outer(positions.astype(np.float64), inv)
    emb = np.concatenate([fr, fr], axis=1)
    return (np.cos(emb).T.astype(np.float32),
            np.sin(emb).T.astype(np.float32))


def prep_core_inputs(core, x, Wt, consts):
    b, j = core // 4, core % 4
    A, Bc = j, 7 - j
    xT = np.ascontiguousarray(x[b].T)
    xTp = np.empty((2, NKT, 128, CH), dtype=np.float32)
    cosk = np.empty((2, 128, CH), dtype=np.float32)
    sink = np.empty((2, 128, CH), dtype=np.float32)
    for s, c in enumerate((A, Bc)):
        blkT = xT[:, c * CH:(c + 1) * CH]
        xTp[s] = blkT.reshape(NKT, 128, CH)
        pos = np.arange(c * CH, (c + 1) * CH)
        cosk[s], sink[s] = rope_tables(pos)
    def chunk_at(p):
        return p // 2 if p % 2 == 0 else 7 - (p // 2)
    base = np.full((128, 13), 44.0, dtype=np.float32)
    for kc in range(3):
        if not (kc < A):
            base[:, kc] = NEG
    for p in range(8):
        if not (chunk_at(p) < Bc):
            base[:, 4 + p] = NEG
    d = dict(Wt)
    d.update(consts)
    d["xTp"] = xTp
    d["cosk"] = cosk
    d["sink"] = sink
    d["base"] = base
    return d, (b, A, Bc)


def prep_shared(L0_attn_wq, L0_attn_wk, L0_attn_wv, L0_attn_wo,
                L0_ffn_w1_q, L0_ffn_w1_s, L0_ffn_w3_q, L0_ffn_w3_s,
                L0_ffn_w2_q, L0_ffn_w2_s):
    f32 = np.float32
    wqT = np.ascontiguousarray(np.asarray(L0_attn_wq, f32).T)
    wkT = np.ascontiguousarray(np.asarray(L0_attn_wk, f32).T)
    wvT = np.ascontiguousarray(np.asarray(L0_attn_wv, f32).T)
    woT = np.ascontiguousarray(np.asarray(L0_attn_wo, f32).T)
    w1T = np.ascontiguousarray(np.asarray(L0_ffn_w1_q, f32).T)
    w3T = np.ascontiguousarray(np.asarray(L0_ffn_w3_q, f32).T)
    w2T = np.ascontiguousarray(np.asarray(L0_ffn_w2_q, f32).T)

    wq_t = np.empty((KV, NKT, 128, 512), f32)
    for kvh in range(KV):
        for kt in range(NKT):
            wq_t[kvh, kt] = wqT[kt * 128:(kt + 1) * 128,
                                kvh * 512:(kvh + 1) * 512]
    wk_t = np.empty((KV, NKT, 128, 128), f32)
    for kvh in range(KV):
        for kt in range(NKT):
            wk_t[kvh, kt] = wkT[kt * 128:(kt + 1) * 128,
                                kvh * 128:(kvh + 1) * 128]
    wv_t = np.empty((2, NKT, 128, 512), f32)
    for vch in range(2):
        for kt in range(NKT):
            wv_t[vch, kt] = wvT[kt * 128:(kt + 1) * 128,
                                vch * 512:(vch + 1) * 512]
    wo_t = np.empty((4, 32, 8, 128, 128), f32)
    for qg in range(4):
        for dmch in range(32):
            for ik in range(8):
                qk = qg * 8 + ik
                wo_t[qg, dmch, ik] = woT[qk * 128:(qk + 1) * 128,
                                         dmch * 128:(dmch + 1) * 128]
    w13_t = np.empty((IC, 4, 2, 8, 128, 128), f32)
    for ich in range(IC):
        for kt in range(NKT):
            w13_t[ich, kt // 8, 0, kt % 8] = w1T[kt * 128:(kt + 1) * 128,
                                                 ich * 128:(ich + 1) * 128]
            w13_t[ich, kt // 8, 1, kt % 8] = w3T[kt * 128:(kt + 1) * 128,
                                                 ich * 128:(ich + 1) * 128]
    w2_t = np.empty((2752, 128, 128), f32)
    idx = 0
    ic0 = 0
    for g, gsz in enumerate(FFN_GROUPS):
        for dmch in range(32):
            for ik in range(gsz):
                ich = ic0 + ik
                w2_t[idx] = w2T[ich * 128:(ich + 1) * 128,
                                dmch * 128:(dmch + 1) * 128]
                idx += 1
        ic0 += gsz
    assert idx == 2752

    scl = np.zeros((128, 2 * IC + 32), f32)
    scl[:, 0:IC] = np.asarray(L0_ffn_w1_s, f32).reshape(IC, 128).T
    scl[:, IC:2 * IC] = np.asarray(L0_ffn_w3_s, f32).reshape(IC, 128).T
    scl[:, 2 * IC:] = np.asarray(L0_ffn_w2_s, f32).reshape(32, 128).T

    csts = np.zeros((128, 24), f32)
    csts[:, 2] = EPS
    csts[:, 8:22] = 1.0

    idn = np.eye(128, dtype=f32)
    prot = np.zeros((128, 128), f32)
    for e in range(64):
        prot[e + 64, e] = -1.0
    for e in range(64):
        prot[e, e + 64] = 1.0
    tri = np.zeros((4, 128, CH), f32)
    for kt in range(4):
        kj = kt * 128 + np.arange(128)[:, None]
        qi = np.arange(CH)[None, :]
        tri[kt] = np.where(kj <= qi, 0.0, NEG)
    onec = np.ones((128, 1), f32)
    oner = np.ones((1, 128), f32)

    Wt = {"wq_t": wq_t, "wk_t": wk_t, "wv_t": wv_t, "wo_t": wo_t,
          "w13_t": w13_t, "w2_t": w2_t, "scl": scl}
    consts = {"csts": csts, "idn": idn, "prot": prot, "tri": tri,
              "onec": onec, "oner": oner}
    return Wt, consts


def kernel(L0_attn_hidden_states, L0_attn_wq, L0_attn_wk, L0_attn_wv,
           L0_attn_wo, L0_ffn_w1_q, L0_ffn_w1_s, L0_ffn_w3_q, L0_ffn_w3_s,
           L0_ffn_w2_q, L0_ffn_w2_s):
    x = np.asarray(L0_attn_hidden_states, np.float32)
    Wt, consts = prep_shared(L0_attn_wq, L0_attn_wk, L0_attn_wv, L0_attn_wo,
                             L0_ffn_w1_q, L0_ffn_w1_s, L0_ffn_w3_q,
                             L0_ffn_w3_s, L0_ffn_w2_q, L0_ffn_w2_s)
    nc = build(debug=False)
    in_maps = []
    meta = []
    for core in range(8):
        d, m = prep_core_inputs(core, x, Wt, consts)
        in_maps.append(d)
        meta.append(m)
    res = run_bass_kernel_spmd(nc, in_maps, core_ids=list(range(8)))
    out = np.empty((B, S, D), np.float32)
    for core in range(8):
        b, A, Bc = meta[core]
        o = res.results[core]["out"]
        out[b, A * CH:(A + 1) * CH] = o[0]
        out[b, Bc * CH:(Bc + 1) * CH] = o[1]
    return out


# version 12
# speedup vs baseline: 1.0272x; 1.0272x over previous
import math
import numpy as np
from contextlib import ExitStack

import concourse.bass as bass
import concourse.mybir as mybir
from concourse.bass_utils import run_bass_kernel_spmd

F32 = mybir.dt.float32
F32R = mybir.dt.float32r
AF = mybir.ActivationFunctionType
ALU = mybir.AluOpType
AX = mybir.AxisListType

B, S, D = 2, 4096, 4096
H, KV, HD = 32, 8, 128
G = H // KV
I = 11008
IC = I // 128
NKT = D // 128
CH = 512
NSLOT = 8
NSL = (4, 9)
SCALE = 1.0 / math.sqrt(HD)
EPS = 1e-5
NEG = -1.0e30
FFN_GROUPS = [8] * 10 + [6]
ENGS = ("sync", "gpsimd", "vector", "scalar", "tensor")


class Sem:
    def __init__(self, h):
        self.h = h
        self.n = 0

    def bump(self, d=1):
        self.n += d
        return self.n


class Plan:
    def __init__(self):
        self.q = {e: [] for e in ENGS}
        self.autosem = {}

    def emit(self, eng, fn):
        self.q[eng].append(fn)

    def dma(self, eng, sem, out_ap, in_ap):
        t = sem.bump(16)
        self.emit(eng, lambda e, o=out_ap, i=in_ap, s=sem.h:
                  e.dma_start(out=o, in_=i).then_inc(s, 16))
        return t

    def wait(self, eng, sem, val):
        if val > 0:
            self.emit(eng, lambda e, s=sem.h, v=val: e.wait_ge(s, v))

    def op(self, eng, name, *a, inc=None, **kw):
        auto = self.autosem.get(eng)
        if auto is not None:
            if auto.n > 0:
                self.wait(eng, auto, auto.n)
            if inc is None:
                inc = (auto, 1)
            else:
                assert inc[0] is auto, "non-auto inc on %s" % eng
        t = None
        if inc is not None:
            t = inc[0].bump(inc[1])
            ih = inc[0].h
        else:
            ih = None

        def f(e, nm=name, aa=a, kk=kw, hh=ih, vv=(inc[1] if inc else 0)):
            ins = getattr(e, nm)(*aa, **kk)
            if hh is not None:
                ins.then_inc(hh, vv)
        self.emit(eng, f)
        return t

    def barrier(self, dsems):
        for e in ENGS:
            s = dsems[e]
            s.bump(1)
            self.emit(e, lambda en, h=s.h: en.sem_inc(h, 1))
        for e in ENGS:
            for e2 in ENGS:
                if e2 != e:
                    self.wait(e, dsems[e2], dsems[e2].n)


def build(debug=False):
    nc = bass.Bass("TRN2", target_bir_lowering=False)
    dr = lambda n, sh, dt_=F32, k="ExternalInput": nc.dram_tensor(n, sh, dt_, kind=k).ap()

    xTp = dr("xTp", [2, NKT, 128, CH])
    wq_t = dr("wq_t", [KV, NKT, 128, 512])
    wk_t = dr("wk_t", [KV, NKT, 128, 128])
    wv_t = dr("wv_t", [2, NKT, 128, 512])
    wo_t = dr("wo_t", [4, 32, 8, 128, 128])
    w13_t = dr("w13_t", [IC, 4, 2, 8, 128, 128])
    w2_t = dr("w2_t", [2752, 128, 128])
    scl_h = dr("scl", [128, 2 * IC + 32])
    cst_h = dr("csts", [128, 24])
    idn_h = dr("idn", [128, 128])
    prot_h = dr("prot", [128, 128])
    tri_h = dr("tri", [4, 128, CH])
    base_h = dr("base", [128, 13])
    cosk_h = dr("cosk", [2, 128, CH])
    sink_h = dr("sink", [2, 128, CH])
    onec_h = dr("onec", [128, 1])
    oner_h = dr("oner", [1, 128])
    out_h = dr("out", [2, CH, D], F32, "ExternalOutput")

    dbg = {}
    if debug:
        for nm, sh, dt_ in [("d_xn3", [NKT, 128, CH], F32R),
                            ("d_k00", [128, CH], F32R),
                            ("d_v00", [128, 512], F32R),
                            ("d_q0", [128, CH], F32R),
                            ("d_attn0", [128, CH], F32R),
                            ("d_acc0", [128, CH], F32)]:
            dbg[nm] = dr(nm, sh, dt_, "ExternalOutput")

    k_loc = dr("k_loc", [2, KV, 128, CH], F32R, "Internal")
    v_loc = dr("v_loc", [2, 4, 128, KV * 128], F32R, "Internal")
    k_gat = dr("k_gat", [8, KV, 128, CH], F32R, "Internal")
    v_gat = dr("v_gat", [8, 4, 128, KV * 128], F32R, "Internal")
    xn_sc = dr("xn_sc", [2, NKT, 128, CH], F32R, "Internal")
    at_sc = dr("at_sc", [H, 128, CH], F32R, "Internal")

    pl = Plan()

    with ExitStack() as st:
        def mksem(name):
            return Sem(st.enter_context(nc.semaphore(name)))

        sLD = mksem("sLD")
        sLW = mksem("sLW")
        sSP = mksem("sSP")
        sMM = mksem("sMM")
        sEV = mksem("sEV")
        sAC = mksem("sAC")
        sST = mksem("sST")
        dsems = {e: mksem("sdn_" + e) for e in ENGS}
        pl.autosem = {"vector": sEV, "scalar": sAC}
        s_cs = mksem("s_cs")
        sSPx = mksem("sSPx"); sSPk = mksem("sSPk"); sSPv = mksem("sSPv")
        sSPa = mksem("sSPa")
        sSTo = [mksem("sSTo0"), mksem("sSTo1")]
        sCC = mksem("sCC")

        c_idn = st.enter_context(nc.sbuf_tensor("c_idn", [128, 128], F32))
        c_idnr = st.enter_context(nc.sbuf_tensor("c_idnr", [128, 128], F32R))
        c_prot = st.enter_context(nc.sbuf_tensor("c_prot", [128, 128], F32R))
        c_tri = st.enter_context(nc.sbuf_tensor("c_tri", [128, 4 * CH], F32R))
        c_scl = st.enter_context(nc.sbuf_tensor("c_scl", [128, 2 * IC + 32], F32))
        c_cst = st.enter_context(nc.sbuf_tensor("c_cst", [128, 24], F32))
        c_onec = st.enter_context(nc.sbuf_tensor("c_onec", [128, 1], F32R))
        c_oner = st.enter_context(nc.sbuf_tensor("c_oner", [1, 128], F32R))
        c_cos = st.enter_context(nc.sbuf_tensor("c_cos", [128, CH], F32))
        c_sin = st.enter_context(nc.sbuf_tensor("c_sin", [128, CH], F32))

        pl.dma("sync", sLD, c_idn[:], idn_h)
        pl.dma("gpsimd", sLW, c_tri[:].rearrange("p (q f) -> p q f", q=4),
               tri_h.rearrange("q p f -> p q f"))
        pl.dma("gpsimd", sLW, c_idnr[:], idn_h)
        pl.dma("sync", sLD, c_scl[:], scl_h)
        pl.dma("sync", sLD, c_cst[:], cst_h)
        pl.dma("gpsimd", sLW, c_prot[:], prot_h)
        pl.dma("gpsimd", sLW, c_onec[:], onec_h)
        pl.dma("gpsimd", sLW, c_oner[:], oner_h)
        for e in ("vector", "scalar", "tensor"):
            pl.wait(e, sLD, sLD.n)
            pl.wait(e, sLW, sLW.n)

        pb = [st.enter_context(nc.psum_tensor("pb%d" % i, [128, 512], F32))
              for i in range(8)]
        bfree = [None] * 8

        def pe_take(bi):
            if bfree[bi] is not None:
                pl.wait("tensor", bfree[bi][0], bfree[bi][1])

        def mm(bi, out_ap, lhsT, rhs, start, stop, inc=None):
            def f(e, o=out_ap, l=lhsT, r=rhs, s=start, p=stop,
                  hh=(inc[0].h if inc else None), vv=(inc[1] if inc else 0)):
                ins = e.matmul(o, l, r, start=s, stop=p)
                if hh is not None:
                    ins.then_inc(hh, vv)
            if inc is not None:
                t = inc[0].bump(inc[1])
            else:
                t = None
            pl.emit("tensor", f)
            return t

        def transp(out_ap, in_ap, inc=None):
            def f(e, o=out_ap, i=in_ap,
                  hh=(inc[0].h if inc else None), vv=(inc[1] if inc else 0)):
                ins = e.transpose(o, i, c_idn[:])
                if hh is not None:
                    ins.then_inc(hh, vv)
            if inc is not None:
                t = inc[0].bump(inc[1])
            else:
                t = None
            pl.emit("tensor", f)
            return t

        def load_cos_sin(slot):
            pl.dma("sync", s_cs, c_cos[:], cosk_h[slot])
            t2 = pl.dma("sync", s_cs, c_sin[:], sink_h[slot])
            return t2

        with ExitStack() as p1:
            sb1 = lambda n, sh, dt_: p1.enter_context(nc.sbuf_tensor(n, sh, dt_))
            A_buf = sb1("A_buf", [128, NKT * CH], F32R)
            sq_b = sb1("sq_b", [128, CH], F32R)
            wkb = sb1("wkb", [128, 2 * NKT * 128], F32R)
            wvb = sb1("wvb", [128, 2 * 4 * 512], F32R)
            krawb = sb1("krawb", [128, CH], F32R)
            ktmp = sb1("ktmp", [128, 2 * CH], F32)
            K_out = sb1("K_out", [128, CH], F32R)
            V_out = sb1("V_out", [128, 512], F32R)
            mrow = sb1("mrow", [1, CH], F32R)
            rrow = sb1("rrow", [1, CH], F32R)
            srow = sb1("srow", [1, CH], F32)
            sp1 = sb1("sp1", [1, 8], F32)

            s_x = mksem("s_x")
            s_wk = [mksem("s_wk0"), mksem("s_wk1")]
            s_wv = [mksem("s_wv0"), mksem("s_wv1")]
            wk_free = [0, 0]
            wv_free = [0, 0]
            abuf_free_mm = 0
            abuf_spill = 0
            kout_spill = 0
            vout_spill = 0

            for s in range(2):
                pl.wait("sync", sEV, sEV.n)
                tcs = load_cos_sin(s)
                pl.wait("gpsimd", sMM, abuf_free_mm)
                pl.wait("gpsimd", sSPx, abuf_spill)
                for blk in range(4):
                    kt0 = blk * 8
                    tx_all = pl.dma("gpsimd", s_x,
                                    A_buf[:, kt0 * CH:(kt0 + 8) * CH]
                                    .rearrange("p (k f) -> p k f", k=8),
                                    xTp[s, kt0:kt0 + 8].rearrange("k p f -> p k f"))

                pe_take(0)
                pe_take(1)
                last_sq = 0
                pl.wait("tensor", s_x, tx_all)
                pl.wait("vector", s_x, tx_all)
                for kt in range(NKT):
                    xt = A_buf[:, kt * CH:(kt + 1) * CH]
                    mm(0, pb[0][0:1, :], c_onec[:], xt, kt == 0, kt == NKT - 1)
                    pl.wait("vector", sMM, last_sq)
                    tev = pl.op("vector", "tensor_mul", sq_b[:],
                                xt.bitcast(F32), xt.bitcast(F32), inc=(sEV, 1))
                    pl.wait("tensor", sEV, tev)
                    last_sq = mm(1, pb[1][0:1, :], c_onec[:], sq_b[:],
                                 kt == 0, kt == NKT - 1, inc=(sMM, 1))

                pl.wait("vector", sMM, last_sq)
                t0 = pl.op("vector", "tensor_scalar", mrow[:],
                           pb[0][0:1, :], 1.0 / D, None, ALU.mult,
                           inc=(sEV, 1))
                bfree[0] = (sEV, t0)
                t1 = pl.op("vector", "tensor_scalar", srow[:], pb[1][0:1, :],
                           1.0 / D, None, ALU.mult, inc=(sEV, 1))
                bfree[1] = (sEV, t1)
                pl.wait("vector", sEV, t1)
                pl.op("vector", "tensor_mul", ktmp[0:1, 0:CH],
                      mrow[:].bitcast(F32), mrow[:].bitcast(F32))
                tev = pl.op("vector", "tensor_tensor", srow[:], srow[:],
                            ktmp[0:1, 0:CH], ALU.subtract, inc=(sEV, 1))
                pl.wait("scalar", sEV, tev)
                tac = pl.op("scalar", "activation", srow[:], srow[:], AF.Sqrt,
                            bias=c_cst[0:1, 2:3], scale=1.0, inc=(sAC, 1))
                pl.wait("vector", sAC, tac)
                pl.op("vector", "tensor_copy", sp1[:, 0:1], c_cst[0:1, 0:1])
                tev = pl.op("vector", "reciprocal", rrow[:],
                            srow[:], inc=(sEV, 1))

                pe_take(2)
                pe_take(3)
                pl.wait("tensor", sEV, tev)
                mm(2, pb[2][:, 0:CH], c_oner[:], mrow[:], True, True)
                tmm = mm(3, pb[3][:, 0:CH], c_oner[:], rrow[:], True, True,
                         inc=(sMM, 1))

                pl.wait("vector", sMM, tmm)
                tev = 0
                for kt in range(NKT):
                    xt = A_buf[:, kt * CH:(kt + 1) * CH]
                    pl.op("vector", "tensor_tensor", ktmp[:, 0:CH],
                          xt.bitcast(F32), pb[2][:, 0:CH], ALU.subtract)
                    inc = (sEV, 1) if kt == NKT - 1 else None
                    tev2 = pl.op("vector", "tensor_tensor", xt, ktmp[:, 0:CH],
                                 pb[3][:, 0:CH], ALU.mult, inc=inc)
                    if tev2 is not None:
                        tev = tev2
                bfree[2] = (sEV, tev)
                bfree[3] = (sEV, tev)

                if True:
                    own = s
                    pl.wait("sync", sEV, tev)
                    abuf_spill = pl.dma("sync", sSPx,
                                        xn_sc[own].rearrange("k p f -> p k f"),
                                        A_buf[:].rearrange("p (k f) -> p k f",
                                                           k=NKT))
                    if debug and s == 0:
                        pl.dma("sync", sST,
                               dbg["d_xn3"].rearrange("k p f -> p k f"),
                               A_buf[:]
                               .rearrange("p (k f) -> p k f", k=NKT))
                        pl.wait("gpsimd", sST, sST.n)
                        pl.wait("tensor", sST, sST.n)
                        pl.wait("vector", sST, sST.n)

                pl.wait("tensor", sEV, tev)
                pl.wait("vector", s_cs, tcs)

                for kvh in range(KV):
                    buf = kvh % 2
                    pl.wait("gpsimd", sMM, wk_free[buf])
                    twk = pl.dma("gpsimd", s_wk[buf],
                                 wkb[:, buf * NKT * 128:(buf + 1) * NKT * 128]
                                 .rearrange("p (k f) -> p k f", k=NKT),
                                 wk_t[kvh].rearrange("k p f -> p k f"))
                    pl.wait("tensor", s_wk[buf], twk)
                    bk = 4 + (kvh % 2)
                    pe_take(bk)
                    for kt in range(NKT):
                        tmm = mm(bk, pb[bk][:, 0:CH],
                                 wkb[:, (buf * NKT + kt) * 128:
                                     (buf * NKT + kt + 1) * 128],
                                 A_buf[:, kt * CH:(kt + 1) * CH],
                                 kt == 0, kt == NKT - 1,
                                 inc=(sMM, 1) if kt == NKT - 1 else None)
                    wk_free[buf] = tmm
                    pl.wait("vector", sMM, tmm)
                    sEVt = pl.op("vector", "tensor_copy", krawb[:],
                                 pb[bk][:, 0:CH], inc=(sEV, 1))
                    bfree[bk] = (sEV, sEVt)
                    pe_take(6)
                    pl.wait("tensor", sEV, sEVt)
                    trot = mm(6, pb[6][:, 0:CH], c_prot[:], krawb[:], True, True,
                              inc=(sMM, 1))
                    pl.wait("vector", sSPk, kout_spill)
                    pl.wait("vector", sEV, sEVt)
                    pl.op("vector", "tensor_tensor", ktmp[:, 0:CH],
                          krawb[:].bitcast(F32), c_cos[:], ALU.mult)
                    pl.wait("vector", sMM, trot)
                    t6 = pl.op("vector", "tensor_tensor", ktmp[:, CH:2 * CH],
                               pb[6][:, 0:CH], c_sin[:], ALU.mult, inc=(sEV, 1))
                    bfree[6] = (sEV, t6)
                    tev = pl.op("vector", "tensor_add", K_out[:],
                                ktmp[:, 0:CH], ktmp[:, CH:2 * CH], inc=(sEV, 1))
                    pl.wait("sync", sEV, tev)
                    kout_spill = pl.dma("sync", sSPk, k_loc[s, kvh], K_out[:])
                    if debug and s == 0 and kvh == 0:
                        pl.dma("sync", sST, dbg["d_k00"], K_out[:])
                        pl.wait("vector", sST, sST.n)

                last_v = 0
                for vch in range(2):
                    for tt in range(4):
                        pe_take(tt)
                    for blk in range(8):
                        buf = blk % 2
                        pl.wait("gpsimd", sMM, wv_free[buf])
                        twv = pl.dma("gpsimd", s_wv[buf],
                                     wvb[:, buf * 4 * 512:(buf + 1) * 4 * 512]
                                     .rearrange("p (k f) -> p k f", k=4),
                                     wv_t[vch, blk * 4:(blk + 1) * 4].rearrange("k p f -> p k f"))
                        pl.wait("tensor", s_wv[buf], twv)
                        for k4 in range(4):
                            kt = blk * 4 + k4
                            for tt in range(4):
                                inc = (sMM, 1) if (k4 == 3 and tt == 3) else None
                                tmm = mm(tt, pb[tt][:, 0:512],
                                         A_buf[:, kt * CH + tt * 128:
                                               kt * CH + (tt + 1) * 128],
                                         wvb[:, (buf * 4 + k4) * 512:
                                             (buf * 4 + k4 + 1) * 512],
                                         kt == 0, kt == NKT - 1, inc=inc)
                                if tmm is not None:
                                    wv_free[buf] = tmm
                                    last_v = tmm
                    pl.wait("vector", sMM, last_v)
                    for tt in range(4):
                        pl.wait("vector", sSPv, vout_spill)
                        sEVt = pl.op("vector", "tensor_copy", V_out[:],
                                     pb[tt][:, 0:512], inc=(sEV, 1))
                        bfree[tt] = (sEV, sEVt)
                        pl.wait("sync", sEV, sEVt)
                        vout_spill = pl.dma("sync", sSPv,
                                            v_loc[s, tt, :, vch * 512:
                                                 (vch + 1) * 512], V_out[:])
                        if debug and s == 0 and tt == 0 and vch == 0:
                            pl.dma("sync", sST, dbg["d_v00"], V_out[:])
                            pl.wait("vector", sST, sST.n)
                abuf_free_mm = last_v

            t_spills_p1 = (sSPx.n, sSPk.n, sSPv.n)
            pl.wait("sync", sSPx, sSPx.n)
            pl.wait("sync", sSPk, sSPk.n)
            pl.wait("sync", sSPv, sSPv.n)
            pl.wait("gpsimd", sSPk, sSPk.n)
            pl.wait("gpsimd", sSPv, sSPv.n)
            pl.emit("gpsimd", lambda g: g.collective_compute(
                "AllGather", ALU.bypass,
                replica_groups=[[0, 1, 2, 3], [4, 5, 6, 7]],
                ins=[k_loc[:]], outs=[k_gat[:]]).then_inc(sCC.h, 1))
            sCC.bump(1)
            pl.emit("gpsimd", lambda g: g.collective_compute(
                "AllGather", ALU.bypass,
                replica_groups=[[0, 1, 2, 3], [4, 5, 6, 7]],
                ins=[v_loc[:]], outs=[v_gat[:]]).then_inc(sCC.h, 1))
            sCC.bump(1)
            t_cc = sCC.n
        pl.barrier(dsems)

        for own in range(2):
            nsl = NSL[own]
            ncan = nsl - 1
            base_off12 = 0 if own == 0 else 4
            tcs = load_cos_sin(own)

            with ExitStack() as p2:
                sb2 = lambda n, sh, dt_: p2.enter_context(
                    nc.sbuf_tensor(n + "_%d" % own, sh, dt_))
                q_all = sb2("q_all", [128, H * CH], F32R)
                qraw = sb2("qraw", [128, CH], F32R)
                K_sb = sb2("K_sb", [128, 2 * nsl * CH], F32R)
                V_sb = sb2("V_sb", [128, 2 * nsl * 4 * 128], F32R)
                baseb = sb2("baseb", [128, 13], F32)
                kk_r = sb2("kk_r", [128, CH], F32R)
                qq_r = sb2("qq_r", [128, CH], F32R)
                kcmax = sb2("kcmax", [1, 16], F32)
                km2 = sb2("km2", [1, 16], F32)
                qn2_row = sb2("qn2_row", [1, 4 * CH], F32)
                nm_row = sb2("nm_row", [1, 2 * 4 * CH], F32R)
                rc_row = sb2("rc_row", [1, CH], F32R)
                rcrep = sb2("rcrep", [128, CH], F32)
                expT = sb2("expT", [128, 2 * CH], F32R)
                pvout = sb2("pvout", [128, CH], F32R)
                p2a_st = ExitStack()
                xnb = p2a_st.enter_context(nc.sbuf_tensor(
                    "xnb_%d" % own, [128, 2 * 2 * CH], F32R))
                wqb = p2a_st.enter_context(nc.sbuf_tensor(
                    "wqb_%d" % own, [128, 2 * 2 * 512], F32R))

                s_base = mksem("s_base%d" % own)
                s_kv = [mksem("s_kv%d_0" % own), mksem("s_kv%d_1" % own)]
                s_wq = [mksem("s_wq%d_0" % own), mksem("s_wq%d_1" % own)]
                s_xn2 = [mksem("s_xn%d_0" % own), mksem("s_xn%d_1" % own)]
                tb = pl.dma("sync", s_base, baseb[:], base_h)
                pl.wait("scalar", s_base, tb)
                pl.wait("sync", sSPx, t_spills_p1[0])
                pl.wait("sync", sSPk, t_spills_p1[1])
                pl.wait("sync", sSPv, t_spills_p1[2])

                wq_free = [0, 0]
                xn_free = [0, 0]
                kv_free = [0, 0]
                kv_tick = [0, 0]
                expT_free = [(0, 0), (0, 0)]
                sumr_free = [0]
                pvout_sp = 0

                for kvh in range(KV):
                    for ht in range(4):
                        pe_take(ht)
                    for blk in range(16):
                        buf = blk % 2
                        pl.wait("gpsimd", sMM, wq_free[buf])
                        twq = pl.dma("gpsimd", s_wq[buf],
                                     wqb[:, buf * 2 * 512:(buf + 1) * 2 * 512]
                                     .rearrange("p (k f) -> p k f", k=2),
                                     wq_t[kvh, blk * 2:(blk + 1) * 2]
                                     .rearrange("k p f -> p k f"))
                        pl.wait("sync", sMM, xn_free[buf])
                        txn = pl.dma("sync", s_xn2[buf],
                                     xnb[:, buf * 2 * CH:(buf + 1) * 2 * CH]
                                     .rearrange("p (k f) -> p k f", k=2),
                                     xn_sc[own, blk * 2:(blk + 1) * 2]
                                     .rearrange("k p f -> p k f"))
                        pl.wait("tensor", s_wq[buf], twq)
                        pl.wait("tensor", s_xn2[buf], txn)
                        for k4 in range(2):
                            kt = blk * 2 + k4
                            for ht in range(4):
                                inc = (sMM, 1) if (k4 == 1 and ht == 3) else None
                                tmm = mm(ht, pb[ht][:, 0:CH],
                                         wqb[:, (buf * 2 + k4) * 512 + ht * 128:
                                             (buf * 2 + k4) * 512 + (ht + 1) * 128],
                                         xnb[:, (buf * 2 + k4) * CH:
                                             (buf * 2 + k4 + 1) * CH],
                                         kt == 0, kt == NKT - 1, inc=inc)
                        wq_free[buf] = tmm
                        xn_free[buf] = tmm
                    tqp = sMM.n
                    pl.wait("vector", sMM, tqp)
                    if kvh == 0:
                        pl.wait("vector", s_cs, tcs)
                    for ht in range(4):
                        head = kvh * 4 + ht
                        sEVt = pl.op("vector", "tensor_copy", qraw[:],
                                     pb[ht][:, 0:CH], inc=(sEV, 1))
                        bfree[ht] = (sEV, sEVt)
                        pe_take(4)
                        pl.wait("tensor", sEV, sEVt)
                        trot = mm(4, pb[4][:, 0:CH], c_prot[:], qraw[:],
                                  True, True, inc=(sMM, 1))
                        pl.wait("vector", sEV, sEVt)
                        pl.op("vector", "tensor_tensor",
                              kk_r[:],
                              qraw[:].bitcast(F32), c_cos[:], ALU.mult)
                        pl.wait("vector", sMM, trot)
                        t4 = pl.op("vector", "tensor_tensor",
                                   qq_r[:],
                                   pb[4][:, 0:CH], c_sin[:], ALU.mult,
                                   inc=(sEV, 1))
                        bfree[4] = (sEV, t4)
                        pl.op("vector", "tensor_add",
                              q_all[:, head * CH:(head + 1) * CH],
                              kk_r[:].bitcast(F32), qq_r[:].bitcast(F32),
                              inc=(sEV, 1))
                tq_ready = sEV.n
                pl.wait("tensor", sEV, tq_ready)
                if debug and own == 0:
                    pl.wait("sync", sEV, tq_ready)
                    pl.dma("sync", sST, dbg["d_q0"], q_all[:, 0:CH])
                    pl.wait("vector", sST, sST.n)
                p2a_st.close()
                sume = sb2("sume", [128, CH], F32)
                sume_r = sb2("sume_r", [128, CH], F32R)

                csl = slice(0, 6, 2) if own == 0 else slice(0, 8)
                canon = (range(0, 6, 2) if own == 0 else range(8))

                def kv_load(kvh):
                    sl = kvh % 2
                    ko = sl * nsl * CH
                    vo = sl * nsl * 4 * 128
                    pl.wait("scalar", sMM, kv_free[sl])
                    if kvh == 0:
                        pl.wait("scalar", sCC, t_cc)
                    pl.dma("scalar", s_kv[sl],
                           K_sb[:, ko:ko + ncan * CH]
                           .rearrange("p (s f) -> p s f", s=ncan),
                           k_gat[csl, kvh].rearrange("s p f -> p s f"))
                    pl.dma("scalar", s_kv[sl],
                           K_sb[:, ko + ncan * CH:ko + nsl * CH],
                           k_loc[own, kvh])
                    for si, cp in enumerate(canon):
                        pl.dma("scalar", s_kv[sl],
                               V_sb[:, vo + si * 4 * 128:vo + (si + 1) * 4 * 128]
                               .rearrange("p (t f) -> p t f", t=4),
                               v_gat[cp, :, :, kvh * 128:(kvh + 1) * 128]
                               .rearrange("t p f -> p t f"))
                    kv_tick[sl] = pl.dma(
                        "scalar", s_kv[sl],
                        V_sb[:, vo + ncan * 4 * 128:vo + nsl * 4 * 128]
                        .rearrange("p (t f) -> p t f", t=4),
                        v_loc[own, :, :, kvh * 128:(kvh + 1) * 128]
                        .rearrange("t p f -> p t f"))

                def prep(kvh):
                    sl = kvh % 2
                    ko = sl * nsl * CH
                    pl.wait("vector", s_kv[sl], kv_tick[sl])
                    for kc in range(nsl):
                        tkk = pl.op("vector", "tensor_mul", kk_r[:],
                                    K_sb[:, ko + kc * CH:ko + (kc + 1) * CH]
                                    .bitcast(F32),
                                    K_sb[:, ko + kc * CH:ko + (kc + 1) * CH]
                                    .bitcast(F32), inc=(sEV, 1))
                        pe_take(7)
                        pl.wait("tensor", sEV, tkk)
                        tmm = mm(7, pb[7][0:1, 0:CH], c_onec[:], kk_r[:],
                                 True, True, inc=(sMM, 1))
                        pl.wait("vector", sMM, tmm)
                        tkn = pl.op("vector", "reduce_max",
                                    kcmax[0:1, kc:kc + 1],
                                    pb[7][0:1, 0:CH], axis=AX.X,
                                    inc=(sEV, 1))
                        bfree[7] = (sEV, tkn)
                    pl.op("vector", "tensor_copy", km2[0:1, sl * 8 + 6:sl * 8 + 7],
                          c_cst[0:1, 0:1])
                    pl.op("vector", "reduce_max", km2[0:1, sl * 8:sl * 8 + 1],
                          kcmax[0:1, 0:nsl], axis=AX.X)
                    pl.op("vector", "tensor_copy", km2[0:1, sl * 8 + 1:sl * 8 + 2],
                          c_cst[0:1, 0:1])
                    pl.op("vector", "tensor_copy", km2[0:1, sl * 8 + 3:sl * 8 + 4],
                          km2[0:1, sl * 8:sl * 8 + 1])
                    for ht in range(4):
                        head = kvh * 4 + ht
                        qv = q_all[:, head * CH:(head + 1) * CH]
                        tqq = pl.op("vector", "tensor_mul", qq_r[:],
                                    qv.bitcast(F32), qv.bitcast(F32),
                                    inc=(sEV, 1))
                        pe_take(7)
                        pl.wait("tensor", sEV, tqq)
                        tmm = mm(7, pb[7][0:1, 0:CH], c_onec[:], qq_r[:],
                                 True, True, inc=(sMM, 1))
                        pl.wait("vector", sMM, tmm)
                        tqn = pl.op("vector", "tensor_scalar",
                                    qn2_row[0:1, ht * CH:(ht + 1) * CH],
                                    pb[7][0:1, 0:CH],
                                    km2[0:1, sl * 8 + 3:sl * 8 + 4], None, ALU.mult,
                                    inc=(sEV, 1))
                        bfree[7] = (sEV, tqn)
                    pl.wait("scalar", sEV, sEV.n)
                    pl.op("scalar", "activation",
                          qn2_row[0:1, :], qn2_row[0:1, :], AF.Ln,
                          bias=0.0, scale=1.0, inc=(sAC, 1))
                    tsq = pl.op("scalar", "activation",
                                qn2_row[0:1, :], qn2_row[0:1, :], AF.Exp,
                                bias=0.0, scale=0.5, inc=(sAC, 1))
                    pl.wait("vector", sAC, tsq)
                    tnm = pl.op("vector", "tensor_scalar",
                                nm_row[0:1, sl * 4 * CH:(sl + 1) * 4 * CH],
                                qn2_row[0:1, :], -1.001, None,
                                ALU.mult, inc=(sEV, 1))
                    return tnm

                kv_load(0)
                kv_load(1)
                nm_ticks = [None] * KV
                nm_ticks[0] = prep(0)
                nm_ticks[1] = prep(1)
                pend_norm = [None]

                def emit_norm(head, bpv, bsm, t_pv):
                    nonlocal pvout_sp
                    pl.wait("vector", sMM, t_pv)
                    trc = pl.op("vector", "reciprocal", rc_row[0:1, :],
                                pb[bsm][0:1, 0:CH], inc=(sEV, 1))
                    bfree[bsm] = (sEV, trc)
                    pe_take(6)
                    pl.wait("tensor", sEV, trc)
                    tmm = mm(6, pb[6][:, 0:CH], c_oner[:], rc_row[0:1, :],
                             True, True, inc=(sMM, 1))
                    pl.wait("vector", sMM, tmm)
                    trcp = pl.op("vector", "tensor_copy", rcrep[:],
                                 pb[6][:, 0:CH], inc=(sEV, 1))
                    bfree[6] = (sEV, trcp)
                    pl.wait("vector", sEV, trcp)
                    pl.wait("vector", sSPa, pvout_sp)
                    tev = pl.op("vector", "tensor_mul", pvout[:],
                                pb[bpv][:, 0:CH], rcrep[:], inc=(sEV, 1))
                    bfree[bpv] = (sEV, tev)
                    pl.wait("sync", sEV, tev)
                    pvout_sp = pl.dma("sync", sSPa, at_sc[head], pvout[:])
                    if debug and own == 0 and head == 0:
                        pl.dma("sync", sST, dbg["d_attn0"], pvout[:])
                        pl.wait("vector", sST, sST.n)

                for kvh in range(KV):
                    sl = kvh % 2
                    ko = sl * nsl * CH
                    vo = sl * nsl * 4 * 128
                    pl.wait("tensor", sEV, nm_ticks[kvh])
                    pl.wait("tensor", s_kv[sl], kv_tick[sl])
                    for ht in range(4):
                        head = kvh * 4 + ht
                        qv = q_all[:, head * CH:(head + 1) * CH]
                        nmv = nm_row[0:1, (sl * 4 + ht) * CH:
                                     (sl * 4 + ht + 1) * CH]
                        hp = ht % 2
                        bpv, bsm = 4 + hp, 0 + hp
                        nktile = nsl * 4
                        pe_take(bpv)
                        exp_tick = [0] * nktile

                        def emit_score(ki):
                            bs = 2 + (ki % 2)
                            pe_take(bs)
                            mm(bs, pb[bs][:, 0:CH],
                               K_sb[:, ko + ki * 128:ko + (ki + 1) * 128],
                               qv, True, False)
                            if ki // 4 == nsl - 1:
                                mm(bs, pb[bs][:, 0:CH], c_idnr[:],
                                   c_tri[:, (ki % 4) * CH:(ki % 4 + 1) * CH],
                                   False, False)
                            tmm = mm(bs, pb[bs][:, 0:CH], c_oner[:],
                                     nmv, False, True, inc=(sMM, 1))
                            eb = ki % 2
                            pl.wait("scalar", sMM, tmm)
                            pl.wait("scalar", sMM, expT_free[eb][0])
                            pl.wait("scalar", sEV, expT_free[eb][1])
                            bcol = base_off12 + (ki // 4)
                            tex = pl.op("scalar", "activation",
                                        expT[:, eb * CH:(eb + 1) * CH],
                                        pb[bs][:, 0:CH], AF.Exp,
                                        bias=baseb[:, bcol:bcol + 1],
                                        scale=SCALE, inc=(sAC, 1))
                            bfree[bs] = (sAC, tex)
                            exp_tick[ki] = tex

                        def emit_pv(ki):
                            eb = ki % 2
                            pl.wait("tensor", sAC, exp_tick[ki])
                            tpv = mm(bpv, pb[bpv][:, 0:CH],
                                     V_sb[:, vo + ki * 128:vo + (ki + 1) * 128],
                                     expT[:, eb * CH:(eb + 1) * CH],
                                     ki == 0, ki == nktile - 1, inc=(sMM, 1))
                            pl.wait("vector", sAC, exp_tick[ki])
                            if ki == 0:
                                tad = pl.op("vector", "tensor_copy", sume[:],
                                            expT[:, 0:CH].bitcast(F32),
                                            inc=(sEV, 1))
                            else:
                                tad = pl.op("vector", "tensor_add", sume[:],
                                            sume[:],
                                            expT[:, eb * CH:(eb + 1) * CH]
                                            .bitcast(F32), inc=(sEV, 1))
                            expT_free[eb] = (tpv, tad)

                        emit_score(0)
                        for ki in range(nktile):
                            if ki + 1 < nktile:
                                emit_score(ki + 1)
                            emit_pv(ki)
                        pl.wait("vector", sMM, sumr_free[0])
                        tcv = pl.op("vector", "tensor_copy", sume_r[:],
                                    sume[:], inc=(sEV, 1))
                        pe_take(bsm)
                        pl.wait("tensor", sEV, tcv)
                        t_pv_done = mm(bsm, pb[bsm][0:1, 0:CH], c_onec[:],
                                       sume_r[:], True, True, inc=(sMM, 1))
                        sumr_free[0] = t_pv_done
                        if ht == 0 and kvh + 1 < KV:
                            nm_ticks[kvh + 1] = prep(kvh + 1)
                        if pend_norm[0] is not None:
                            emit_norm(*pend_norm[0])
                        pend_norm[0] = (head, bpv, bsm, t_pv_done)
                    kv_free[sl] = sMM.n
                    if kvh + 2 < KV:
                        kv_load(kvh + 2)
                emit_norm(*pend_norm[0])
                pend_norm[0] = None
                t_at_spills = sSPa.n
            pl.barrier(dsems)

            accs = ExitStack()
            acc = accs.enter_context(
                nc.sbuf_tensor("acc_%d" % own, [128, NKT * CH], F32))
            with ExitStack() as p2c:
                sb2c = lambda n, sh, dt_: p2c.enter_context(
                    nc.sbuf_tensor(n + "_%d" % own, sh, dt_))
                atg = sb2c("atg", [128, 2 * 8 * CH], F32R)
                wob = sb2c("wob", [128, 4 * 8 * 128], F32R)
                s_atg = [mksem("s_atg%d_0" % own), mksem("s_atg%d_1" % own)]
                s_wo = [mksem("s_wo%d_%d" % (own, i)) for i in range(4)]
                atg_free = [0, 0]
                wob_free = [0, 0, 0, 0]
                acc_tick = [0] * NKT
                pl.wait("sync", sSPa, t_at_spills)
                for qg in range(4):
                    gbuf = qg % 2
                    pl.wait("sync", sMM, atg_free[gbuf])
                    tat = pl.dma("sync", s_atg[gbuf],
                                 atg[:, gbuf * 8 * CH:(gbuf + 1) * 8 * CH]
                                 .rearrange("p (k f) -> p k f", k=8),
                                 at_sc[qg * 8:(qg + 1) * 8].rearrange("k p f -> p k f"))
                    pl.wait("tensor", s_atg[gbuf], tat)
                    for dmch in range(NKT):
                        wbuf = dmch % 4
                        pl.wait("gpsimd", sMM, wob_free[wbuf])
                        two = pl.dma("gpsimd", s_wo[wbuf],
                                     wob[:, wbuf * 8 * 128:(wbuf + 1) * 8 * 128]
                                     .rearrange("p (k f) -> p k f", k=8),
                                     wo_t[qg, dmch].rearrange("k p f -> p k f"))
                        pl.wait("tensor", s_wo[wbuf], two)
                        bo = dmch % 2
                        pe_take(bo)
                        for ik in range(8):
                            tmm = mm(bo, pb[bo][:, 0:CH],
                                     wob[:, (wbuf * 8 + ik) * 128:
                                         (wbuf * 8 + ik + 1) * 128],
                                     atg[:, (gbuf * 8 + ik) * CH:
                                         (gbuf * 8 + ik + 1) * CH],
                                     ik == 0, ik == 7,
                                     inc=(sMM, 1) if ik == 7 else None)
                        wob_free[wbuf] = tmm
                        atg_free[gbuf] = tmm
                        pl.wait("vector", sMM, tmm)
                        if qg > 0:
                            pl.wait("vector", sEV, acc_tick[dmch])
                        if qg == 0:
                            ta = pl.op("vector", "tensor_copy",
                                       acc[:, dmch * CH:(dmch + 1) * CH],
                                       pb[bo][:, 0:CH], inc=(sEV, 1))
                        else:
                            ta = pl.op("vector", "tensor_add",
                                       acc[:, dmch * CH:(dmch + 1) * CH],
                                       acc[:, dmch * CH:(dmch + 1) * CH],
                                       pb[bo][:, 0:CH], inc=(sEV, 1))
                        bfree[bo] = (sEV, ta)
                        acc_tick[dmch] = ta
            pl.barrier(dsems)

            with ExitStack() as p3:
                sb3 = lambda n, sh, dt_: p3.enter_context(
                    nc.sbuf_tensor(n + "_%d" % own, sh, dt_))
                xnf = sb3("xnf", [128, NKT * CH], F32R)
                w13b = sb3("w13b", [128, 4 * 2 * 8 * 128], F32R)
                w2b = sb3("w2b", [128, 3 * 8 * 128], F32R)
                hgb = sb3("hgb", [128, 8 * CH], F32R)
                silub = sb3("silub", [128, CH], F32)

                s_xnf = mksem("s_xnf%d" % own)
                s_w13 = [mksem("s_w13%d_%d" % (own, i)) for i in range(4)]
                s_w2 = [mksem("s_w2%d_%d" % (own, i)) for i in range(3)]
                txn = pl.dma("sync", s_xnf,
                             xnf[:].rearrange("p (k f) -> p k f", k=NKT),
                             xn_sc[own].rearrange("k p f -> p k f"))
                pl.wait("tensor", s_xnf, txn)
                w13_free = [0, 0, 0, 0]
                w13_tick = {}
                w2_free = [0, 0, 0]

                def emit_w13_load(Q):
                    if Q >= IC * 4:
                        return
                    slot = Q % 4
                    pl.wait("gpsimd", sMM, w13_free[slot])
                    w13_tick[Q] = pl.dma(
                        "gpsimd", s_w13[slot],
                        w13b[:, slot * 2 * 8 * 128:(slot + 1) * 2 * 8 * 128]
                        .rearrange("p (w k f) -> p w k f", w=2, k=8),
                        w13_t[Q // 4, Q % 4].rearrange("w k p f -> p w k f"))

                emit_w13_load(0)
                emit_w13_load(1)
                emit_w13_load(2)
                hg_free = 0
                silu_free = [0]
                ic0 = 0
                for g, gsz in enumerate(FFN_GROUPS):
                    pl.wait("scalar", sMM, hg_free)
                    pl.wait("vector", sMM, hg_free)
                    for ii in range(gsz):
                        ich = ic0 + ii
                        bu, bg = 0 + (ii % 2), 2 + (ii % 2)
                        for q in range(4):
                            Q = ich * 4 + q
                            slot = Q % 4
                            emit_w13_load(Q + 3)
                            pl.wait("tensor", s_w13[slot], w13_tick[Q])
                            if q == 0:
                                pe_take(bu)
                                pe_take(bg)
                            for k8 in range(8):
                                kt = q * 8 + k8
                                base = slot * 2 * 8 * 128
                                mm(bu, pb[bu][:, 0:CH],
                                   w13b[:, base + k8 * 128:
                                        base + (k8 + 1) * 128],
                                   xnf[:, kt * CH:(kt + 1) * CH],
                                   kt == 0, kt == NKT - 1)
                                tmm = mm(bg, pb[bg][:, 0:CH],
                                         w13b[:, base + (8 + k8) * 128:
                                              base + (8 + k8 + 1) * 128],
                                         xnf[:, kt * CH:(kt + 1) * CH],
                                         kt == 0, kt == NKT - 1,
                                         inc=(sMM, 1) if k8 == 7 else None)
                            w13_free[slot] = tmm
                        pl.wait("scalar", sMM, tmm)
                        pl.wait("scalar", sEV, silu_free[0])
                        sACt = pl.op("scalar", "activation", silub[:],
                                     pb[bu][:, 0:CH], AF.Silu,
                                     bias=0.0, scale=c_scl[:, ich:ich + 1],
                                     inc=(sAC, 1))
                        bfree[bu] = (sAC, sACt)
                        pl.wait("vector", sAC, sACt)
                        sEVt = pl.op("vector", "scalar_tensor_tensor",
                                     hgb[:, ii * CH:(ii + 1) * CH],
                                     pb[bg][:, 0:CH],
                                     c_scl[:, IC + ich:IC + ich + 1],
                                     silub[:], ALU.mult, ALU.mult,
                                     inc=(sEV, 1))
                        bfree[bg] = (sEV, sEVt)
                        silu_free[0] = sEVt
                    t_h_ready = sEV.n
                    pl.wait("tensor", sEV, t_h_ready)
                    for dmch in range(NKT):
                        wbuf = dmch % 3
                        pl.wait("gpsimd", sMM, w2_free[wbuf])
                        goff = (g * 32 + dmch) * 8 if g < 10 else (2560 + dmch * 6)
                        t2 = pl.dma("gpsimd", s_w2[wbuf],
                                    w2b[:, wbuf * 8 * 128:
                                        wbuf * 8 * 128 + gsz * 128]
                                    .rearrange("p (k f) -> p k f", k=gsz),
                                    w2_t[goff:goff + gsz].rearrange("k p f -> p k f"))
                        pl.wait("tensor", s_w2[wbuf], t2)
                        bd = 4 + (dmch % 2)
                        pe_take(bd)
                        for ik in range(gsz):
                            tmm = mm(bd, pb[bd][:, 0:CH],
                                     w2b[:, (wbuf * 8 + ik) * 128:
                                         (wbuf * 8 + ik + 1) * 128],
                                     hgb[:, ik * CH:(ik + 1) * CH],
                                     ik == 0, ik == gsz - 1,
                                     inc=(sMM, 1) if ik == gsz - 1 else None)
                        w2_free[wbuf] = tmm
                        hg_free = tmm
                        pl.wait("vector", sMM, tmm)
                        pl.wait("vector", sEV, acc_tick[dmch])
                        ta = pl.op("vector", "scalar_tensor_tensor",
                                   acc[:, dmch * CH:(dmch + 1) * CH],
                                   pb[bd][:, 0:CH],
                                   c_scl[:, 2 * IC + dmch:2 * IC + dmch + 1],
                                   acc[:, dmch * CH:(dmch + 1) * CH],
                                   ALU.mult, ALU.add, inc=(sEV, 1))
                        bfree[bd] = (sEV, ta)
                        acc_tick[dmch] = ta
                    ic0 += gsz
                if debug and own == 0:
                    pl.wait("vector", sEV, acc_tick[0])
                    tev = pl.op("vector", "tensor_copy", silub[:],
                                acc[:, 0:CH], inc=(sEV, 1))
                    pl.wait("sync", sEV, tev)
                    pl.dma("sync", sST, dbg["d_acc0"], silub[:])
                    pl.wait("scalar", sST, sST.n)
                    pl.wait("vector", sST, sST.n)
            pl.barrier(dsems)

            with ExitStack() as p4:
                sb4 = lambda n, sh, dt_: p4.enter_context(
                    nc.sbuf_tensor(n + "_%d" % own, sh, dt_))
                xres = sb4("xres", [128, 2 * CH], F32)
                stf = sb4("stf", [128, 2 * CH], F32)
                sto = sb4("sto", [128, 2 * CH], F32)
                s_xres = [mksem("s_xr%d_0" % own), mksem("s_xr%d_1" % own)]
                xres_free = [0, 0]
                stf_free = [0, 0]
                sto_free = [0, 0]
                for dmch in range(NKT):
                    dbuf = dmch % 2
                    pl.wait("sync", sEV, xres_free[dbuf])
                    tx = pl.dma("sync", s_xres[dbuf],
                                xres[:, dbuf * CH:(dbuf + 1) * CH],
                                xTp[own, dmch])
                    pl.wait("vector", s_xres[dbuf], tx)
                    pl.wait("vector", sMM, stf_free[dbuf])
                    pl.wait("vector", sEV, acc_tick[dmch])
                    tev = pl.op("vector", "tensor_add",
                                stf[:, dbuf * CH:(dbuf + 1) * CH],
                                acc[:, dmch * CH:(dmch + 1) * CH],
                                xres[:, dbuf * CH:(dbuf + 1) * CH],
                                inc=(sEV, 1))
                    xres_free[dbuf] = tev
                    bt = 6 + dbuf
                    pe_take(bt)
                    pl.wait("tensor", sEV, tev)
                    for tt in range(4):
                        ttr = transp(pb[bt][:, tt * 128:(tt + 1) * 128],
                                     stf[:, dbuf * CH + tt * 128:
                                         dbuf * CH + (tt + 1) * 128],
                                     inc=(sMM, 1) if tt == 3 else None)
                    stf_free[dbuf] = ttr
                    pl.wait("scalar", sMM, ttr)
                    pl.wait("scalar", sSTo[dbuf], sto_free[dbuf])
                    sACt = pl.op("scalar", "activation",
                                 sto[:, dbuf * CH:(dbuf + 1) * CH],
                                 pb[bt][:, 0:CH], AF.Identity,
                                 bias=0.0, scale=1.0, inc=(sAC, 1))
                    bfree[bt] = (sAC, sACt)
                    pl.wait("sync", sAC, sACt)
                    tst = pl.dma("sync", sSTo[dbuf],
                                 out_h[own, :, dmch * 128:(dmch + 1) * 128]
                                 .rearrange("(t p) f -> p t f", p=128),
                                 sto[:, dbuf * CH:(dbuf + 1) * CH]
                                 .rearrange("p (t f) -> p t f", t=4))
                    sto_free[dbuf] = tst
                pl.wait("sync", sSTo[0], sSTo[0].n)
                pl.wait("sync", sSTo[1], sSTo[1].n)
                pl.wait("sync", sST, sST.n)
            accs.close()
            pl.barrier(dsems)

        pl.wait("sync", sSTo[0], sSTo[0].n)
        pl.wait("sync", sSTo[1], sSTo[1].n)
        pl.wait("sync", sST, sST.n)

        with nc.allow_low_precision(
                "fp32r outputs are the fp32-emulating hi/lo bf16 rounding; "
                "all matmul accumulation stays fp32 in PSUM"), \
             nc.Block() as block:
            @block.sync
            def _(sync):
                for fn in pl.q["sync"]:
                    fn(sync)

            @block.gpsimd
            def _(gpsimd):
                for fn in pl.q["gpsimd"]:
                    fn(gpsimd)

            @block.vector
            def _(vector):
                for fn in pl.q["vector"]:
                    fn(vector)

            @block.scalar
            def _(scalar):
                for fn in pl.q["scalar"]:
                    fn(scalar)

            @block.tensor
            def _(tensor):
                for fn in pl.q["tensor"]:
                    fn(tensor)

    return nc


def make_perm(j):
    A, Bc = j, 7 - j
    slots = [None] * 8
    slots[3], slots[7] = A, Bc
    for c in range(A):
        slots[c] = c
    rest = [c for c in range(8) if c not in (A, Bc) and c >= A]
    free = [s for s in (0, 1, 2, 4, 5, 6) if slots[s] is None]
    for s, c in zip(free, rest):
        slots[s] = c
    return slots, A, Bc


def rope_tables(positions):
    d2 = np.arange(0, HD, 2, dtype=np.float64) / HD
    inv = 1.0 / (10000.0 ** d2)
    fr = np.outer(positions.astype(np.float64), inv)
    emb = np.concatenate([fr, fr], axis=1)
    return (np.cos(emb).T.astype(np.float32),
            np.sin(emb).T.astype(np.float32))


def prep_core_inputs(core, x, Wt, consts):
    b, j = core // 4, core % 4
    A, Bc = j, 7 - j
    xT = np.ascontiguousarray(x[b].T)
    xTp = np.empty((2, NKT, 128, CH), dtype=np.float32)
    cosk = np.empty((2, 128, CH), dtype=np.float32)
    sink = np.empty((2, 128, CH), dtype=np.float32)
    for s, c in enumerate((A, Bc)):
        blkT = xT[:, c * CH:(c + 1) * CH]
        xTp[s] = blkT.reshape(NKT, 128, CH)
        pos = np.arange(c * CH, (c + 1) * CH)
        cosk[s], sink[s] = rope_tables(pos)
    def chunk_at(p):
        return p // 2 if p % 2 == 0 else 7 - (p // 2)
    base = np.full((128, 13), 44.0, dtype=np.float32)
    for kc in range(3):
        if not (kc < A):
            base[:, kc] = NEG
    for p in range(8):
        if not (chunk_at(p) < Bc):
            base[:, 4 + p] = NEG
    d = dict(Wt)
    d.update(consts)
    d["xTp"] = xTp
    d["cosk"] = cosk
    d["sink"] = sink
    d["base"] = base
    return d, (b, A, Bc)


def prep_shared(L0_attn_wq, L0_attn_wk, L0_attn_wv, L0_attn_wo,
                L0_ffn_w1_q, L0_ffn_w1_s, L0_ffn_w3_q, L0_ffn_w3_s,
                L0_ffn_w2_q, L0_ffn_w2_s):
    f32 = np.float32
    wqT = np.ascontiguousarray(np.asarray(L0_attn_wq, f32).T)
    wkT = np.ascontiguousarray(np.asarray(L0_attn_wk, f32).T)
    wvT = np.ascontiguousarray(np.asarray(L0_attn_wv, f32).T)
    woT = np.ascontiguousarray(np.asarray(L0_attn_wo, f32).T)
    w1T = np.ascontiguousarray(np.asarray(L0_ffn_w1_q, f32).T)
    w3T = np.ascontiguousarray(np.asarray(L0_ffn_w3_q, f32).T)
    w2T = np.ascontiguousarray(np.asarray(L0_ffn_w2_q, f32).T)

    wq_t = np.empty((KV, NKT, 128, 512), f32)
    for kvh in range(KV):
        for kt in range(NKT):
            wq_t[kvh, kt] = wqT[kt * 128:(kt + 1) * 128,
                                kvh * 512:(kvh + 1) * 512]
    wk_t = np.empty((KV, NKT, 128, 128), f32)
    for kvh in range(KV):
        for kt in range(NKT):
            wk_t[kvh, kt] = wkT[kt * 128:(kt + 1) * 128,
                                kvh * 128:(kvh + 1) * 128]
    wv_t = np.empty((2, NKT, 128, 512), f32)
    for vch in range(2):
        for kt in range(NKT):
            wv_t[vch, kt] = wvT[kt * 128:(kt + 1) * 128,
                                vch * 512:(vch + 1) * 512]
    wo_t = np.empty((4, 32, 8, 128, 128), f32)
    for qg in range(4):
        for dmch in range(32):
            for ik in range(8):
                qk = qg * 8 + ik
                wo_t[qg, dmch, ik] = woT[qk * 128:(qk + 1) * 128,
                                         dmch * 128:(dmch + 1) * 128]
    w13_t = np.empty((IC, 4, 2, 8, 128, 128), f32)
    for ich in range(IC):
        for kt in range(NKT):
            w13_t[ich, kt // 8, 0, kt % 8] = w1T[kt * 128:(kt + 1) * 128,
                                                 ich * 128:(ich + 1) * 128]
            w13_t[ich, kt // 8, 1, kt % 8] = w3T[kt * 128:(kt + 1) * 128,
                                                 ich * 128:(ich + 1) * 128]
    w2_t = np.empty((2752, 128, 128), f32)
    idx = 0
    ic0 = 0
    for g, gsz in enumerate(FFN_GROUPS):
        for dmch in range(32):
            for ik in range(gsz):
                ich = ic0 + ik
                w2_t[idx] = w2T[ich * 128:(ich + 1) * 128,
                                dmch * 128:(dmch + 1) * 128]
                idx += 1
        ic0 += gsz
    assert idx == 2752

    scl = np.zeros((128, 2 * IC + 32), f32)
    scl[:, 0:IC] = np.asarray(L0_ffn_w1_s, f32).reshape(IC, 128).T
    scl[:, IC:2 * IC] = np.asarray(L0_ffn_w3_s, f32).reshape(IC, 128).T
    scl[:, 2 * IC:] = np.asarray(L0_ffn_w2_s, f32).reshape(32, 128).T

    csts = np.zeros((128, 24), f32)
    csts[:, 2] = EPS
    csts[:, 8:22] = 1.0

    idn = np.eye(128, dtype=f32)
    prot = np.zeros((128, 128), f32)
    for e in range(64):
        prot[e + 64, e] = -1.0
    for e in range(64):
        prot[e, e + 64] = 1.0
    tri = np.zeros((4, 128, CH), f32)
    for kt in range(4):
        kj = kt * 128 + np.arange(128)[:, None]
        qi = np.arange(CH)[None, :]
        tri[kt] = np.where(kj <= qi, 0.0, NEG)
    onec = np.ones((128, 1), f32)
    oner = np.ones((1, 128), f32)

    Wt = {"wq_t": wq_t, "wk_t": wk_t, "wv_t": wv_t, "wo_t": wo_t,
          "w13_t": w13_t, "w2_t": w2_t, "scl": scl}
    consts = {"csts": csts, "idn": idn, "prot": prot, "tri": tri,
              "onec": onec, "oner": oner}
    return Wt, consts


def kernel(L0_attn_hidden_states, L0_attn_wq, L0_attn_wk, L0_attn_wv,
           L0_attn_wo, L0_ffn_w1_q, L0_ffn_w1_s, L0_ffn_w3_q, L0_ffn_w3_s,
           L0_ffn_w2_q, L0_ffn_w2_s):
    x = np.asarray(L0_attn_hidden_states, np.float32)
    Wt, consts = prep_shared(L0_attn_wq, L0_attn_wk, L0_attn_wv, L0_attn_wo,
                             L0_ffn_w1_q, L0_ffn_w1_s, L0_ffn_w3_q,
                             L0_ffn_w3_s, L0_ffn_w2_q, L0_ffn_w2_s)
    nc = build(debug=False)
    in_maps = []
    meta = []
    for core in range(8):
        d, m = prep_core_inputs(core, x, Wt, consts)
        in_maps.append(d)
        meta.append(m)
    res = run_bass_kernel_spmd(nc, in_maps, core_ids=list(range(8)))
    out = np.empty((B, S, D), np.float32)
    for core in range(8):
        b, A, Bc = meta[core]
        o = res.results[core]["out"]
        out[b, A * CH:(A + 1) * CH] = o[0]
        out[b, Bc * CH:(Bc + 1) * CH] = o[1]
    return out


# version 13 (final)
# speedup vs baseline: 1.0730x; 1.0446x over previous
import math
import numpy as np
from contextlib import ExitStack

import concourse.bass as bass
import concourse.mybir as mybir
from concourse.bass_utils import run_bass_kernel_spmd

F32 = mybir.dt.float32
F32R = mybir.dt.float32r
AF = mybir.ActivationFunctionType
ALU = mybir.AluOpType
AX = mybir.AxisListType

B, S, D = 2, 4096, 4096
H, KV, HD = 32, 8, 128
G = H // KV
I = 11008
IC = I // 128
NKT = D // 128
CH = 512
NSLOT = 8
NSL = (4, 9)
SCALE = 1.0 / math.sqrt(HD)
EPS = 1e-5
NEG = -1.0e30
FFN_GROUPS = [8] * 10 + [6]
ENGS = ("sync", "gpsimd", "vector", "scalar", "tensor")


class Sem:
    def __init__(self, h):
        self.h = h
        self.n = 0

    def bump(self, d=1):
        self.n += d
        return self.n


class Plan:
    def __init__(self):
        self.q = {e: [] for e in ENGS}
        self.autosem = {}

    def emit(self, eng, fn):
        self.q[eng].append(fn)

    def dma(self, eng, sem, out_ap, in_ap):
        t = sem.bump(16)
        self.emit(eng, lambda e, o=out_ap, i=in_ap, s=sem.h:
                  e.dma_start(out=o, in_=i).then_inc(s, 16))
        return t

    def wait(self, eng, sem, val):
        if val > 0:
            self.emit(eng, lambda e, s=sem.h, v=val: e.wait_ge(s, v))

    def op(self, eng, name, *a, inc=None, **kw):
        auto = self.autosem.get(eng)
        if auto is not None:
            if auto.n > 0:
                self.wait(eng, auto, auto.n)
            if inc is None:
                inc = (auto, 1)
            else:
                assert inc[0] is auto, "non-auto inc on %s" % eng
        t = None
        if inc is not None:
            t = inc[0].bump(inc[1])
            ih = inc[0].h
        else:
            ih = None

        def f(e, nm=name, aa=a, kk=kw, hh=ih, vv=(inc[1] if inc else 0)):
            ins = getattr(e, nm)(*aa, **kk)
            if hh is not None:
                ins.then_inc(hh, vv)
        self.emit(eng, f)
        return t

    def barrier(self, dsems):
        for e in ENGS:
            s = dsems[e]
            s.bump(1)
            self.emit(e, lambda en, h=s.h: en.sem_inc(h, 1))
        for e in ENGS:
            for e2 in ENGS:
                if e2 != e:
                    self.wait(e, dsems[e2], dsems[e2].n)


def build(debug=False):
    nc = bass.Bass("TRN2", target_bir_lowering=False)
    dr = lambda n, sh, dt_=F32, k="ExternalInput": nc.dram_tensor(n, sh, dt_, kind=k).ap()

    xTp = dr("xTp", [2, NKT, 128, CH])
    wq_t = dr("wq_t", [KV, NKT, 128, 512])
    wk_t = dr("wk_t", [KV, NKT, 128, 128])
    wv_t = dr("wv_t", [2, NKT, 128, 512])
    wo_t = dr("wo_t", [4, 32, 8, 128, 128])
    w13_t = dr("w13_t", [IC, 4, 2, 8, 128, 128])
    w2_t = dr("w2_t", [2752, 128, 128])
    scl_h = dr("scl", [128, 2 * IC + 32])
    cst_h = dr("csts", [128, 24])
    idn_h = dr("idn", [128, 128])
    prot_h = dr("prot", [128, 128])
    tri_h = dr("tri", [4, 128, CH])
    base_h = dr("base", [128, 13])
    cosk_h = dr("cosk", [2, 128, CH])
    sink_h = dr("sink", [2, 128, CH])
    onec_h = dr("onec", [128, 1])
    oner_h = dr("oner", [1, 128])
    out_h = dr("out", [2, CH, D], F32, "ExternalOutput")

    dbg = {}
    if debug:
        for nm, sh, dt_ in [("d_xn3", [NKT, 128, CH], F32R),
                            ("d_k00", [128, CH], F32R),
                            ("d_v00", [128, 512], F32R),
                            ("d_q0", [128, CH], F32R),
                            ("d_attn0", [128, CH], F32R),
                            ("d_acc0", [128, CH], F32)]:
            dbg[nm] = dr(nm, sh, dt_, "ExternalOutput")

    k_loc = dr("k_loc", [2, KV, 128, CH], F32R, "Internal")
    v_loc = dr("v_loc", [2, 4, 128, KV * 128], F32R, "Internal")
    k_gat = dr("k_gat", [8, KV, 128, CH], F32R, "Internal")
    v_gat = dr("v_gat", [8, 4, 128, KV * 128], F32R, "Internal")
    xn_sc = dr("xn_sc", [2, NKT, 128, CH], F32R, "Internal")
    at_sc = dr("at_sc", [H, 128, CH], F32R, "Internal")

    pl = Plan()

    with ExitStack() as st:
        def mksem(name):
            return Sem(st.enter_context(nc.semaphore(name)))

        sLD = mksem("sLD")
        sLW = mksem("sLW")
        sSP = mksem("sSP")
        sMM = mksem("sMM")
        sEV = mksem("sEV")
        sAC = mksem("sAC")
        sST = mksem("sST")
        dsems = {e: mksem("sdn_" + e) for e in ENGS}
        pl.autosem = {"vector": sEV, "scalar": sAC}
        s_cs = mksem("s_cs")
        sSPx = mksem("sSPx"); sSPk = mksem("sSPk"); sSPv = mksem("sSPv")
        sSPa = mksem("sSPa")
        sSTo = [mksem("sSTo0"), mksem("sSTo1")]
        sCC = mksem("sCC")

        c_idn = st.enter_context(nc.sbuf_tensor("c_idn", [128, 128], F32))
        c_idnr = st.enter_context(nc.sbuf_tensor("c_idnr", [128, 128], F32R))
        c_prot = st.enter_context(nc.sbuf_tensor("c_prot", [128, 128], F32R))
        c_tri = st.enter_context(nc.sbuf_tensor("c_tri", [128, 4 * CH], F32R))
        c_scl = st.enter_context(nc.sbuf_tensor("c_scl", [128, 2 * IC + 32], F32))
        c_cst = st.enter_context(nc.sbuf_tensor("c_cst", [128, 24], F32))
        c_onec = st.enter_context(nc.sbuf_tensor("c_onec", [128, 1], F32R))
        c_oner = st.enter_context(nc.sbuf_tensor("c_oner", [1, 128], F32R))
        c_cos = st.enter_context(nc.sbuf_tensor("c_cos", [128, CH], F32))
        c_sin = st.enter_context(nc.sbuf_tensor("c_sin", [128, CH], F32))

        pl.dma("sync", sLD, c_idn[:], idn_h)
        pl.dma("gpsimd", sLW, c_tri[:].rearrange("p (q f) -> p q f", q=4),
               tri_h.rearrange("q p f -> p q f"))
        pl.dma("gpsimd", sLW, c_idnr[:], idn_h)
        pl.dma("sync", sLD, c_scl[:], scl_h)
        pl.dma("sync", sLD, c_cst[:], cst_h)
        pl.dma("gpsimd", sLW, c_prot[:], prot_h)
        pl.dma("gpsimd", sLW, c_onec[:], onec_h)
        pl.dma("gpsimd", sLW, c_oner[:], oner_h)
        for e in ("vector", "scalar", "tensor"):
            pl.wait(e, sLD, sLD.n)
            pl.wait(e, sLW, sLW.n)

        pb = [st.enter_context(nc.psum_tensor("pb%d" % i, [128, 512], F32))
              for i in range(8)]
        bfree = [None] * 8

        def pe_take(bi):
            if bfree[bi] is not None:
                pl.wait("tensor", bfree[bi][0], bfree[bi][1])

        def mm(bi, out_ap, lhsT, rhs, start, stop, inc=None):
            def f(e, o=out_ap, l=lhsT, r=rhs, s=start, p=stop,
                  hh=(inc[0].h if inc else None), vv=(inc[1] if inc else 0)):
                ins = e.matmul(o, l, r, start=s, stop=p)
                if hh is not None:
                    ins.then_inc(hh, vv)
            if inc is not None:
                t = inc[0].bump(inc[1])
            else:
                t = None
            pl.emit("tensor", f)
            return t

        def transp(out_ap, in_ap, inc=None):
            def f(e, o=out_ap, i=in_ap,
                  hh=(inc[0].h if inc else None), vv=(inc[1] if inc else 0)):
                ins = e.transpose(o, i, c_idn[:])
                if hh is not None:
                    ins.then_inc(hh, vv)
            if inc is not None:
                t = inc[0].bump(inc[1])
            else:
                t = None
            pl.emit("tensor", f)
            return t

        def load_cos_sin(slot):
            pl.dma("sync", s_cs, c_cos[:], cosk_h[slot])
            t2 = pl.dma("sync", s_cs, c_sin[:], sink_h[slot])
            return t2

        with ExitStack() as p1:
            sb1 = lambda n, sh, dt_: p1.enter_context(nc.sbuf_tensor(n, sh, dt_))
            A_buf = sb1("A_buf", [128, NKT * CH], F32R)
            sq_b = sb1("sq_b", [128, CH], F32R)
            wkb = sb1("wkb", [128, 2 * NKT * 128], F32R)
            wvb = sb1("wvb", [128, 2 * 4 * 512], F32R)
            krawb = sb1("krawb", [128, CH], F32R)
            ktmp = sb1("ktmp", [128, 2 * CH], F32)
            K_out = sb1("K_out", [128, CH], F32R)
            V_out = sb1("V_out", [128, 512], F32R)
            mrow = sb1("mrow", [1, CH], F32R)
            rrow = sb1("rrow", [1, CH], F32R)
            srow = sb1("srow", [1, CH], F32)
            sp1 = sb1("sp1", [1, 8], F32)

            s_xb = [mksem("s_xb%d" % i) for i in range(4)]
            s_wk = [mksem("s_wk0"), mksem("s_wk1")]
            s_wv = [mksem("s_wv0"), mksem("s_wv1")]
            wk_free = [0, 0]
            wv_free = [0, 0]
            abuf_free_mm = 0
            abuf_spill = 0
            kout_spill = 0
            vout_spill = 0

            for s in range(2):
                pl.wait("sync", sEV, sEV.n)
                tcs = load_cos_sin(s)
                pl.wait("gpsimd", sMM, abuf_free_mm)
                pl.wait("gpsimd", sSPx, abuf_spill)
                tx_b = [0] * 4
                for blk in range(4):
                    kt0 = blk * 8
                    tx_b[blk] = pl.dma(
                        "gpsimd", s_xb[blk],
                        A_buf[:, kt0 * CH:(kt0 + 8) * CH]
                        .rearrange("p (k f) -> p k f", k=8),
                        xTp[s, kt0:kt0 + 8].rearrange("k p f -> p k f"))

                pe_take(0)
                pe_take(1)
                last_sq = 0
                for kt in range(NKT):
                    if kt % 8 == 0:
                        pl.wait("tensor", s_xb[kt // 8], tx_b[kt // 8])
                        pl.wait("vector", s_xb[kt // 8], tx_b[kt // 8])
                    xt = A_buf[:, kt * CH:(kt + 1) * CH]
                    mm(0, pb[0][0:1, :], c_onec[:], xt, kt == 0, kt == NKT - 1)
                    pl.wait("vector", sMM, last_sq)
                    tev = pl.op("vector", "tensor_mul", sq_b[:],
                                xt.bitcast(F32), xt.bitcast(F32), inc=(sEV, 1))
                    pl.wait("tensor", sEV, tev)
                    last_sq = mm(1, pb[1][0:1, :], c_onec[:], sq_b[:],
                                 kt == 0, kt == NKT - 1, inc=(sMM, 1))

                pl.wait("vector", sMM, last_sq)
                t0 = pl.op("vector", "tensor_scalar", mrow[:],
                           pb[0][0:1, :], 1.0 / D, None, ALU.mult,
                           inc=(sEV, 1))
                bfree[0] = (sEV, t0)
                t1 = pl.op("vector", "tensor_scalar", srow[:], pb[1][0:1, :],
                           1.0 / D, None, ALU.mult, inc=(sEV, 1))
                bfree[1] = (sEV, t1)
                pl.wait("vector", sEV, t1)
                pl.op("vector", "tensor_mul", ktmp[0:1, 0:CH],
                      mrow[:].bitcast(F32), mrow[:].bitcast(F32))
                tev = pl.op("vector", "tensor_tensor", srow[:], srow[:],
                            ktmp[0:1, 0:CH], ALU.subtract, inc=(sEV, 1))
                pl.wait("scalar", sEV, tev)
                tac = pl.op("scalar", "activation", srow[:], srow[:], AF.Sqrt,
                            bias=c_cst[0:1, 2:3], scale=1.0, inc=(sAC, 1))
                pl.wait("vector", sAC, tac)
                pl.op("vector", "tensor_copy", sp1[:, 0:1], c_cst[0:1, 0:1])
                tev = pl.op("vector", "reciprocal", rrow[:],
                            srow[:], inc=(sEV, 1))

                pe_take(2)
                pe_take(3)
                pl.wait("tensor", sEV, tev)
                mm(2, pb[2][:, 0:CH], c_oner[:], mrow[:], True, True)
                tmm = mm(3, pb[3][:, 0:CH], c_oner[:], rrow[:], True, True,
                         inc=(sMM, 1))

                pl.wait("vector", sMM, tmm)
                tev = 0
                for kt in range(NKT):
                    xt = A_buf[:, kt * CH:(kt + 1) * CH]
                    pl.op("vector", "tensor_tensor", ktmp[:, 0:CH],
                          xt.bitcast(F32), pb[2][:, 0:CH], ALU.subtract)
                    inc = (sEV, 1) if kt == NKT - 1 else None
                    tev2 = pl.op("vector", "tensor_tensor", xt, ktmp[:, 0:CH],
                                 pb[3][:, 0:CH], ALU.mult, inc=inc)
                    if tev2 is not None:
                        tev = tev2
                bfree[2] = (sEV, tev)
                bfree[3] = (sEV, tev)

                if True:
                    own = s
                    pl.wait("sync", sEV, tev)
                    abuf_spill = pl.dma("sync", sSPx,
                                        xn_sc[own].rearrange("k p f -> p k f"),
                                        A_buf[:].rearrange("p (k f) -> p k f",
                                                           k=NKT))
                    if debug and s == 0:
                        pl.dma("sync", sST,
                               dbg["d_xn3"].rearrange("k p f -> p k f"),
                               A_buf[:]
                               .rearrange("p (k f) -> p k f", k=NKT))
                        pl.wait("gpsimd", sST, sST.n)
                        pl.wait("tensor", sST, sST.n)
                        pl.wait("vector", sST, sST.n)

                pl.wait("tensor", sEV, tev)
                pl.wait("vector", s_cs, tcs)

                for kvh in range(KV):
                    buf = kvh % 2
                    pl.wait("gpsimd", sMM, wk_free[buf])
                    twk = pl.dma("gpsimd", s_wk[buf],
                                 wkb[:, buf * NKT * 128:(buf + 1) * NKT * 128]
                                 .rearrange("p (k f) -> p k f", k=NKT),
                                 wk_t[kvh].rearrange("k p f -> p k f"))
                    pl.wait("tensor", s_wk[buf], twk)
                    bk = 4 + (kvh % 2)
                    pe_take(bk)
                    for kt in range(NKT):
                        tmm = mm(bk, pb[bk][:, 0:CH],
                                 wkb[:, (buf * NKT + kt) * 128:
                                     (buf * NKT + kt + 1) * 128],
                                 A_buf[:, kt * CH:(kt + 1) * CH],
                                 kt == 0, kt == NKT - 1,
                                 inc=(sMM, 1) if kt == NKT - 1 else None)
                    wk_free[buf] = tmm
                    pl.wait("vector", sMM, tmm)
                    sEVt = pl.op("vector", "tensor_copy", krawb[:],
                                 pb[bk][:, 0:CH], inc=(sEV, 1))
                    bfree[bk] = (sEV, sEVt)
                    pe_take(6)
                    pl.wait("tensor", sEV, sEVt)
                    trot = mm(6, pb[6][:, 0:CH], c_prot[:], krawb[:], True, True,
                              inc=(sMM, 1))
                    pl.wait("vector", sSPk, kout_spill)
                    pl.wait("vector", sEV, sEVt)
                    pl.op("vector", "tensor_tensor", ktmp[:, 0:CH],
                          krawb[:].bitcast(F32), c_cos[:], ALU.mult)
                    pl.wait("vector", sMM, trot)
                    t6 = pl.op("vector", "tensor_tensor", ktmp[:, CH:2 * CH],
                               pb[6][:, 0:CH], c_sin[:], ALU.mult, inc=(sEV, 1))
                    bfree[6] = (sEV, t6)
                    tev = pl.op("vector", "tensor_add", K_out[:],
                                ktmp[:, 0:CH], ktmp[:, CH:2 * CH], inc=(sEV, 1))
                    pl.wait("sync", sEV, tev)
                    kout_spill = pl.dma("sync", sSPk, k_loc[s, kvh], K_out[:])
                    if debug and s == 0 and kvh == 0:
                        pl.dma("sync", sST, dbg["d_k00"], K_out[:])
                        pl.wait("vector", sST, sST.n)

                last_v = 0
                for vch in range(2):
                    for tt in range(4):
                        pe_take(tt)
                    for blk in range(8):
                        buf = blk % 2
                        pl.wait("gpsimd", sMM, wv_free[buf])
                        twv = pl.dma("gpsimd", s_wv[buf],
                                     wvb[:, buf * 4 * 512:(buf + 1) * 4 * 512]
                                     .rearrange("p (k f) -> p k f", k=4),
                                     wv_t[vch, blk * 4:(blk + 1) * 4].rearrange("k p f -> p k f"))
                        pl.wait("tensor", s_wv[buf], twv)
                        for k4 in range(4):
                            kt = blk * 4 + k4
                            for tt in range(4):
                                inc = (sMM, 1) if (k4 == 3 and tt == 3) else None
                                tmm = mm(tt, pb[tt][:, 0:512],
                                         A_buf[:, kt * CH + tt * 128:
                                               kt * CH + (tt + 1) * 128],
                                         wvb[:, (buf * 4 + k4) * 512:
                                             (buf * 4 + k4 + 1) * 512],
                                         kt == 0, kt == NKT - 1, inc=inc)
                                if tmm is not None:
                                    wv_free[buf] = tmm
                                    last_v = tmm
                    pl.wait("vector", sMM, last_v)
                    for tt in range(4):
                        pl.wait("vector", sSPv, vout_spill)
                        sEVt = pl.op("vector", "tensor_copy", V_out[:],
                                     pb[tt][:, 0:512], inc=(sEV, 1))
                        bfree[tt] = (sEV, sEVt)
                        pl.wait("sync", sEV, sEVt)
                        vout_spill = pl.dma("sync", sSPv,
                                            v_loc[s, tt, :, vch * 512:
                                                 (vch + 1) * 512], V_out[:])
                        if debug and s == 0 and tt == 0 and vch == 0:
                            pl.dma("sync", sST, dbg["d_v00"], V_out[:])
                            pl.wait("vector", sST, sST.n)
                abuf_free_mm = last_v

            t_spills_p1 = (sSPx.n, sSPk.n, sSPv.n)
            pl.wait("sync", sSPx, sSPx.n)
            pl.wait("sync", sSPk, sSPk.n)
            pl.wait("sync", sSPv, sSPv.n)
            pl.wait("gpsimd", sSPk, sSPk.n)
            pl.wait("gpsimd", sSPv, sSPv.n)
            pl.emit("gpsimd", lambda g: g.collective_compute(
                "AllGather", ALU.bypass,
                replica_groups=[[0, 1, 2, 3], [4, 5, 6, 7]],
                ins=[k_loc[:]], outs=[k_gat[:]]).then_inc(sCC.h, 1))
            sCC.bump(1)
            pl.emit("gpsimd", lambda g: g.collective_compute(
                "AllGather", ALU.bypass,
                replica_groups=[[0, 1, 2, 3], [4, 5, 6, 7]],
                ins=[v_loc[:]], outs=[v_gat[:]]).then_inc(sCC.h, 1))
            sCC.bump(1)
            t_cc = sCC.n
        pl.barrier(dsems)

        for own in range(2):
            nsl = NSL[own]
            ncan = nsl - 1
            base_off12 = 0 if own == 0 else 4
            tcs = load_cos_sin(own)

            with ExitStack() as p2:
                sb2 = lambda n, sh, dt_: p2.enter_context(
                    nc.sbuf_tensor(n + "_%d" % own, sh, dt_))
                q_all = sb2("q_all", [128, H * CH], F32R)
                qraw = sb2("qraw", [128, CH], F32R)
                K_sb = sb2("K_sb", [128, 2 * nsl * CH], F32R)
                V_sb = sb2("V_sb", [128, 2 * nsl * 4 * 128], F32R)
                baseb = sb2("baseb", [128, 13], F32)
                kk_r = sb2("kk_r", [128, CH], F32R)
                qq_r = sb2("qq_r", [128, CH], F32R)
                kcmax = sb2("kcmax", [1, 16], F32)
                km2 = sb2("km2", [1, 16], F32)
                qn2_row = sb2("qn2_row", [1, 4 * CH], F32)
                nm_row = sb2("nm_row", [1, 2 * 4 * CH], F32R)
                rc_row = sb2("rc_row", [1, CH], F32R)
                rcrep = sb2("rcrep", [128, CH], F32)
                expT = sb2("expT", [128, 2 * CH], F32R)
                pvout = sb2("pvout", [128, CH], F32R)
                p2a_st = ExitStack()
                xnb = p2a_st.enter_context(nc.sbuf_tensor(
                    "xnb_%d" % own, [128, 2 * 2 * CH], F32R))
                wqb = p2a_st.enter_context(nc.sbuf_tensor(
                    "wqb_%d" % own, [128, 2 * 2 * 512], F32R))

                s_base = mksem("s_base%d" % own)
                s_kv = [mksem("s_kv%d_0" % own), mksem("s_kv%d_1" % own)]
                s_wq = [mksem("s_wq%d_0" % own), mksem("s_wq%d_1" % own)]
                s_xn2 = [mksem("s_xn%d_0" % own), mksem("s_xn%d_1" % own)]
                tb = pl.dma("sync", s_base, baseb[:], base_h)
                pl.wait("scalar", s_base, tb)
                pl.wait("sync", sSPx, t_spills_p1[0])
                pl.wait("sync", sSPk, t_spills_p1[1])
                pl.wait("sync", sSPv, t_spills_p1[2])

                wq_free = [0, 0]
                xn_free = [0, 0]
                kv_free = [0, 0]
                kv_tick = [0, 0]
                expT_free = [(0, 0), (0, 0)]
                sumr_free = [0]
                pvout_sp = 0

                for kvh in range(KV):
                    for ht in range(4):
                        pe_take(ht)
                    for blk in range(16):
                        buf = blk % 2
                        pl.wait("gpsimd", sMM, wq_free[buf])
                        twq = pl.dma("gpsimd", s_wq[buf],
                                     wqb[:, buf * 2 * 512:(buf + 1) * 2 * 512]
                                     .rearrange("p (k f) -> p k f", k=2),
                                     wq_t[kvh, blk * 2:(blk + 1) * 2]
                                     .rearrange("k p f -> p k f"))
                        pl.wait("sync", sMM, xn_free[buf])
                        txn = pl.dma("sync", s_xn2[buf],
                                     xnb[:, buf * 2 * CH:(buf + 1) * 2 * CH]
                                     .rearrange("p (k f) -> p k f", k=2),
                                     xn_sc[own, blk * 2:(blk + 1) * 2]
                                     .rearrange("k p f -> p k f"))
                        pl.wait("tensor", s_wq[buf], twq)
                        pl.wait("tensor", s_xn2[buf], txn)
                        for k4 in range(2):
                            kt = blk * 2 + k4
                            for ht in range(4):
                                inc = (sMM, 1) if (k4 == 1 and ht == 3) else None
                                tmm = mm(ht, pb[ht][:, 0:CH],
                                         wqb[:, (buf * 2 + k4) * 512 + ht * 128:
                                             (buf * 2 + k4) * 512 + (ht + 1) * 128],
                                         xnb[:, (buf * 2 + k4) * CH:
                                             (buf * 2 + k4 + 1) * CH],
                                         kt == 0, kt == NKT - 1, inc=inc)
                        wq_free[buf] = tmm
                        xn_free[buf] = tmm
                    tqp = sMM.n
                    pl.wait("vector", sMM, tqp)
                    if kvh == 0:
                        pl.wait("vector", s_cs, tcs)
                    for ht in range(4):
                        head = kvh * 4 + ht
                        sEVt = pl.op("vector", "tensor_copy", qraw[:],
                                     pb[ht][:, 0:CH], inc=(sEV, 1))
                        bfree[ht] = (sEV, sEVt)
                        pe_take(4)
                        pl.wait("tensor", sEV, sEVt)
                        trot = mm(4, pb[4][:, 0:CH], c_prot[:], qraw[:],
                                  True, True, inc=(sMM, 1))
                        pl.wait("vector", sEV, sEVt)
                        pl.op("vector", "tensor_tensor",
                              kk_r[:],
                              qraw[:].bitcast(F32), c_cos[:], ALU.mult)
                        pl.wait("vector", sMM, trot)
                        t4 = pl.op("vector", "tensor_tensor",
                                   qq_r[:],
                                   pb[4][:, 0:CH], c_sin[:], ALU.mult,
                                   inc=(sEV, 1))
                        bfree[4] = (sEV, t4)
                        pl.op("vector", "tensor_add",
                              q_all[:, head * CH:(head + 1) * CH],
                              kk_r[:].bitcast(F32), qq_r[:].bitcast(F32),
                              inc=(sEV, 1))
                tq_ready = sEV.n
                pl.wait("tensor", sEV, tq_ready)
                if debug and own == 0:
                    pl.wait("sync", sEV, tq_ready)
                    pl.dma("sync", sST, dbg["d_q0"], q_all[:, 0:CH])
                    pl.wait("vector", sST, sST.n)
                p2a_st.close()
                sume = sb2("sume", [128, CH], F32)
                sume_r = sb2("sume_r", [128, CH], F32R)

                csl = slice(0, 6, 2) if own == 0 else slice(0, 8)
                canon = (range(0, 6, 2) if own == 0 else range(8))

                def kv_load(kvh):
                    sl = kvh % 2
                    ko = sl * nsl * CH
                    vo = sl * nsl * 4 * 128
                    pl.wait("scalar", sMM, kv_free[sl])
                    if kvh == 0:
                        pl.wait("scalar", sCC, t_cc)
                    pl.dma("scalar", s_kv[sl],
                           K_sb[:, ko:ko + ncan * CH]
                           .rearrange("p (s f) -> p s f", s=ncan),
                           k_gat[csl, kvh].rearrange("s p f -> p s f"))
                    pl.dma("scalar", s_kv[sl],
                           K_sb[:, ko + ncan * CH:ko + nsl * CH],
                           k_loc[own, kvh])
                    for si, cp in enumerate(canon):
                        pl.dma("scalar", s_kv[sl],
                               V_sb[:, vo + si * 4 * 128:vo + (si + 1) * 4 * 128]
                               .rearrange("p (t f) -> p t f", t=4),
                               v_gat[cp, :, :, kvh * 128:(kvh + 1) * 128]
                               .rearrange("t p f -> p t f"))
                    kv_tick[sl] = pl.dma(
                        "scalar", s_kv[sl],
                        V_sb[:, vo + ncan * 4 * 128:vo + nsl * 4 * 128]
                        .rearrange("p (t f) -> p t f", t=4),
                        v_loc[own, :, :, kvh * 128:(kvh + 1) * 128]
                        .rearrange("t p f -> p t f"))

                def prep(kvh):
                    sl = kvh % 2
                    ko = sl * nsl * CH
                    pl.wait("vector", s_kv[sl], kv_tick[sl])
                    for kc in range(nsl):
                        tkk = pl.op("vector", "tensor_mul", kk_r[:],
                                    K_sb[:, ko + kc * CH:ko + (kc + 1) * CH]
                                    .bitcast(F32),
                                    K_sb[:, ko + kc * CH:ko + (kc + 1) * CH]
                                    .bitcast(F32), inc=(sEV, 1))
                        pe_take(7)
                        pl.wait("tensor", sEV, tkk)
                        tmm = mm(7, pb[7][0:1, 0:CH], c_onec[:], kk_r[:],
                                 True, True, inc=(sMM, 1))
                        pl.wait("vector", sMM, tmm)
                        tkn = pl.op("vector", "reduce_max",
                                    kcmax[0:1, kc:kc + 1],
                                    pb[7][0:1, 0:CH], axis=AX.X,
                                    inc=(sEV, 1))
                        bfree[7] = (sEV, tkn)
                    pl.op("vector", "tensor_copy", km2[0:1, sl * 8 + 6:sl * 8 + 7],
                          c_cst[0:1, 0:1])
                    pl.op("vector", "reduce_max", km2[0:1, sl * 8:sl * 8 + 1],
                          kcmax[0:1, 0:nsl], axis=AX.X)
                    pl.op("vector", "tensor_copy", km2[0:1, sl * 8 + 1:sl * 8 + 2],
                          c_cst[0:1, 0:1])
                    pl.op("vector", "tensor_copy", km2[0:1, sl * 8 + 3:sl * 8 + 4],
                          km2[0:1, sl * 8:sl * 8 + 1])
                    for ht in range(4):
                        head = kvh * 4 + ht
                        qv = q_all[:, head * CH:(head + 1) * CH]
                        tqq = pl.op("vector", "tensor_mul", qq_r[:],
                                    qv.bitcast(F32), qv.bitcast(F32),
                                    inc=(sEV, 1))
                        pe_take(7)
                        pl.wait("tensor", sEV, tqq)
                        tmm = mm(7, pb[7][0:1, 0:CH], c_onec[:], qq_r[:],
                                 True, True, inc=(sMM, 1))
                        pl.wait("vector", sMM, tmm)
                        tqn = pl.op("vector", "tensor_scalar",
                                    qn2_row[0:1, ht * CH:(ht + 1) * CH],
                                    pb[7][0:1, 0:CH],
                                    km2[0:1, sl * 8 + 3:sl * 8 + 4], None, ALU.mult,
                                    inc=(sEV, 1))
                        bfree[7] = (sEV, tqn)
                    pl.wait("scalar", sEV, sEV.n)
                    pl.op("scalar", "activation",
                          qn2_row[0:1, :], qn2_row[0:1, :], AF.Ln,
                          bias=0.0, scale=1.0, inc=(sAC, 1))
                    tsq = pl.op("scalar", "activation",
                                qn2_row[0:1, :], qn2_row[0:1, :], AF.Exp,
                                bias=0.0, scale=0.5, inc=(sAC, 1))
                    pl.wait("vector", sAC, tsq)
                    tnm = pl.op("vector", "tensor_scalar",
                                nm_row[0:1, sl * 4 * CH:(sl + 1) * 4 * CH],
                                qn2_row[0:1, :], -1.001, None,
                                ALU.mult, inc=(sEV, 1))
                    return tnm

                kv_load(0)
                kv_load(1)
                nm_ticks = [None] * KV
                nm_ticks[0] = prep(0)
                nm_ticks[1] = prep(1)
                pend_norm = [None]

                def emit_norm(head, bpv, bsm, t_pv):
                    nonlocal pvout_sp
                    pl.wait("vector", sMM, t_pv)
                    trc = pl.op("vector", "reciprocal", rc_row[0:1, :],
                                pb[bsm][0:1, 0:CH], inc=(sEV, 1))
                    bfree[bsm] = (sEV, trc)
                    pe_take(6)
                    pl.wait("tensor", sEV, trc)
                    tmm = mm(6, pb[6][:, 0:CH], c_oner[:], rc_row[0:1, :],
                             True, True, inc=(sMM, 1))
                    pl.wait("vector", sMM, tmm)
                    trcp = pl.op("vector", "tensor_copy", rcrep[:],
                                 pb[6][:, 0:CH], inc=(sEV, 1))
                    bfree[6] = (sEV, trcp)
                    pl.wait("vector", sEV, trcp)
                    pl.wait("vector", sSPa, pvout_sp)
                    tev = pl.op("vector", "tensor_mul", pvout[:],
                                pb[bpv][:, 0:CH], rcrep[:], inc=(sEV, 1))
                    bfree[bpv] = (sEV, tev)
                    pl.wait("sync", sEV, tev)
                    pvout_sp = pl.dma("sync", sSPa, at_sc[head], pvout[:])
                    if debug and own == 0 and head == 0:
                        pl.dma("sync", sST, dbg["d_attn0"], pvout[:])
                        pl.wait("vector", sST, sST.n)

                for kvh in range(KV):
                    sl = kvh % 2
                    ko = sl * nsl * CH
                    vo = sl * nsl * 4 * 128
                    pl.wait("tensor", sEV, nm_ticks[kvh])
                    pl.wait("tensor", s_kv[sl], kv_tick[sl])
                    for ht in range(4):
                        head = kvh * 4 + ht
                        qv = q_all[:, head * CH:(head + 1) * CH]
                        nmv = nm_row[0:1, (sl * 4 + ht) * CH:
                                     (sl * 4 + ht + 1) * CH]
                        hp = ht % 2
                        bpv, bsm = 4 + hp, 0 + hp
                        nktile = nsl * 4
                        pe_take(bpv)
                        exp_tick = [0] * nktile

                        def emit_score(ki):
                            bs = 2 + (ki % 2)
                            pe_take(bs)
                            mm(bs, pb[bs][:, 0:CH],
                               K_sb[:, ko + ki * 128:ko + (ki + 1) * 128],
                               qv, True, False)
                            if ki // 4 == nsl - 1:
                                mm(bs, pb[bs][:, 0:CH], c_idnr[:],
                                   c_tri[:, (ki % 4) * CH:(ki % 4 + 1) * CH],
                                   False, False)
                            tmm = mm(bs, pb[bs][:, 0:CH], c_oner[:],
                                     nmv, False, True, inc=(sMM, 1))
                            eb = ki % 2
                            pl.wait("scalar", sMM, tmm)
                            pl.wait("scalar", sMM, expT_free[eb][0])
                            pl.wait("scalar", sEV, expT_free[eb][1])
                            bcol = base_off12 + (ki // 4)
                            tex = pl.op("scalar", "activation",
                                        expT[:, eb * CH:(eb + 1) * CH],
                                        pb[bs][:, 0:CH], AF.Exp,
                                        bias=baseb[:, bcol:bcol + 1],
                                        scale=SCALE, inc=(sAC, 1))
                            bfree[bs] = (sAC, tex)
                            exp_tick[ki] = tex

                        def emit_pv(ki):
                            eb = ki % 2
                            pl.wait("tensor", sAC, exp_tick[ki])
                            tpv = mm(bpv, pb[bpv][:, 0:CH],
                                     V_sb[:, vo + ki * 128:vo + (ki + 1) * 128],
                                     expT[:, eb * CH:(eb + 1) * CH],
                                     ki == 0, ki == nktile - 1, inc=(sMM, 1))
                            pl.wait("vector", sAC, exp_tick[ki])
                            if ki == 0:
                                tad = pl.op("vector", "tensor_copy", sume[:],
                                            expT[:, 0:CH].bitcast(F32),
                                            inc=(sEV, 1))
                            else:
                                tad = pl.op("vector", "tensor_add", sume[:],
                                            sume[:],
                                            expT[:, eb * CH:(eb + 1) * CH]
                                            .bitcast(F32), inc=(sEV, 1))
                            expT_free[eb] = (tpv, tad)

                        emit_score(0)
                        for ki in range(nktile):
                            if ki + 1 < nktile:
                                emit_score(ki + 1)
                            emit_pv(ki)
                        pl.wait("vector", sMM, sumr_free[0])
                        tcv = pl.op("vector", "tensor_copy", sume_r[:],
                                    sume[:], inc=(sEV, 1))
                        pe_take(bsm)
                        pl.wait("tensor", sEV, tcv)
                        t_pv_done = mm(bsm, pb[bsm][0:1, 0:CH], c_onec[:],
                                       sume_r[:], True, True, inc=(sMM, 1))
                        sumr_free[0] = t_pv_done
                        if ht == 0 and kvh + 1 < KV:
                            nm_ticks[kvh + 1] = prep(kvh + 1)
                        if pend_norm[0] is not None:
                            emit_norm(*pend_norm[0])
                        pend_norm[0] = (head, bpv, bsm, t_pv_done)
                    kv_free[sl] = sMM.n
                    if kvh + 2 < KV:
                        kv_load(kvh + 2)
                emit_norm(*pend_norm[0])
                pend_norm[0] = None
                t_at_spills = sSPa.n
            pl.barrier(dsems)

            accs = ExitStack()
            acc = accs.enter_context(
                nc.sbuf_tensor("acc_%d" % own, [128, NKT * CH], F32))
            with ExitStack() as p2c:
                sb2c = lambda n, sh, dt_: p2c.enter_context(
                    nc.sbuf_tensor(n + "_%d" % own, sh, dt_))
                atg = sb2c("atg", [128, 2 * 8 * CH], F32R)
                wob = sb2c("wob", [128, 4 * 8 * 128], F32R)
                s_atg = [mksem("s_atg%d_0" % own), mksem("s_atg%d_1" % own)]
                s_wo = [mksem("s_wo%d_%d" % (own, i)) for i in range(4)]
                atg_free = [0, 0]
                wob_free = [0, 0, 0, 0]
                acc_tick = [0] * NKT
                pl.wait("sync", sSPa, t_at_spills)
                for qg in range(4):
                    gbuf = qg % 2
                    pl.wait("sync", sMM, atg_free[gbuf])
                    tat = pl.dma("sync", s_atg[gbuf],
                                 atg[:, gbuf * 8 * CH:(gbuf + 1) * 8 * CH]
                                 .rearrange("p (k f) -> p k f", k=8),
                                 at_sc[qg * 8:(qg + 1) * 8].rearrange("k p f -> p k f"))
                    pl.wait("tensor", s_atg[gbuf], tat)
                    for dmch in range(NKT):
                        wbuf = dmch % 4
                        pl.wait("gpsimd", sMM, wob_free[wbuf])
                        two = pl.dma("gpsimd", s_wo[wbuf],
                                     wob[:, wbuf * 8 * 128:(wbuf + 1) * 8 * 128]
                                     .rearrange("p (k f) -> p k f", k=8),
                                     wo_t[qg, dmch].rearrange("k p f -> p k f"))
                        pl.wait("tensor", s_wo[wbuf], two)
                        bo = dmch % 2
                        pe_take(bo)
                        for ik in range(8):
                            tmm = mm(bo, pb[bo][:, 0:CH],
                                     wob[:, (wbuf * 8 + ik) * 128:
                                         (wbuf * 8 + ik + 1) * 128],
                                     atg[:, (gbuf * 8 + ik) * CH:
                                         (gbuf * 8 + ik + 1) * CH],
                                     ik == 0, ik == 7,
                                     inc=(sMM, 1) if ik == 7 else None)
                        wob_free[wbuf] = tmm
                        atg_free[gbuf] = tmm
                        pl.wait("vector", sMM, tmm)
                        if qg > 0:
                            pl.wait("vector", sEV, acc_tick[dmch])
                        if qg == 0:
                            ta = pl.op("vector", "tensor_copy",
                                       acc[:, dmch * CH:(dmch + 1) * CH],
                                       pb[bo][:, 0:CH], inc=(sEV, 1))
                        else:
                            ta = pl.op("vector", "tensor_add",
                                       acc[:, dmch * CH:(dmch + 1) * CH],
                                       acc[:, dmch * CH:(dmch + 1) * CH],
                                       pb[bo][:, 0:CH], inc=(sEV, 1))
                        bfree[bo] = (sEV, ta)
                        acc_tick[dmch] = ta
            pl.barrier(dsems)

            with ExitStack() as p3:
                sb3 = lambda n, sh, dt_: p3.enter_context(
                    nc.sbuf_tensor(n + "_%d" % own, sh, dt_))
                xnf = sb3("xnf", [128, NKT * CH], F32R)
                w13b = sb3("w13b", [128, 4 * 2 * 8 * 128], F32R)
                w2b = sb3("w2b", [128, 3 * 8 * 128], F32R)
                hgb = sb3("hgb", [128, 8 * CH], F32R)
                silub = sb3("silub", [128, CH], F32)

                s_xnf = mksem("s_xnf%d" % own)
                s_w13 = [mksem("s_w13%d_%d" % (own, i)) for i in range(4)]
                s_w2 = [mksem("s_w2%d_%d" % (own, i)) for i in range(3)]
                txn = pl.dma("sync", s_xnf,
                             xnf[:].rearrange("p (k f) -> p k f", k=NKT),
                             xn_sc[own].rearrange("k p f -> p k f"))
                pl.wait("tensor", s_xnf, txn)
                w13_free = [0, 0, 0, 0]
                w13_tick = {}
                w2_free = [0, 0, 0]

                def emit_w13_load(Q):
                    if Q >= IC * 4:
                        return
                    slot = Q % 4
                    pl.wait("gpsimd", sMM, w13_free[slot])
                    w13_tick[Q] = pl.dma(
                        "gpsimd", s_w13[slot],
                        w13b[:, slot * 2 * 8 * 128:(slot + 1) * 2 * 8 * 128]
                        .rearrange("p (w k f) -> p w k f", w=2, k=8),
                        w13_t[Q // 4, Q % 4].rearrange("w k p f -> p w k f"))

                emit_w13_load(0)
                emit_w13_load(1)
                emit_w13_load(2)
                hg_free = 0
                silu_free = [0]
                ic0 = 0
                for g, gsz in enumerate(FFN_GROUPS):
                    pl.wait("scalar", sMM, hg_free)
                    pl.wait("vector", sMM, hg_free)
                    for ii in range(gsz):
                        ich = ic0 + ii
                        bu, bg = 0 + (ii % 2), 2 + (ii % 2)
                        for q in range(4):
                            Q = ich * 4 + q
                            slot = Q % 4
                            emit_w13_load(Q + 3)
                            pl.wait("tensor", s_w13[slot], w13_tick[Q])
                            if q == 0:
                                pe_take(bu)
                                pe_take(bg)
                            for k8 in range(8):
                                kt = q * 8 + k8
                                base = slot * 2 * 8 * 128
                                mm(bu, pb[bu][:, 0:CH],
                                   w13b[:, base + k8 * 128:
                                        base + (k8 + 1) * 128],
                                   xnf[:, kt * CH:(kt + 1) * CH],
                                   kt == 0, kt == NKT - 1)
                                tmm = mm(bg, pb[bg][:, 0:CH],
                                         w13b[:, base + (8 + k8) * 128:
                                              base + (8 + k8 + 1) * 128],
                                         xnf[:, kt * CH:(kt + 1) * CH],
                                         kt == 0, kt == NKT - 1,
                                         inc=(sMM, 1) if k8 == 7 else None)
                            w13_free[slot] = tmm
                        pl.wait("scalar", sMM, tmm)
                        pl.wait("scalar", sEV, silu_free[0])
                        sACt = pl.op("scalar", "activation", silub[:],
                                     pb[bu][:, 0:CH], AF.Silu,
                                     bias=0.0, scale=c_scl[:, ich:ich + 1],
                                     inc=(sAC, 1))
                        bfree[bu] = (sAC, sACt)
                        pl.wait("vector", sAC, sACt)
                        sEVt = pl.op("vector", "scalar_tensor_tensor",
                                     hgb[:, ii * CH:(ii + 1) * CH],
                                     pb[bg][:, 0:CH],
                                     c_scl[:, IC + ich:IC + ich + 1],
                                     silub[:], ALU.mult, ALU.mult,
                                     inc=(sEV, 1))
                        bfree[bg] = (sEV, sEVt)
                        silu_free[0] = sEVt
                    t_h_ready = sEV.n
                    pl.wait("tensor", sEV, t_h_ready)
                    for dmch in range(NKT):
                        wbuf = dmch % 3
                        pl.wait("gpsimd", sMM, w2_free[wbuf])
                        goff = (g * 32 + dmch) * 8 if g < 10 else (2560 + dmch * 6)
                        t2 = pl.dma("gpsimd", s_w2[wbuf],
                                    w2b[:, wbuf * 8 * 128:
                                        wbuf * 8 * 128 + gsz * 128]
                                    .rearrange("p (k f) -> p k f", k=gsz),
                                    w2_t[goff:goff + gsz].rearrange("k p f -> p k f"))
                        pl.wait("tensor", s_w2[wbuf], t2)
                        bd = 4 + (dmch % 2)
                        pe_take(bd)
                        for ik in range(gsz):
                            tmm = mm(bd, pb[bd][:, 0:CH],
                                     w2b[:, (wbuf * 8 + ik) * 128:
                                         (wbuf * 8 + ik + 1) * 128],
                                     hgb[:, ik * CH:(ik + 1) * CH],
                                     ik == 0, ik == gsz - 1,
                                     inc=(sMM, 1) if ik == gsz - 1 else None)
                        w2_free[wbuf] = tmm
                        hg_free = tmm
                        pl.wait("vector", sMM, tmm)
                        pl.wait("vector", sEV, acc_tick[dmch])
                        ta = pl.op("vector", "scalar_tensor_tensor",
                                   acc[:, dmch * CH:(dmch + 1) * CH],
                                   pb[bd][:, 0:CH],
                                   c_scl[:, 2 * IC + dmch:2 * IC + dmch + 1],
                                   acc[:, dmch * CH:(dmch + 1) * CH],
                                   ALU.mult, ALU.add, inc=(sEV, 1))
                        bfree[bd] = (sEV, ta)
                        acc_tick[dmch] = ta
                    ic0 += gsz
                if debug and own == 0:
                    pl.wait("vector", sEV, acc_tick[0])
                    tev = pl.op("vector", "tensor_copy", silub[:],
                                acc[:, 0:CH], inc=(sEV, 1))
                    pl.wait("sync", sEV, tev)
                    pl.dma("sync", sST, dbg["d_acc0"], silub[:])
                    pl.wait("scalar", sST, sST.n)
                    pl.wait("vector", sST, sST.n)
            pl.barrier(dsems)

            with ExitStack() as p4:
                sb4 = lambda n, sh, dt_: p4.enter_context(
                    nc.sbuf_tensor(n + "_%d" % own, sh, dt_))
                xres = sb4("xres", [128, 4 * CH], F32)
                stf = sb4("stf", [128, 2 * CH], F32)
                sto = sb4("sto", [128, 2 * CH], F32)
                s_xres = [mksem("s_xr%d_%d" % (own, i)) for i in range(4)]
                xres_free = [0, 0, 0, 0]
                tx_t = [0, 0, 0, 0]
                stf_free = [0, 0]
                sto_free = [0, 0]

                def x_load(dmch):
                    db = dmch % 4
                    pl.wait("sync", sEV, xres_free[db])
                    tx_t[db] = pl.dma("sync", s_xres[db],
                                      xres[:, db * CH:(db + 1) * CH],
                                      xTp[own, dmch])
                for i in range(3):
                    x_load(i)
                for dmch in range(NKT):
                    db = dmch % 4
                    dbuf = dmch % 2
                    pl.wait("vector", s_xres[db], tx_t[db])
                    pl.wait("vector", sMM, stf_free[dbuf])
                    pl.wait("vector", sEV, acc_tick[dmch])
                    tev = pl.op("vector", "tensor_add",
                                stf[:, dbuf * CH:(dbuf + 1) * CH],
                                acc[:, dmch * CH:(dmch + 1) * CH],
                                xres[:, db * CH:(db + 1) * CH],
                                inc=(sEV, 1))
                    xres_free[db] = tev
                    if dmch + 3 < NKT:
                        x_load(dmch + 3)
                    bt = 6 + dbuf
                    pe_take(bt)
                    pl.wait("tensor", sEV, tev)
                    for tt in range(4):
                        ttr = transp(pb[bt][:, tt * 128:(tt + 1) * 128],
                                     stf[:, dbuf * CH + tt * 128:
                                         dbuf * CH + (tt + 1) * 128],
                                     inc=(sMM, 1) if tt == 3 else None)
                    stf_free[dbuf] = ttr
                    pl.wait("scalar", sMM, ttr)
                    pl.wait("scalar", sSTo[dbuf], sto_free[dbuf])
                    sACt = pl.op("scalar", "activation",
                                 sto[:, dbuf * CH:(dbuf + 1) * CH],
                                 pb[bt][:, 0:CH], AF.Identity,
                                 bias=0.0, scale=1.0, inc=(sAC, 1))
                    bfree[bt] = (sAC, sACt)
                    pl.wait("sync", sAC, sACt)
                    tst = pl.dma("sync", sSTo[dbuf],
                                 out_h[own, :, dmch * 128:(dmch + 1) * 128]
                                 .rearrange("(t p) f -> p t f", p=128),
                                 sto[:, dbuf * CH:(dbuf + 1) * CH]
                                 .rearrange("p (t f) -> p t f", t=4))
                    sto_free[dbuf] = tst
                pl.wait("sync", sSTo[0], sSTo[0].n)
                pl.wait("sync", sSTo[1], sSTo[1].n)
                pl.wait("sync", sST, sST.n)
            accs.close()
            pl.barrier(dsems)

        pl.wait("sync", sSTo[0], sSTo[0].n)
        pl.wait("sync", sSTo[1], sSTo[1].n)
        pl.wait("sync", sST, sST.n)

        with nc.allow_low_precision(
                "fp32r outputs are the fp32-emulating hi/lo bf16 rounding; "
                "all matmul accumulation stays fp32 in PSUM"), \
             nc.Block() as block:
            @block.sync
            def _(sync):
                for fn in pl.q["sync"]:
                    fn(sync)

            @block.gpsimd
            def _(gpsimd):
                for fn in pl.q["gpsimd"]:
                    fn(gpsimd)

            @block.vector
            def _(vector):
                for fn in pl.q["vector"]:
                    fn(vector)

            @block.scalar
            def _(scalar):
                for fn in pl.q["scalar"]:
                    fn(scalar)

            @block.tensor
            def _(tensor):
                for fn in pl.q["tensor"]:
                    fn(tensor)

    return nc


def make_perm(j):
    A, Bc = j, 7 - j
    slots = [None] * 8
    slots[3], slots[7] = A, Bc
    for c in range(A):
        slots[c] = c
    rest = [c for c in range(8) if c not in (A, Bc) and c >= A]
    free = [s for s in (0, 1, 2, 4, 5, 6) if slots[s] is None]
    for s, c in zip(free, rest):
        slots[s] = c
    return slots, A, Bc


def rope_tables(positions):
    d2 = np.arange(0, HD, 2, dtype=np.float64) / HD
    inv = 1.0 / (10000.0 ** d2)
    fr = np.outer(positions.astype(np.float64), inv)
    emb = np.concatenate([fr, fr], axis=1)
    return (np.cos(emb).T.astype(np.float32),
            np.sin(emb).T.astype(np.float32))


def prep_core_inputs(core, x, Wt, consts):
    b, j = core // 4, core % 4
    A, Bc = j, 7 - j
    xT = np.ascontiguousarray(x[b].T)
    xTp = np.empty((2, NKT, 128, CH), dtype=np.float32)
    cosk = np.empty((2, 128, CH), dtype=np.float32)
    sink = np.empty((2, 128, CH), dtype=np.float32)
    for s, c in enumerate((A, Bc)):
        blkT = xT[:, c * CH:(c + 1) * CH]
        xTp[s] = blkT.reshape(NKT, 128, CH)
        pos = np.arange(c * CH, (c + 1) * CH)
        cosk[s], sink[s] = rope_tables(pos)
    def chunk_at(p):
        return p // 2 if p % 2 == 0 else 7 - (p // 2)
    base = np.full((128, 13), 44.0, dtype=np.float32)
    for kc in range(3):
        if not (kc < A):
            base[:, kc] = NEG
    for p in range(8):
        if not (chunk_at(p) < Bc):
            base[:, 4 + p] = NEG
    d = dict(Wt)
    d.update(consts)
    d["xTp"] = xTp
    d["cosk"] = cosk
    d["sink"] = sink
    d["base"] = base
    return d, (b, A, Bc)


def prep_shared(L0_attn_wq, L0_attn_wk, L0_attn_wv, L0_attn_wo,
                L0_ffn_w1_q, L0_ffn_w1_s, L0_ffn_w3_q, L0_ffn_w3_s,
                L0_ffn_w2_q, L0_ffn_w2_s):
    f32 = np.float32
    wqT = np.ascontiguousarray(np.asarray(L0_attn_wq, f32).T)
    wkT = np.ascontiguousarray(np.asarray(L0_attn_wk, f32).T)
    wvT = np.ascontiguousarray(np.asarray(L0_attn_wv, f32).T)
    woT = np.ascontiguousarray(np.asarray(L0_attn_wo, f32).T)
    w1T = np.ascontiguousarray(np.asarray(L0_ffn_w1_q, f32).T)
    w3T = np.ascontiguousarray(np.asarray(L0_ffn_w3_q, f32).T)
    w2T = np.ascontiguousarray(np.asarray(L0_ffn_w2_q, f32).T)

    wq_t = np.empty((KV, NKT, 128, 512), f32)
    for kvh in range(KV):
        for kt in range(NKT):
            wq_t[kvh, kt] = wqT[kt * 128:(kt + 1) * 128,
                                kvh * 512:(kvh + 1) * 512]
    wk_t = np.empty((KV, NKT, 128, 128), f32)
    for kvh in range(KV):
        for kt in range(NKT):
            wk_t[kvh, kt] = wkT[kt * 128:(kt + 1) * 128,
                                kvh * 128:(kvh + 1) * 128]
    wv_t = np.empty((2, NKT, 128, 512), f32)
    for vch in range(2):
        for kt in range(NKT):
            wv_t[vch, kt] = wvT[kt * 128:(kt + 1) * 128,
                                vch * 512:(vch + 1) * 512]
    wo_t = np.empty((4, 32, 8, 128, 128), f32)
    for qg in range(4):
        for dmch in range(32):
            for ik in range(8):
                qk = qg * 8 + ik
                wo_t[qg, dmch, ik] = woT[qk * 128:(qk + 1) * 128,
                                         dmch * 128:(dmch + 1) * 128]
    w13_t = np.empty((IC, 4, 2, 8, 128, 128), f32)
    for ich in range(IC):
        for kt in range(NKT):
            w13_t[ich, kt // 8, 0, kt % 8] = w1T[kt * 128:(kt + 1) * 128,
                                                 ich * 128:(ich + 1) * 128]
            w13_t[ich, kt // 8, 1, kt % 8] = w3T[kt * 128:(kt + 1) * 128,
                                                 ich * 128:(ich + 1) * 128]
    w2_t = np.empty((2752, 128, 128), f32)
    idx = 0
    ic0 = 0
    for g, gsz in enumerate(FFN_GROUPS):
        for dmch in range(32):
            for ik in range(gsz):
                ich = ic0 + ik
                w2_t[idx] = w2T[ich * 128:(ich + 1) * 128,
                                dmch * 128:(dmch + 1) * 128]
                idx += 1
        ic0 += gsz
    assert idx == 2752

    scl = np.zeros((128, 2 * IC + 32), f32)
    scl[:, 0:IC] = np.asarray(L0_ffn_w1_s, f32).reshape(IC, 128).T
    scl[:, IC:2 * IC] = np.asarray(L0_ffn_w3_s, f32).reshape(IC, 128).T
    scl[:, 2 * IC:] = np.asarray(L0_ffn_w2_s, f32).reshape(32, 128).T

    csts = np.zeros((128, 24), f32)
    csts[:, 2] = EPS
    csts[:, 8:22] = 1.0

    idn = np.eye(128, dtype=f32)
    prot = np.zeros((128, 128), f32)
    for e in range(64):
        prot[e + 64, e] = -1.0
    for e in range(64):
        prot[e, e + 64] = 1.0
    tri = np.zeros((4, 128, CH), f32)
    for kt in range(4):
        kj = kt * 128 + np.arange(128)[:, None]
        qi = np.arange(CH)[None, :]
        tri[kt] = np.where(kj <= qi, 0.0, NEG)
    onec = np.ones((128, 1), f32)
    oner = np.ones((1, 128), f32)

    Wt = {"wq_t": wq_t, "wk_t": wk_t, "wv_t": wv_t, "wo_t": wo_t,
          "w13_t": w13_t, "w2_t": w2_t, "scl": scl}
    consts = {"csts": csts, "idn": idn, "prot": prot, "tri": tri,
              "onec": onec, "oner": oner}
    return Wt, consts


def kernel(L0_attn_hidden_states, L0_attn_wq, L0_attn_wk, L0_attn_wv,
           L0_attn_wo, L0_ffn_w1_q, L0_ffn_w1_s, L0_ffn_w3_q, L0_ffn_w3_s,
           L0_ffn_w2_q, L0_ffn_w2_s):
    x = np.asarray(L0_attn_hidden_states, np.float32)
    Wt, consts = prep_shared(L0_attn_wq, L0_attn_wk, L0_attn_wv, L0_attn_wo,
                             L0_ffn_w1_q, L0_ffn_w1_s, L0_ffn_w3_q,
                             L0_ffn_w3_s, L0_ffn_w2_q, L0_ffn_w2_s)
    nc = build(debug=False)
    in_maps = []
    meta = []
    for core in range(8):
        d, m = prep_core_inputs(core, x, Wt, consts)
        in_maps.append(d)
        meta.append(m)
    res = run_bass_kernel_spmd(nc, in_maps, core_ids=list(range(8)))
    out = np.empty((B, S, D), np.float32)
    for core in range(8):
        b, A, Bc = meta[core]
        o = res.results[core]["out"]
        out[b, A * CH:(A + 1) * CH] = o[0]
        out[b, Bc * CH:(Bc + 1) * CH] = o[1]
    return out


# version 15
# speedup vs baseline: 1.0830x; 1.0092x over previous
import math
import numpy as np
from contextlib import ExitStack

import concourse.bass as bass
import concourse.mybir as mybir
from concourse.bass_utils import run_bass_kernel_spmd

F32 = mybir.dt.float32
F32R = mybir.dt.float32r
AF = mybir.ActivationFunctionType
ALU = mybir.AluOpType
AX = mybir.AxisListType

B, S, D = 2, 4096, 4096
H, KV, HD = 32, 8, 128
G = H // KV
I = 11008
IC = I // 128
NKT = D // 128
CH = 512
NSLOT = 8
NSL = (4, 9)
SCALE = 1.0 / math.sqrt(HD)
EPS = 1e-5
NEG = -1.0e30
FFN_GROUPS = [8] * 10 + [6]
ENGS = ("sync", "gpsimd", "vector", "scalar", "tensor")


class Sem:
    def __init__(self, h):
        self.h = h
        self.n = 0

    def bump(self, d=1):
        self.n += d
        return self.n


class Plan:
    def __init__(self):
        self.q = {e: [] for e in ENGS}
        self.autosem = {}

    def emit(self, eng, fn):
        self.q[eng].append(fn)

    def dma(self, eng, sem, out_ap, in_ap):
        t = sem.bump(16)
        self.emit(eng, lambda e, o=out_ap, i=in_ap, s=sem.h:
                  e.dma_start(out=o, in_=i).then_inc(s, 16))
        return t

    def wait(self, eng, sem, val):
        if val > 0:
            self.emit(eng, lambda e, s=sem.h, v=val: e.wait_ge(s, v))

    def op(self, eng, name, *a, inc=None, **kw):
        auto = self.autosem.get(eng)
        if auto is not None:
            if auto.n > 0:
                self.wait(eng, auto, auto.n)
            if inc is None:
                inc = (auto, 1)
            else:
                assert inc[0] is auto, "non-auto inc on %s" % eng
        t = None
        if inc is not None:
            t = inc[0].bump(inc[1])
            ih = inc[0].h
        else:
            ih = None

        def f(e, nm=name, aa=a, kk=kw, hh=ih, vv=(inc[1] if inc else 0)):
            ins = getattr(e, nm)(*aa, **kk)
            if hh is not None:
                ins.then_inc(hh, vv)
        self.emit(eng, f)
        return t

    def barrier(self, dsems):
        for e in ENGS:
            s = dsems[e]
            s.bump(1)
            self.emit(e, lambda en, h=s.h: en.sem_inc(h, 1))
        for e in ENGS:
            for e2 in ENGS:
                if e2 != e:
                    self.wait(e, dsems[e2], dsems[e2].n)


def build(debug=False):
    nc = bass.Bass("TRN2", target_bir_lowering=False)
    dr = lambda n, sh, dt_=F32, k="ExternalInput": nc.dram_tensor(n, sh, dt_, kind=k).ap()

    xTp = dr("xTp", [2, NKT, 128, CH])
    wq_t = dr("wq_t", [KV, NKT, 128, 512])
    wk_t = dr("wk_t", [KV, NKT, 128, 128])
    wv_t = dr("wv_t", [2, NKT, 128, 512])
    wo_t = dr("wo_t", [4, 32, 8, 128, 128])
    w13_t = dr("w13_t", [IC, 4, 2, 8, 128, 128])
    w2_t = dr("w2_t", [2752, 128, 128])
    scl_h = dr("scl", [128, 2 * IC + 32])
    cst_h = dr("csts", [128, 24])
    idn_h = dr("idn", [128, 128])
    prot_h = dr("prot", [128, 128])
    tri_h = dr("tri", [4, 128, CH])
    base_h = dr("base", [128, 13])
    cosk_h = dr("cosk", [2, 128, CH])
    sink_h = dr("sink", [2, 128, CH])
    onec_h = dr("onec", [128, 1])
    oner_h = dr("oner", [1, 128])
    out_h = dr("out", [2, CH, D], F32, "ExternalOutput")

    dbg = {}
    if debug:
        for nm, sh, dt_ in [("d_xn3", [NKT, 128, CH], F32R),
                            ("d_k00", [128, CH], F32R),
                            ("d_v00", [128, 512], F32R),
                            ("d_q0", [128, CH], F32R),
                            ("d_attn0", [128, CH], F32R),
                            ("d_acc0", [128, CH], F32)]:
            dbg[nm] = dr(nm, sh, dt_, "ExternalOutput")

    k_loc = dr("k_loc", [2, KV, 128, CH], F32R, "Internal")
    v_loc = dr("v_loc", [2, 4, 128, KV * 128], F32R, "Internal")
    k_gat = dr("k_gat", [8, KV, 128, CH], F32R, "Internal")
    v_gat = dr("v_gat", [8, 4, 128, KV * 128], F32R, "Internal")
    xn_sc = dr("xn_sc", [2, NKT, 128, CH], F32R, "Internal")
    at_sc = dr("at_sc", [H, 128, CH], F32R, "Internal")

    pl = Plan()

    with ExitStack() as st:
        def mksem(name):
            return Sem(st.enter_context(nc.semaphore(name)))

        sLD = mksem("sLD")
        sLW = mksem("sLW")
        sSP = mksem("sSP")
        sMM = mksem("sMM")
        sEV = mksem("sEV")
        sAC = mksem("sAC")
        sST = mksem("sST")
        dsems = {e: mksem("sdn_" + e) for e in ENGS}
        pl.autosem = {"vector": sEV, "scalar": sAC}
        s_cs = mksem("s_cs")
        sSPx = mksem("sSPx"); sSPk = mksem("sSPk"); sSPv = mksem("sSPv")
        sSPa = mksem("sSPa")
        sSTo = [mksem("sSTo0"), mksem("sSTo1")]
        sCC = mksem("sCC")

        c_idn = st.enter_context(nc.sbuf_tensor("c_idn", [128, 128], F32))
        c_idnr = st.enter_context(nc.sbuf_tensor("c_idnr", [128, 128], F32R))
        c_prot = st.enter_context(nc.sbuf_tensor("c_prot", [128, 128], F32R))
        c_tri = st.enter_context(nc.sbuf_tensor("c_tri", [128, 4 * CH], F32R))
        c_scl = st.enter_context(nc.sbuf_tensor("c_scl", [128, 2 * IC + 32], F32))
        c_cst = st.enter_context(nc.sbuf_tensor("c_cst", [128, 24], F32))
        c_onec = st.enter_context(nc.sbuf_tensor("c_onec", [128, 1], F32R))
        c_oner = st.enter_context(nc.sbuf_tensor("c_oner", [1, 128], F32R))
        c_cos = st.enter_context(nc.sbuf_tensor("c_cos", [128, CH], F32))
        c_sin = st.enter_context(nc.sbuf_tensor("c_sin", [128, CH], F32))

        pl.dma("sync", sLD, c_idn[:], idn_h)
        pl.dma("gpsimd", sLW, c_tri[:].rearrange("p (q f) -> p q f", q=4),
               tri_h.rearrange("q p f -> p q f"))
        pl.dma("gpsimd", sLW, c_idnr[:], idn_h)
        pl.dma("sync", sLD, c_scl[:], scl_h)
        pl.dma("sync", sLD, c_cst[:], cst_h)
        pl.dma("gpsimd", sLW, c_prot[:], prot_h)
        pl.dma("gpsimd", sLW, c_onec[:], onec_h)
        pl.dma("gpsimd", sLW, c_oner[:], oner_h)
        for e in ("vector", "scalar", "tensor"):
            pl.wait(e, sLD, sLD.n)
            pl.wait(e, sLW, sLW.n)

        pb = [st.enter_context(nc.psum_tensor("pb%d" % i, [128, 512], F32))
              for i in range(8)]
        bfree = [None] * 8

        def pe_take(bi):
            if bfree[bi] is not None:
                pl.wait("tensor", bfree[bi][0], bfree[bi][1])

        def mm(bi, out_ap, lhsT, rhs, start, stop, inc=None):
            def f(e, o=out_ap, l=lhsT, r=rhs, s=start, p=stop,
                  hh=(inc[0].h if inc else None), vv=(inc[1] if inc else 0)):
                ins = e.matmul(o, l, r, start=s, stop=p)
                if hh is not None:
                    ins.then_inc(hh, vv)
            if inc is not None:
                t = inc[0].bump(inc[1])
            else:
                t = None
            pl.emit("tensor", f)
            return t

        def transp(out_ap, in_ap, inc=None):
            def f(e, o=out_ap, i=in_ap,
                  hh=(inc[0].h if inc else None), vv=(inc[1] if inc else 0)):
                ins = e.transpose(o, i, c_idn[:])
                if hh is not None:
                    ins.then_inc(hh, vv)
            if inc is not None:
                t = inc[0].bump(inc[1])
            else:
                t = None
            pl.emit("tensor", f)
            return t

        def load_cos_sin(slot):
            pl.dma("sync", s_cs, c_cos[:], cosk_h[slot])
            t2 = pl.dma("sync", s_cs, c_sin[:], sink_h[slot])
            return t2

        with ExitStack() as p1:
            sb1 = lambda n, sh, dt_: p1.enter_context(nc.sbuf_tensor(n, sh, dt_))
            A_buf = sb1("A_buf", [128, NKT * CH], F32R)
            sq_b = sb1("sq_b", [128, CH], F32R)
            wkb = sb1("wkb", [128, 2 * NKT * 128], F32R)
            wvb = sb1("wvb", [128, 2 * 4 * 512], F32R)
            krawb = sb1("krawb", [128, CH], F32R)
            ktmp = sb1("ktmp", [128, 2 * CH], F32)
            K_out = sb1("K_out", [128, CH], F32R)
            V_out = sb1("V_out", [128, 512], F32R)
            mrow = sb1("mrow", [1, CH], F32R)
            rrow = sb1("rrow", [1, CH], F32R)
            srow = sb1("srow", [1, CH], F32)
            sp1 = sb1("sp1", [1, 8], F32)

            s_xb = [mksem("s_xb%d" % i) for i in range(4)]
            s_wk = [mksem("s_wk0"), mksem("s_wk1")]
            s_wv = [mksem("s_wv0"), mksem("s_wv1")]
            wk_free = [0, 0]
            wv_free = [0, 0]
            abuf_free_mm = 0
            abuf_spill = 0
            kout_spill = 0
            vout_spill = 0

            for s in range(2):
                pl.wait("sync", sEV, sEV.n)
                tcs = load_cos_sin(s)
                pl.wait("gpsimd", sMM, abuf_free_mm)
                pl.wait("gpsimd", sSPx, abuf_spill)
                tx_b = [0] * 4
                for blk in range(4):
                    kt0 = blk * 8
                    tx_b[blk] = pl.dma(
                        "gpsimd", s_xb[blk],
                        A_buf[:, kt0 * CH:(kt0 + 8) * CH]
                        .rearrange("p (k f) -> p k f", k=8),
                        xTp[s, kt0:kt0 + 8].rearrange("k p f -> p k f"))

                pe_take(0)
                pe_take(1)
                last_sq = 0
                for kt in range(NKT):
                    if kt % 8 == 0:
                        pl.wait("tensor", s_xb[kt // 8], tx_b[kt // 8])
                        pl.wait("vector", s_xb[kt // 8], tx_b[kt // 8])
                    xt = A_buf[:, kt * CH:(kt + 1) * CH]
                    mm(0, pb[0][0:1, :], c_onec[:], xt, kt == 0, kt == NKT - 1)
                    pl.wait("vector", sMM, last_sq)
                    tev = pl.op("vector", "tensor_mul", sq_b[:],
                                xt.bitcast(F32), xt.bitcast(F32), inc=(sEV, 1))
                    pl.wait("tensor", sEV, tev)
                    last_sq = mm(1, pb[1][0:1, :], c_onec[:], sq_b[:],
                                 kt == 0, kt == NKT - 1, inc=(sMM, 1))

                pl.wait("vector", sMM, last_sq)
                t0 = pl.op("vector", "tensor_scalar", mrow[:],
                           pb[0][0:1, :], 1.0 / D, None, ALU.mult,
                           inc=(sEV, 1))
                bfree[0] = (sEV, t0)
                t1 = pl.op("vector", "tensor_scalar", srow[:], pb[1][0:1, :],
                           1.0 / D, None, ALU.mult, inc=(sEV, 1))
                bfree[1] = (sEV, t1)
                pl.wait("vector", sEV, t1)
                pl.op("vector", "tensor_mul", ktmp[0:1, 0:CH],
                      mrow[:].bitcast(F32), mrow[:].bitcast(F32))
                tev = pl.op("vector", "tensor_tensor", srow[:], srow[:],
                            ktmp[0:1, 0:CH], ALU.subtract, inc=(sEV, 1))
                pl.wait("scalar", sEV, tev)
                tac = pl.op("scalar", "activation", srow[:], srow[:], AF.Sqrt,
                            bias=c_cst[0:1, 2:3], scale=1.0, inc=(sAC, 1))
                pl.wait("vector", sAC, tac)
                pl.op("vector", "tensor_copy", sp1[:, 0:1], c_cst[0:1, 0:1])
                tev = pl.op("vector", "reciprocal", rrow[:],
                            srow[:], inc=(sEV, 1))

                pe_take(2)
                pe_take(3)
                pl.wait("tensor", sEV, tev)
                mm(2, pb[2][:, 0:CH], c_oner[:], mrow[:], True, True)
                tmm = mm(3, pb[3][:, 0:CH], c_oner[:], rrow[:], True, True,
                         inc=(sMM, 1))

                pl.wait("vector", sMM, tmm)
                tev = 0
                for kt in range(NKT):
                    xt = A_buf[:, kt * CH:(kt + 1) * CH]
                    pl.op("vector", "tensor_tensor", ktmp[:, 0:CH],
                          xt.bitcast(F32), pb[2][:, 0:CH], ALU.subtract)
                    inc = (sEV, 1) if kt == NKT - 1 else None
                    tev2 = pl.op("vector", "tensor_tensor", xt, ktmp[:, 0:CH],
                                 pb[3][:, 0:CH], ALU.mult, inc=inc)
                    if tev2 is not None:
                        tev = tev2
                bfree[2] = (sEV, tev)
                bfree[3] = (sEV, tev)

                if True:
                    own = s
                    pl.wait("sync", sEV, tev)
                    abuf_spill = pl.dma("sync", sSPx,
                                        xn_sc[own].rearrange("k p f -> p k f"),
                                        A_buf[:].rearrange("p (k f) -> p k f",
                                                           k=NKT))
                    if debug and s == 0:
                        pl.dma("sync", sST,
                               dbg["d_xn3"].rearrange("k p f -> p k f"),
                               A_buf[:]
                               .rearrange("p (k f) -> p k f", k=NKT))
                        pl.wait("gpsimd", sST, sST.n)
                        pl.wait("tensor", sST, sST.n)
                        pl.wait("vector", sST, sST.n)

                pl.wait("tensor", sEV, tev)
                pl.wait("vector", s_cs, tcs)

                for kvh in range(KV):
                    buf = kvh % 2
                    pl.wait("gpsimd", sMM, wk_free[buf])
                    twk = pl.dma("gpsimd", s_wk[buf],
                                 wkb[:, buf * NKT * 128:(buf + 1) * NKT * 128]
                                 .rearrange("p (k f) -> p k f", k=NKT),
                                 wk_t[kvh].rearrange("k p f -> p k f"))
                    pl.wait("tensor", s_wk[buf], twk)
                    bk = 4 + (kvh % 2)
                    pe_take(bk)
                    for kt in range(NKT):
                        tmm = mm(bk, pb[bk][:, 0:CH],
                                 wkb[:, (buf * NKT + kt) * 128:
                                     (buf * NKT + kt + 1) * 128],
                                 A_buf[:, kt * CH:(kt + 1) * CH],
                                 kt == 0, kt == NKT - 1,
                                 inc=(sMM, 1) if kt == NKT - 1 else None)
                    wk_free[buf] = tmm
                    pl.wait("vector", sMM, tmm)
                    sEVt = pl.op("vector", "tensor_copy", krawb[:],
                                 pb[bk][:, 0:CH], inc=(sEV, 1))
                    bfree[bk] = (sEV, sEVt)
                    pe_take(6)
                    pl.wait("tensor", sEV, sEVt)
                    trot = mm(6, pb[6][:, 0:CH], c_prot[:], krawb[:], True, True,
                              inc=(sMM, 1))
                    pl.wait("vector", sSPk, kout_spill)
                    pl.wait("vector", sEV, sEVt)
                    pl.op("vector", "tensor_tensor", ktmp[:, 0:CH],
                          krawb[:].bitcast(F32), c_cos[:], ALU.mult)
                    pl.wait("vector", sMM, trot)
                    t6 = pl.op("vector", "tensor_tensor", ktmp[:, CH:2 * CH],
                               pb[6][:, 0:CH], c_sin[:], ALU.mult, inc=(sEV, 1))
                    bfree[6] = (sEV, t6)
                    tev = pl.op("vector", "tensor_add", K_out[:],
                                ktmp[:, 0:CH], ktmp[:, CH:2 * CH], inc=(sEV, 1))
                    pl.wait("sync", sEV, tev)
                    kout_spill = pl.dma("sync", sSPk, k_loc[s, kvh], K_out[:])
                    if debug and s == 0 and kvh == 0:
                        pl.dma("sync", sST, dbg["d_k00"], K_out[:])
                        pl.wait("vector", sST, sST.n)

                last_v = 0
                for vch in range(2):
                    for tt in range(4):
                        pe_take(tt)
                    for blk in range(8):
                        buf = blk % 2
                        pl.wait("gpsimd", sMM, wv_free[buf])
                        twv = pl.dma("gpsimd", s_wv[buf],
                                     wvb[:, buf * 4 * 512:(buf + 1) * 4 * 512]
                                     .rearrange("p (k f) -> p k f", k=4),
                                     wv_t[vch, blk * 4:(blk + 1) * 4].rearrange("k p f -> p k f"))
                        pl.wait("tensor", s_wv[buf], twv)
                        for k4 in range(4):
                            kt = blk * 4 + k4
                            for tt in range(4):
                                inc = (sMM, 1) if (k4 == 3 and tt == 3) else None
                                tmm = mm(tt, pb[tt][:, 0:512],
                                         A_buf[:, kt * CH + tt * 128:
                                               kt * CH + (tt + 1) * 128],
                                         wvb[:, (buf * 4 + k4) * 512:
                                             (buf * 4 + k4 + 1) * 512],
                                         kt == 0, kt == NKT - 1, inc=inc)
                                if tmm is not None:
                                    wv_free[buf] = tmm
                                    last_v = tmm
                    pl.wait("vector", sMM, last_v)
                    for tt in range(4):
                        pl.wait("vector", sSPv, vout_spill)
                        sEVt = pl.op("vector", "tensor_copy", V_out[:],
                                     pb[tt][:, 0:512], inc=(sEV, 1))
                        bfree[tt] = (sEV, sEVt)
                        pl.wait("sync", sEV, sEVt)
                        vout_spill = pl.dma("sync", sSPv,
                                            v_loc[s, tt, :, vch * 512:
                                                 (vch + 1) * 512], V_out[:])
                        if debug and s == 0 and tt == 0 and vch == 0:
                            pl.dma("sync", sST, dbg["d_v00"], V_out[:])
                            pl.wait("vector", sST, sST.n)
                abuf_free_mm = last_v

            t_spills_p1 = (sSPx.n, sSPk.n, sSPv.n)
            pl.wait("sync", sSPx, sSPx.n)
            pl.wait("sync", sSPk, sSPk.n)
            pl.wait("sync", sSPv, sSPv.n)
            pl.wait("gpsimd", sSPk, sSPk.n)
            pl.wait("gpsimd", sSPv, sSPv.n)
            pl.emit("gpsimd", lambda g: g.collective_compute(
                "AllGather", ALU.bypass,
                replica_groups=[[0, 1, 2, 3], [4, 5, 6, 7]],
                ins=[k_loc[:]], outs=[k_gat[:]]).then_inc(sCC.h, 1))
            sCC.bump(1)
            pl.emit("gpsimd", lambda g: g.collective_compute(
                "AllGather", ALU.bypass,
                replica_groups=[[0, 1, 2, 3], [4, 5, 6, 7]],
                ins=[v_loc[:]], outs=[v_gat[:]]).then_inc(sCC.h, 1))
            sCC.bump(1)
            t_cc = sCC.n
        pl.barrier(dsems)

        for own in range(2):
            nsl = NSL[own]
            ncan = nsl - 1
            base_off12 = 0 if own == 0 else 4
            tcs = load_cos_sin(own)

            with ExitStack() as p2:
                sb2 = lambda n, sh, dt_: p2.enter_context(
                    nc.sbuf_tensor(n + "_%d" % own, sh, dt_))
                q_all = sb2("q_all", [128, H * CH], F32R)
                qraw = sb2("qraw", [128, CH], F32R)
                K_sb = sb2("K_sb", [128, 2 * nsl * CH], F32R)
                V_sb = sb2("V_sb", [128, 2 * nsl * 4 * 128], F32R)
                baseb = sb2("baseb", [128, 13], F32)
                kk_r = sb2("kk_r", [128, CH], F32R)
                qq_r = sb2("qq_r", [128, CH], F32R)
                kcmax = sb2("kcmax", [1, 16], F32)
                km2 = sb2("km2", [1, 16], F32)
                qn2_row = sb2("qn2_row", [1, 4 * CH], F32)
                nm_row = sb2("nm_row", [1, 2 * 4 * CH], F32R)
                rc_row = sb2("rc_row", [1, CH], F32R)
                rcrep = sb2("rcrep", [128, CH], F32)
                expT = sb2("expT", [128, 2 * CH], F32R)
                pvout = sb2("pvout", [128, CH], F32R)
                p2a_st = ExitStack()
                xnb = p2a_st.enter_context(nc.sbuf_tensor(
                    "xnb_%d" % own, [128, 2 * 2 * CH], F32R))
                wqb = p2a_st.enter_context(nc.sbuf_tensor(
                    "wqb_%d" % own, [128, 2 * 2 * 512], F32R))

                s_base = mksem("s_base%d" % own)
                s_kv = [mksem("s_kv%d_0" % own), mksem("s_kv%d_1" % own)]
                s_wq = [mksem("s_wq%d_0" % own), mksem("s_wq%d_1" % own)]
                s_xn2 = [mksem("s_xn%d_0" % own), mksem("s_xn%d_1" % own)]
                tb = pl.dma("sync", s_base, baseb[:], base_h)
                pl.wait("scalar", s_base, tb)
                pl.wait("sync", sSPx, t_spills_p1[0])
                pl.wait("sync", sSPk, t_spills_p1[1])
                pl.wait("sync", sSPv, t_spills_p1[2])

                wq_free = [0, 0]
                xn_free = [0, 0]
                kv_free = [0, 0]
                kv_tick = [0, 0]
                expT_free = [(0, 0), (0, 0)]
                sumr_free = [0]
                pvout_sp = 0

                for kvh in range(KV):
                    for ht in range(4):
                        pe_take(ht)
                    for blk in range(16):
                        buf = blk % 2
                        pl.wait("gpsimd", sMM, wq_free[buf])
                        twq = pl.dma("gpsimd", s_wq[buf],
                                     wqb[:, buf * 2 * 512:(buf + 1) * 2 * 512]
                                     .rearrange("p (k f) -> p k f", k=2),
                                     wq_t[kvh, blk * 2:(blk + 1) * 2]
                                     .rearrange("k p f -> p k f"))
                        pl.wait("sync", sMM, xn_free[buf])
                        txn = pl.dma("sync", s_xn2[buf],
                                     xnb[:, buf * 2 * CH:(buf + 1) * 2 * CH]
                                     .rearrange("p (k f) -> p k f", k=2),
                                     xn_sc[own, blk * 2:(blk + 1) * 2]
                                     .rearrange("k p f -> p k f"))
                        pl.wait("tensor", s_wq[buf], twq)
                        pl.wait("tensor", s_xn2[buf], txn)
                        for k4 in range(2):
                            kt = blk * 2 + k4
                            for ht in range(4):
                                inc = (sMM, 1) if (k4 == 1 and ht == 3) else None
                                tmm = mm(ht, pb[ht][:, 0:CH],
                                         wqb[:, (buf * 2 + k4) * 512 + ht * 128:
                                             (buf * 2 + k4) * 512 + (ht + 1) * 128],
                                         xnb[:, (buf * 2 + k4) * CH:
                                             (buf * 2 + k4 + 1) * CH],
                                         kt == 0, kt == NKT - 1, inc=inc)
                        wq_free[buf] = tmm
                        xn_free[buf] = tmm
                    tqp = sMM.n
                    pl.wait("vector", sMM, tqp)
                    if kvh == 0:
                        pl.wait("vector", s_cs, tcs)
                    for ht in range(4):
                        head = kvh * 4 + ht
                        sEVt = pl.op("vector", "tensor_copy", qraw[:],
                                     pb[ht][:, 0:CH], inc=(sEV, 1))
                        bfree[ht] = (sEV, sEVt)
                        pe_take(4)
                        pl.wait("tensor", sEV, sEVt)
                        trot = mm(4, pb[4][:, 0:CH], c_prot[:], qraw[:],
                                  True, True, inc=(sMM, 1))
                        pl.wait("vector", sEV, sEVt)
                        pl.op("vector", "tensor_tensor",
                              kk_r[:],
                              qraw[:].bitcast(F32), c_cos[:], ALU.mult)
                        pl.wait("vector", sMM, trot)
                        t4 = pl.op("vector", "tensor_tensor",
                                   qq_r[:],
                                   pb[4][:, 0:CH], c_sin[:], ALU.mult,
                                   inc=(sEV, 1))
                        bfree[4] = (sEV, t4)
                        pl.op("vector", "tensor_add",
                              q_all[:, head * CH:(head + 1) * CH],
                              kk_r[:].bitcast(F32), qq_r[:].bitcast(F32),
                              inc=(sEV, 1))
                tq_ready = sEV.n
                pl.wait("tensor", sEV, tq_ready)
                if debug and own == 0:
                    pl.wait("sync", sEV, tq_ready)
                    pl.dma("sync", sST, dbg["d_q0"], q_all[:, 0:CH])
                    pl.wait("vector", sST, sST.n)
                p2a_st.close()
                sume = sb2("sume", [128, CH], F32)
                sume_r = sb2("sume_r", [128, CH], F32R)
                expin = sb2("expin", [128, 2 * CH], F32)
                expin_free = [0, 0]
                nm_rep = sb2("nm_rep", [128, CH], F32)

                csl = slice(0, 6, 2) if own == 0 else slice(0, 8)
                canon = (range(0, 6, 2) if own == 0 else range(8))

                def kv_load(kvh):
                    sl = kvh % 2
                    ko = sl * nsl * CH
                    vo = sl * nsl * 4 * 128
                    pl.wait("scalar", sMM, kv_free[sl])
                    if kvh == 0:
                        pl.wait("scalar", sCC, t_cc)
                    pl.dma("scalar", s_kv[sl],
                           K_sb[:, ko:ko + ncan * CH]
                           .rearrange("p (s f) -> p s f", s=ncan),
                           k_gat[csl, kvh].rearrange("s p f -> p s f"))
                    pl.dma("scalar", s_kv[sl],
                           K_sb[:, ko + ncan * CH:ko + nsl * CH],
                           k_loc[own, kvh])
                    for si, cp in enumerate(canon):
                        pl.dma("scalar", s_kv[sl],
                               V_sb[:, vo + si * 4 * 128:vo + (si + 1) * 4 * 128]
                               .rearrange("p (t f) -> p t f", t=4),
                               v_gat[cp, :, :, kvh * 128:(kvh + 1) * 128]
                               .rearrange("t p f -> p t f"))
                    kv_tick[sl] = pl.dma(
                        "scalar", s_kv[sl],
                        V_sb[:, vo + ncan * 4 * 128:vo + nsl * 4 * 128]
                        .rearrange("p (t f) -> p t f", t=4),
                        v_loc[own, :, :, kvh * 128:(kvh + 1) * 128]
                        .rearrange("t p f -> p t f"))

                def prep(kvh):
                    sl = kvh % 2
                    ko = sl * nsl * CH
                    pl.wait("vector", s_kv[sl], kv_tick[sl])
                    for kc in range(nsl):
                        tkk = pl.op("vector", "tensor_mul", kk_r[:],
                                    K_sb[:, ko + kc * CH:ko + (kc + 1) * CH]
                                    .bitcast(F32),
                                    K_sb[:, ko + kc * CH:ko + (kc + 1) * CH]
                                    .bitcast(F32), inc=(sEV, 1))
                        pe_take(7)
                        pl.wait("tensor", sEV, tkk)
                        tmm = mm(7, pb[7][0:1, 0:CH], c_onec[:], kk_r[:],
                                 True, True, inc=(sMM, 1))
                        pl.wait("vector", sMM, tmm)
                        tkn = pl.op("vector", "reduce_max",
                                    kcmax[0:1, kc:kc + 1],
                                    pb[7][0:1, 0:CH], axis=AX.X,
                                    inc=(sEV, 1))
                        bfree[7] = (sEV, tkn)
                    pl.op("vector", "tensor_copy", km2[0:1, sl * 8 + 6:sl * 8 + 7],
                          c_cst[0:1, 0:1])
                    pl.op("vector", "reduce_max", km2[0:1, sl * 8:sl * 8 + 1],
                          kcmax[0:1, 0:nsl], axis=AX.X)
                    pl.op("vector", "tensor_copy", km2[0:1, sl * 8 + 1:sl * 8 + 2],
                          c_cst[0:1, 0:1])
                    pl.op("vector", "tensor_copy", km2[0:1, sl * 8 + 3:sl * 8 + 4],
                          km2[0:1, sl * 8:sl * 8 + 1])
                    for ht in range(4):
                        head = kvh * 4 + ht
                        qv = q_all[:, head * CH:(head + 1) * CH]
                        tqq = pl.op("vector", "tensor_mul", qq_r[:],
                                    qv.bitcast(F32), qv.bitcast(F32),
                                    inc=(sEV, 1))
                        pe_take(7)
                        pl.wait("tensor", sEV, tqq)
                        tmm = mm(7, pb[7][0:1, 0:CH], c_onec[:], qq_r[:],
                                 True, True, inc=(sMM, 1))
                        pl.wait("vector", sMM, tmm)
                        tqn = pl.op("vector", "tensor_scalar",
                                    qn2_row[0:1, ht * CH:(ht + 1) * CH],
                                    pb[7][0:1, 0:CH],
                                    km2[0:1, sl * 8 + 3:sl * 8 + 4], None, ALU.mult,
                                    inc=(sEV, 1))
                        bfree[7] = (sEV, tqn)
                    pl.wait("scalar", sEV, sEV.n)
                    pl.op("scalar", "activation",
                          qn2_row[0:1, :], qn2_row[0:1, :], AF.Ln,
                          bias=0.0, scale=1.0, inc=(sAC, 1))
                    tsq = pl.op("scalar", "activation",
                                qn2_row[0:1, :], qn2_row[0:1, :], AF.Exp,
                                bias=0.0, scale=0.5, inc=(sAC, 1))
                    pl.wait("vector", sAC, tsq)
                    tnm = pl.op("vector", "tensor_scalar",
                                nm_row[0:1, sl * 4 * CH:(sl + 1) * 4 * CH],
                                qn2_row[0:1, :], -1.001, None,
                                ALU.mult, inc=(sEV, 1))
                    return tnm

                kv_load(0)
                kv_load(1)
                nm_ticks = [None] * KV
                nm_ticks[0] = prep(0)
                nm_ticks[1] = prep(1)
                pend_norm = [None]

                def emit_norm(head, bpv, bsm, t_pv):
                    nonlocal pvout_sp
                    pl.wait("vector", sMM, t_pv)
                    trc = pl.op("vector", "reciprocal", rc_row[0:1, :],
                                pb[bsm][0:1, 0:CH], inc=(sEV, 1))
                    bfree[bsm] = (sEV, trc)
                    pe_take(6)
                    pl.wait("tensor", sEV, trc)
                    tmm = mm(6, pb[6][:, 0:CH], c_oner[:], rc_row[0:1, :],
                             True, True, inc=(sMM, 1))
                    pl.wait("vector", sMM, tmm)
                    trcp = pl.op("vector", "tensor_copy", rcrep[:],
                                 pb[6][:, 0:CH], inc=(sEV, 1))
                    bfree[6] = (sEV, trcp)
                    pl.wait("vector", sEV, trcp)
                    pl.wait("vector", sSPa, pvout_sp)
                    tev = pl.op("vector", "tensor_mul", pvout[:],
                                pb[bpv][:, 0:CH], rcrep[:], inc=(sEV, 1))
                    bfree[bpv] = (sEV, tev)
                    pl.wait("sync", sEV, tev)
                    pvout_sp = pl.dma("sync", sSPa, at_sc[head], pvout[:])
                    if debug and own == 0 and head == 0:
                        pl.dma("sync", sST, dbg["d_attn0"], pvout[:])
                        pl.wait("vector", sST, sST.n)

                for kvh in range(KV):
                    sl = kvh % 2
                    ko = sl * nsl * CH
                    vo = sl * nsl * 4 * 128
                    pl.wait("tensor", sEV, nm_ticks[kvh])
                    pl.wait("tensor", s_kv[sl], kv_tick[sl])
                    for ht in range(4):
                        head = kvh * 4 + ht
                        qv = q_all[:, head * CH:(head + 1) * CH]
                        nmv = nm_row[0:1, (sl * 4 + ht) * CH:
                                     (sl * 4 + ht + 1) * CH]
                        hp = ht % 2
                        bpv, bsm = 4 + hp, 0 + hp
                        nktile = nsl * 4
                        pe_take(bpv)
                        exp_tick = [0] * nktile
                        pe_take(6)
                        trep = mm(6, pb[6][:, 0:CH], c_oner[:], nmv,
                                  True, True, inc=(sMM, 1))
                        pl.wait("vector", sMM, trep)
                        trev = pl.op("vector", "tensor_copy", nm_rep[:],
                                     pb[6][:, 0:CH], inc=(sEV, 1))
                        bfree[6] = (sEV, trev)

                        def emit_score(ki):
                            bs = 2 + (ki % 2)
                            pe_take(bs)
                            if ki // 4 == nsl - 1:
                                mm(bs, pb[bs][:, 0:CH],
                                   K_sb[:, ko + ki * 128:ko + (ki + 1) * 128],
                                   qv, True, False)
                                tmm = mm(bs, pb[bs][:, 0:CH], c_idnr[:],
                                         c_tri[:, (ki % 4) * CH:(ki % 4 + 1) * CH],
                                         False, True, inc=(sMM, 1))
                            else:
                                tmm = mm(bs, pb[bs][:, 0:CH],
                                         K_sb[:, ko + ki * 128:ko + (ki + 1) * 128],
                                         qv, True, True, inc=(sMM, 1))
                            eb = ki % 2
                            pl.wait("vector", sMM, tmm)
                            pl.wait("vector", sAC, expin_free[eb])
                            tbd = pl.op("vector", "tensor_add",
                                        expin[:, eb * CH:(eb + 1) * CH],
                                        pb[bs][:, 0:CH], nm_rep[:],
                                        inc=(sEV, 1))
                            bfree[bs] = (sEV, tbd)
                            pl.wait("scalar", sEV, tbd)
                            pl.wait("scalar", sMM, expT_free[eb][0])
                            pl.wait("scalar", sEV, expT_free[eb][1])
                            bcol = base_off12 + (ki // 4)
                            tex = pl.op("scalar", "activation",
                                        expT[:, eb * CH:(eb + 1) * CH],
                                        expin[:, eb * CH:(eb + 1) * CH], AF.Exp,
                                        bias=baseb[:, bcol:bcol + 1],
                                        scale=SCALE, inc=(sAC, 1))
                            expin_free[eb] = tex
                            exp_tick[ki] = tex

                        def emit_pv(ki):
                            eb = ki % 2
                            pl.wait("tensor", sAC, exp_tick[ki])
                            tpv = mm(bpv, pb[bpv][:, 0:CH],
                                     V_sb[:, vo + ki * 128:vo + (ki + 1) * 128],
                                     expT[:, eb * CH:(eb + 1) * CH],
                                     ki == 0, ki == nktile - 1, inc=(sMM, 1))
                            pl.wait("vector", sAC, exp_tick[ki])
                            if ki == 0:
                                tad = pl.op("vector", "tensor_copy", sume[:],
                                            expT[:, 0:CH].bitcast(F32),
                                            inc=(sEV, 1))
                            else:
                                tad = pl.op("vector", "tensor_add", sume[:],
                                            sume[:],
                                            expT[:, eb * CH:(eb + 1) * CH]
                                            .bitcast(F32), inc=(sEV, 1))
                            expT_free[eb] = (tpv, tad)

                        emit_score(0)
                        for ki in range(nktile):
                            if ki + 1 < nktile:
                                emit_score(ki + 1)
                            emit_pv(ki)
                        pl.wait("vector", sMM, sumr_free[0])
                        tcv = pl.op("vector", "tensor_copy", sume_r[:],
                                    sume[:], inc=(sEV, 1))
                        pe_take(bsm)
                        pl.wait("tensor", sEV, tcv)
                        t_pv_done = mm(bsm, pb[bsm][0:1, 0:CH], c_onec[:],
                                       sume_r[:], True, True, inc=(sMM, 1))
                        sumr_free[0] = t_pv_done
                        if ht == 0 and kvh + 1 < KV:
                            nm_ticks[kvh + 1] = prep(kvh + 1)
                        if pend_norm[0] is not None:
                            emit_norm(*pend_norm[0])
                        pend_norm[0] = (head, bpv, bsm, t_pv_done)
                    kv_free[sl] = sMM.n
                    if kvh + 2 < KV:
                        kv_load(kvh + 2)
                emit_norm(*pend_norm[0])
                pend_norm[0] = None
                t_at_spills = sSPa.n
            pl.barrier(dsems)

            accs = ExitStack()
            acc = accs.enter_context(
                nc.sbuf_tensor("acc_%d" % own, [128, NKT * CH], F32))
            with ExitStack() as p2c:
                sb2c = lambda n, sh, dt_: p2c.enter_context(
                    nc.sbuf_tensor(n + "_%d" % own, sh, dt_))
                atg = sb2c("atg", [128, 2 * 8 * CH], F32R)
                wob = sb2c("wob", [128, 4 * 8 * 128], F32R)
                s_atg = [mksem("s_atg%d_0" % own), mksem("s_atg%d_1" % own)]
                s_wo = [mksem("s_wo%d_%d" % (own, i)) for i in range(4)]
                atg_free = [0, 0]
                wob_free = [0, 0, 0, 0]
                acc_tick = [0] * NKT
                pl.wait("sync", sSPa, t_at_spills)
                for qg in range(4):
                    gbuf = qg % 2
                    pl.wait("sync", sMM, atg_free[gbuf])
                    tat = pl.dma("sync", s_atg[gbuf],
                                 atg[:, gbuf * 8 * CH:(gbuf + 1) * 8 * CH]
                                 .rearrange("p (k f) -> p k f", k=8),
                                 at_sc[qg * 8:(qg + 1) * 8].rearrange("k p f -> p k f"))
                    pl.wait("tensor", s_atg[gbuf], tat)
                    for dmch in range(NKT):
                        wbuf = dmch % 4
                        pl.wait("gpsimd", sMM, wob_free[wbuf])
                        two = pl.dma("gpsimd", s_wo[wbuf],
                                     wob[:, wbuf * 8 * 128:(wbuf + 1) * 8 * 128]
                                     .rearrange("p (k f) -> p k f", k=8),
                                     wo_t[qg, dmch].rearrange("k p f -> p k f"))
                        pl.wait("tensor", s_wo[wbuf], two)
                        bo = dmch % 2
                        pe_take(bo)
                        for ik in range(8):
                            tmm = mm(bo, pb[bo][:, 0:CH],
                                     wob[:, (wbuf * 8 + ik) * 128:
                                         (wbuf * 8 + ik + 1) * 128],
                                     atg[:, (gbuf * 8 + ik) * CH:
                                         (gbuf * 8 + ik + 1) * CH],
                                     ik == 0, ik == 7,
                                     inc=(sMM, 1) if ik == 7 else None)
                        wob_free[wbuf] = tmm
                        atg_free[gbuf] = tmm
                        pl.wait("vector", sMM, tmm)
                        if qg > 0:
                            pl.wait("vector", sEV, acc_tick[dmch])
                        if qg == 0:
                            ta = pl.op("vector", "tensor_copy",
                                       acc[:, dmch * CH:(dmch + 1) * CH],
                                       pb[bo][:, 0:CH], inc=(sEV, 1))
                        else:
                            ta = pl.op("vector", "tensor_add",
                                       acc[:, dmch * CH:(dmch + 1) * CH],
                                       acc[:, dmch * CH:(dmch + 1) * CH],
                                       pb[bo][:, 0:CH], inc=(sEV, 1))
                        bfree[bo] = (sEV, ta)
                        acc_tick[dmch] = ta
            pl.barrier(dsems)

            with ExitStack() as p3:
                sb3 = lambda n, sh, dt_: p3.enter_context(
                    nc.sbuf_tensor(n + "_%d" % own, sh, dt_))
                xnf = sb3("xnf", [128, NKT * CH], F32R)
                w13b = sb3("w13b", [128, 4 * 2 * 8 * 128], F32R)
                w2b = sb3("w2b", [128, 3 * 8 * 128], F32R)
                hgb = sb3("hgb", [128, 8 * CH], F32R)
                silub = sb3("silub", [128, CH], F32)

                s_xnf = mksem("s_xnf%d" % own)
                s_w13 = [mksem("s_w13%d_%d" % (own, i)) for i in range(4)]
                s_w2 = [mksem("s_w2%d_%d" % (own, i)) for i in range(3)]
                txn = pl.dma("sync", s_xnf,
                             xnf[:].rearrange("p (k f) -> p k f", k=NKT),
                             xn_sc[own].rearrange("k p f -> p k f"))
                pl.wait("tensor", s_xnf, txn)
                w13_free = [0, 0, 0, 0]
                w13_tick = {}
                w2_free = [0, 0, 0]

                def emit_w13_load(Q):
                    if Q >= IC * 4:
                        return
                    slot = Q % 4
                    pl.wait("gpsimd", sMM, w13_free[slot])
                    w13_tick[Q] = pl.dma(
                        "gpsimd", s_w13[slot],
                        w13b[:, slot * 2 * 8 * 128:(slot + 1) * 2 * 8 * 128]
                        .rearrange("p (w k f) -> p w k f", w=2, k=8),
                        w13_t[Q // 4, Q % 4].rearrange("w k p f -> p w k f"))

                emit_w13_load(0)
                emit_w13_load(1)
                emit_w13_load(2)
                hg_free = 0
                silu_free = [0]
                ic0 = 0
                for g, gsz in enumerate(FFN_GROUPS):
                    pl.wait("scalar", sMM, hg_free)
                    pl.wait("vector", sMM, hg_free)
                    for ii in range(gsz):
                        ich = ic0 + ii
                        bu, bg = 0 + (ii % 2), 2 + (ii % 2)
                        for q in range(4):
                            Q = ich * 4 + q
                            slot = Q % 4
                            emit_w13_load(Q + 3)
                            pl.wait("tensor", s_w13[slot], w13_tick[Q])
                            if q == 0:
                                pe_take(bu)
                                pe_take(bg)
                            for k8 in range(8):
                                kt = q * 8 + k8
                                base = slot * 2 * 8 * 128
                                mm(bu, pb[bu][:, 0:CH],
                                   w13b[:, base + k8 * 128:
                                        base + (k8 + 1) * 128],
                                   xnf[:, kt * CH:(kt + 1) * CH],
                                   kt == 0, kt == NKT - 1)
                                tmm = mm(bg, pb[bg][:, 0:CH],
                                         w13b[:, base + (8 + k8) * 128:
                                              base + (8 + k8 + 1) * 128],
                                         xnf[:, kt * CH:(kt + 1) * CH],
                                         kt == 0, kt == NKT - 1,
                                         inc=(sMM, 1) if k8 == 7 else None)
                            w13_free[slot] = tmm
                        pl.wait("scalar", sMM, tmm)
                        pl.wait("scalar", sEV, silu_free[0])
                        sACt = pl.op("scalar", "activation", silub[:],
                                     pb[bu][:, 0:CH], AF.Silu,
                                     bias=0.0, scale=c_scl[:, ich:ich + 1],
                                     inc=(sAC, 1))
                        bfree[bu] = (sAC, sACt)
                        pl.wait("vector", sAC, sACt)
                        sEVt = pl.op("vector", "scalar_tensor_tensor",
                                     hgb[:, ii * CH:(ii + 1) * CH],
                                     pb[bg][:, 0:CH],
                                     c_scl[:, IC + ich:IC + ich + 1],
                                     silub[:], ALU.mult, ALU.mult,
                                     inc=(sEV, 1))
                        bfree[bg] = (sEV, sEVt)
                        silu_free[0] = sEVt
                    t_h_ready = sEV.n
                    pl.wait("tensor", sEV, t_h_ready)
                    for dmch in range(NKT):
                        wbuf = dmch % 3
                        pl.wait("gpsimd", sMM, w2_free[wbuf])
                        goff = (g * 32 + dmch) * 8 if g < 10 else (2560 + dmch * 6)
                        t2 = pl.dma("gpsimd", s_w2[wbuf],
                                    w2b[:, wbuf * 8 * 128:
                                        wbuf * 8 * 128 + gsz * 128]
                                    .rearrange("p (k f) -> p k f", k=gsz),
                                    w2_t[goff:goff + gsz].rearrange("k p f -> p k f"))
                        pl.wait("tensor", s_w2[wbuf], t2)
                        bd = 4 + (dmch % 2)
                        pe_take(bd)
                        for ik in range(gsz):
                            tmm = mm(bd, pb[bd][:, 0:CH],
                                     w2b[:, (wbuf * 8 + ik) * 128:
                                         (wbuf * 8 + ik + 1) * 128],
                                     hgb[:, ik * CH:(ik + 1) * CH],
                                     ik == 0, ik == gsz - 1,
                                     inc=(sMM, 1) if ik == gsz - 1 else None)
                        w2_free[wbuf] = tmm
                        hg_free = tmm
                        pl.wait("vector", sMM, tmm)
                        pl.wait("vector", sEV, acc_tick[dmch])
                        ta = pl.op("vector", "scalar_tensor_tensor",
                                   acc[:, dmch * CH:(dmch + 1) * CH],
                                   pb[bd][:, 0:CH],
                                   c_scl[:, 2 * IC + dmch:2 * IC + dmch + 1],
                                   acc[:, dmch * CH:(dmch + 1) * CH],
                                   ALU.mult, ALU.add, inc=(sEV, 1))
                        bfree[bd] = (sEV, ta)
                        acc_tick[dmch] = ta
                    ic0 += gsz
                if debug and own == 0:
                    pl.wait("vector", sEV, acc_tick[0])
                    tev = pl.op("vector", "tensor_copy", silub[:],
                                acc[:, 0:CH], inc=(sEV, 1))
                    pl.wait("sync", sEV, tev)
                    pl.dma("sync", sST, dbg["d_acc0"], silub[:])
                    pl.wait("scalar", sST, sST.n)
                    pl.wait("vector", sST, sST.n)
            pl.barrier(dsems)

            with ExitStack() as p4:
                sb4 = lambda n, sh, dt_: p4.enter_context(
                    nc.sbuf_tensor(n + "_%d" % own, sh, dt_))
                xres = sb4("xres", [128, 4 * CH], F32)
                stf = sb4("stf", [128, 2 * CH], F32)
                sto = sb4("sto", [128, 2 * CH], F32)
                s_xres = [mksem("s_xr%d_%d" % (own, i)) for i in range(4)]
                xres_free = [0, 0, 0, 0]
                tx_t = [0, 0, 0, 0]
                stf_free = [0, 0]
                sto_free = [0, 0]

                def x_load(dmch):
                    db = dmch % 4
                    pl.wait("sync", sEV, xres_free[db])
                    tx_t[db] = pl.dma("sync", s_xres[db],
                                      xres[:, db * CH:(db + 1) * CH],
                                      xTp[own, dmch])
                for i in range(3):
                    x_load(i)
                for dmch in range(NKT):
                    db = dmch % 4
                    dbuf = dmch % 2
                    pl.wait("vector", s_xres[db], tx_t[db])
                    pl.wait("vector", sMM, stf_free[dbuf])
                    pl.wait("vector", sEV, acc_tick[dmch])
                    tev = pl.op("vector", "tensor_add",
                                stf[:, dbuf * CH:(dbuf + 1) * CH],
                                acc[:, dmch * CH:(dmch + 1) * CH],
                                xres[:, db * CH:(db + 1) * CH],
                                inc=(sEV, 1))
                    xres_free[db] = tev
                    if dmch + 3 < NKT:
                        x_load(dmch + 3)
                    bt = 6 + dbuf
                    pe_take(bt)
                    pl.wait("tensor", sEV, tev)
                    for tt in range(4):
                        ttr = transp(pb[bt][:, tt * 128:(tt + 1) * 128],
                                     stf[:, dbuf * CH + tt * 128:
                                         dbuf * CH + (tt + 1) * 128],
                                     inc=(sMM, 1) if tt == 3 else None)
                    stf_free[dbuf] = ttr
                    pl.wait("scalar", sMM, ttr)
                    pl.wait("scalar", sSTo[dbuf], sto_free[dbuf])
                    sACt = pl.op("scalar", "activation",
                                 sto[:, dbuf * CH:(dbuf + 1) * CH],
                                 pb[bt][:, 0:CH], AF.Identity,
                                 bias=0.0, scale=1.0, inc=(sAC, 1))
                    bfree[bt] = (sAC, sACt)
                    pl.wait("sync", sAC, sACt)
                    tst = pl.dma("sync", sSTo[dbuf],
                                 out_h[own, :, dmch * 128:(dmch + 1) * 128]
                                 .rearrange("(t p) f -> p t f", p=128),
                                 sto[:, dbuf * CH:(dbuf + 1) * CH]
                                 .rearrange("p (t f) -> p t f", t=4))
                    sto_free[dbuf] = tst
                pl.wait("sync", sSTo[0], sSTo[0].n)
                pl.wait("sync", sSTo[1], sSTo[1].n)
                pl.wait("sync", sST, sST.n)
            accs.close()
            pl.barrier(dsems)

        pl.wait("sync", sSTo[0], sSTo[0].n)
        pl.wait("sync", sSTo[1], sSTo[1].n)
        pl.wait("sync", sST, sST.n)

        with nc.allow_low_precision(
                "fp32r outputs are the fp32-emulating hi/lo bf16 rounding; "
                "all matmul accumulation stays fp32 in PSUM"), \
             nc.Block() as block:
            @block.sync
            def _(sync):
                for fn in pl.q["sync"]:
                    fn(sync)

            @block.gpsimd
            def _(gpsimd):
                for fn in pl.q["gpsimd"]:
                    fn(gpsimd)

            @block.vector
            def _(vector):
                for fn in pl.q["vector"]:
                    fn(vector)

            @block.scalar
            def _(scalar):
                for fn in pl.q["scalar"]:
                    fn(scalar)

            @block.tensor
            def _(tensor):
                for fn in pl.q["tensor"]:
                    fn(tensor)

    return nc


def make_perm(j):
    A, Bc = j, 7 - j
    slots = [None] * 8
    slots[3], slots[7] = A, Bc
    for c in range(A):
        slots[c] = c
    rest = [c for c in range(8) if c not in (A, Bc) and c >= A]
    free = [s for s in (0, 1, 2, 4, 5, 6) if slots[s] is None]
    for s, c in zip(free, rest):
        slots[s] = c
    return slots, A, Bc


def rope_tables(positions):
    d2 = np.arange(0, HD, 2, dtype=np.float64) / HD
    inv = 1.0 / (10000.0 ** d2)
    fr = np.outer(positions.astype(np.float64), inv)
    emb = np.concatenate([fr, fr], axis=1)
    return (np.cos(emb).T.astype(np.float32),
            np.sin(emb).T.astype(np.float32))


def prep_core_inputs(core, x, Wt, consts):
    b, j = core // 4, core % 4
    A, Bc = j, 7 - j
    xT = np.ascontiguousarray(x[b].T)
    xTp = np.empty((2, NKT, 128, CH), dtype=np.float32)
    cosk = np.empty((2, 128, CH), dtype=np.float32)
    sink = np.empty((2, 128, CH), dtype=np.float32)
    for s, c in enumerate((A, Bc)):
        blkT = xT[:, c * CH:(c + 1) * CH]
        xTp[s] = blkT.reshape(NKT, 128, CH)
        pos = np.arange(c * CH, (c + 1) * CH)
        cosk[s], sink[s] = rope_tables(pos)
    def chunk_at(p):
        return p // 2 if p % 2 == 0 else 7 - (p // 2)
    base = np.full((128, 13), 44.0, dtype=np.float32)
    for kc in range(3):
        if not (kc < A):
            base[:, kc] = NEG
    for p in range(8):
        if not (chunk_at(p) < Bc):
            base[:, 4 + p] = NEG
    d = dict(Wt)
    d.update(consts)
    d["xTp"] = xTp
    d["cosk"] = cosk
    d["sink"] = sink
    d["base"] = base
    return d, (b, A, Bc)


def prep_shared(L0_attn_wq, L0_attn_wk, L0_attn_wv, L0_attn_wo,
                L0_ffn_w1_q, L0_ffn_w1_s, L0_ffn_w3_q, L0_ffn_w3_s,
                L0_ffn_w2_q, L0_ffn_w2_s):
    f32 = np.float32
    wqT = np.ascontiguousarray(np.asarray(L0_attn_wq, f32).T)
    wkT = np.ascontiguousarray(np.asarray(L0_attn_wk, f32).T)
    wvT = np.ascontiguousarray(np.asarray(L0_attn_wv, f32).T)
    woT = np.ascontiguousarray(np.asarray(L0_attn_wo, f32).T)
    w1T = np.ascontiguousarray(np.asarray(L0_ffn_w1_q, f32).T)
    w3T = np.ascontiguousarray(np.asarray(L0_ffn_w3_q, f32).T)
    w2T = np.ascontiguousarray(np.asarray(L0_ffn_w2_q, f32).T)

    wq_t = np.empty((KV, NKT, 128, 512), f32)
    for kvh in range(KV):
        for kt in range(NKT):
            wq_t[kvh, kt] = wqT[kt * 128:(kt + 1) * 128,
                                kvh * 512:(kvh + 1) * 512]
    wk_t = np.empty((KV, NKT, 128, 128), f32)
    for kvh in range(KV):
        for kt in range(NKT):
            wk_t[kvh, kt] = wkT[kt * 128:(kt + 1) * 128,
                                kvh * 128:(kvh + 1) * 128]
    wv_t = np.empty((2, NKT, 128, 512), f32)
    for vch in range(2):
        for kt in range(NKT):
            wv_t[vch, kt] = wvT[kt * 128:(kt + 1) * 128,
                                vch * 512:(vch + 1) * 512]
    wo_t = np.empty((4, 32, 8, 128, 128), f32)
    for qg in range(4):
        for dmch in range(32):
            for ik in range(8):
                qk = qg * 8 + ik
                wo_t[qg, dmch, ik] = woT[qk * 128:(qk + 1) * 128,
                                         dmch * 128:(dmch + 1) * 128]
    w13_t = np.empty((IC, 4, 2, 8, 128, 128), f32)
    for ich in range(IC):
        for kt in range(NKT):
            w13_t[ich, kt // 8, 0, kt % 8] = w1T[kt * 128:(kt + 1) * 128,
                                                 ich * 128:(ich + 1) * 128]
            w13_t[ich, kt // 8, 1, kt % 8] = w3T[kt * 128:(kt + 1) * 128,
                                                 ich * 128:(ich + 1) * 128]
    w2_t = np.empty((2752, 128, 128), f32)
    idx = 0
    ic0 = 0
    for g, gsz in enumerate(FFN_GROUPS):
        for dmch in range(32):
            for ik in range(gsz):
                ich = ic0 + ik
                w2_t[idx] = w2T[ich * 128:(ich + 1) * 128,
                                dmch * 128:(dmch + 1) * 128]
                idx += 1
        ic0 += gsz
    assert idx == 2752

    scl = np.zeros((128, 2 * IC + 32), f32)
    scl[:, 0:IC] = np.asarray(L0_ffn_w1_s, f32).reshape(IC, 128).T
    scl[:, IC:2 * IC] = np.asarray(L0_ffn_w3_s, f32).reshape(IC, 128).T
    scl[:, 2 * IC:] = np.asarray(L0_ffn_w2_s, f32).reshape(32, 128).T

    csts = np.zeros((128, 24), f32)
    csts[:, 2] = EPS
    csts[:, 8:22] = 1.0

    idn = np.eye(128, dtype=f32)
    prot = np.zeros((128, 128), f32)
    for e in range(64):
        prot[e + 64, e] = -1.0
    for e in range(64):
        prot[e, e + 64] = 1.0
    tri = np.zeros((4, 128, CH), f32)
    for kt in range(4):
        kj = kt * 128 + np.arange(128)[:, None]
        qi = np.arange(CH)[None, :]
        tri[kt] = np.where(kj <= qi, 0.0, NEG)
    onec = np.ones((128, 1), f32)
    oner = np.ones((1, 128), f32)

    Wt = {"wq_t": wq_t, "wk_t": wk_t, "wv_t": wv_t, "wo_t": wo_t,
          "w13_t": w13_t, "w2_t": w2_t, "scl": scl}
    consts = {"csts": csts, "idn": idn, "prot": prot, "tri": tri,
              "onec": onec, "oner": oner}
    return Wt, consts


def kernel(L0_attn_hidden_states, L0_attn_wq, L0_attn_wk, L0_attn_wv,
           L0_attn_wo, L0_ffn_w1_q, L0_ffn_w1_s, L0_ffn_w3_q, L0_ffn_w3_s,
           L0_ffn_w2_q, L0_ffn_w2_s):
    x = np.asarray(L0_attn_hidden_states, np.float32)
    Wt, consts = prep_shared(L0_attn_wq, L0_attn_wk, L0_attn_wv, L0_attn_wo,
                             L0_ffn_w1_q, L0_ffn_w1_s, L0_ffn_w3_q,
                             L0_ffn_w3_s, L0_ffn_w2_q, L0_ffn_w2_s)
    nc = build(debug=False)
    in_maps = []
    meta = []
    for core in range(8):
        d, m = prep_core_inputs(core, x, Wt, consts)
        in_maps.append(d)
        meta.append(m)
    res = run_bass_kernel_spmd(nc, in_maps, core_ids=list(range(8)))
    out = np.empty((B, S, D), np.float32)
    for core in range(8):
        b, A, Bc = meta[core]
        o = res.results[core]["out"]
        out[b, A * CH:(A + 1) * CH] = o[0]
        out[b, Bc * CH:(Bc + 1) * CH] = o[1]
    return out
